# version 04
# baseline (speedup 1.0000x reference)
from contextlib import ExitStack

import numpy as np
import ml_dtypes
import concourse.bass as bass
import concourse.mybir as mybir
from concourse.bass_utils import run_bass_kernel_spmd

F32 = mybir.dt.float32
BF16 = mybir.dt.bfloat16
I32 = mybir.dt.int32
AF = mybir.ActivationFunctionType
ALU = mybir.AluOpType
AX = mybir.AxisListType

NCORES = 8
SEQ = 16384
DM = 2048
TPC = SEQ // NCORES
INTER = 5632
NFB = INTER // 128
NDB = DM // 128
LN_EPS = 1e-5



class Emitter:

    ENGINES = ("sync", "scalar", "vector", "tensor", "gpsimd")
    SELF_SYNC = ("vector", "scalar")

    def __init__(self):
        self.ops = {e: [] for e in self.ENGINES}
        self.cnt = {}
        self.self_sem = {}

    def set_self_sems(self, by_eng):
        self.self_sem = dict(by_eng)

    def wait(self, eng, sem, val):
        if val is None or val <= 0:
            return
        self.ops[eng].append(lambda e, s=sem, v=val: e.wait_ge(s, v))

    def dma(self, eng, out, in_, sem):
        v = self.cnt[sem.name] = self.cnt.get(sem.name, 0) + 16
        self.ops[eng].append(
            lambda e, o=out, i=in_, s=sem: e.dma_start(out=o, in_=i).then_inc(s, 16)
        )
        return v

    def op(self, eng, fn, sem=None):
        if eng in self.SELF_SYNC:
            ssem = self.self_sem[eng]
            assert sem is None or sem is ssem
            cur = self.cnt.get(ssem.name, 0)
            if cur > 0:
                self.ops[eng].append(lambda e, s=ssem, v=cur: e.wait_ge(s, v))
            v = self.cnt[ssem.name] = cur + 1
            self.ops[eng].append(lambda e, f=fn, s=ssem: f(e).then_inc(s, 1))
            return v
        if sem is None:
            self.ops[eng].append(fn)
            return None
        v = self.cnt[sem.name] = self.cnt.get(sem.name, 0) + 1
        self.ops[eng].append(lambda e, f=fn, s=sem: f(e).then_inc(s, 1))
        return v

    def val(self, sem):
        return self.cnt.get(sem.name, 0)

    def milestone(self, eng, sem):
        if eng in self.SELF_SYNC:
            buf = self._ms_v if eng == "vector" else self._ms_s
            if eng == "vector":
                return self.op(eng, lambda e: e.memset(buf, 0.0))
            return self.op(eng, lambda e: e.memzero(buf))
        return self.op(eng, lambda e: e.nop(), sem)

    def set_milestone_bufs(self, vec_buf, sc_buf):
        self._ms_v = vec_buf
        self._ms_s = sc_buf

    def barrier(self, sems_by_eng, waiters=ENGINES):
        targets = []
        tensor_sem = None
        for eng, sem in sems_by_eng.items():
            if eng in self.SELF_SYNC:
                targets.append((self.self_sem[eng], self.milestone(eng, None)))
            else:
                assert eng == "tensor"
                tensor_sem = sem
        if tensor_sem is not None:
            for sem, v in targets:
                self.wait("tensor", sem, v)
            targets.append((tensor_sem, self.milestone("tensor", tensor_sem)))
        for eng in waiters:
            for sem, v in targets:
                if eng == "tensor" and sem is not tensor_sem:
                    continue
                self.wait(eng, sem, v)

    def replay(self, block):
        for eng in self.ENGINES:
            ops = self.ops[eng]
            if not ops:
                continue

            def make(ops_list):
                def body(engine):
                    for f in ops_list:
                        f(engine)
                return body

            getattr(block, eng)(make(ops))


def _ts(i, n=128):
    return slice(i * n, (i + 1) * n)



def emit_ln_to_xnT(em, x_src_ap, xnT, ntt, Tg, sems, bufs):
    ld_x, s_sem, v_sem, t_sem = sems["ld_x"], sems["s"], sems["v"], sems["t"]
    xbuf, stat = bufs["xbuf"], bufs["stat"]
    ps_tr, xn, ident_bf = bufs["ps_tr"], bufs["xn"], bufs["ident_bf"]
    xnT_r = xnT[:].rearrange("p (db t) -> p db t", t=Tg)

    for k in range(ntt):
        xb = xbuf[k % 2]
        if k >= 2:
            em.wait("sync", v_sem, em.val(v_sem))
        lv = em.dma("sync", xb[:], x_src_ap[_ts(k), :], ld_x[k % 2])
        em.wait("vector", ld_x[k % 2], lv)
        em.op("vector", lambda e, xb=xb: e.reduce_sum(stat[:, 0:1], xb[:], axis=AX.X))
        em.op("vector", lambda e: e.tensor_scalar_mul(stat[:, 1:2], stat[:, 0:1], 1.0 / DM))
        vv = em.op("vector", lambda e, xb=xb: e.tensor_scalar(
            out=xb[:], in0=xb[:], scalar1=stat[:, 1:2], scalar2=None,
            op0=ALU.subtract), v_sem)
        em.wait("scalar", v_sem, vv)
        em.op("scalar", lambda e, xb=xb: e.activation(
            out=xn[:], in_=xb[:], func=AF.Square, accum_out=stat[:, 2:3]))
        sv = em.op("scalar", lambda e: e.activation(
            out=stat[:, 3:4], in_=stat[:, 2:3], func=AF.Sqrt,
            scale=1.0 / DM, bias=stat[:, 5:6]), s_sem)
        em.wait("vector", s_sem, sv)
        em.op("vector", lambda e: e.reciprocal(stat[:, 4:5], stat[:, 3:4]))
        em.wait("vector", t_sem, em.val(t_sem))
        vv = em.op("vector", lambda e, xb=xb: e.tensor_scalar(
            out=xn[:], in0=xb[:], scalar1=stat[:, 4:5], scalar2=None,
            op0=ALU.mult), v_sem)
        em.wait("tensor", v_sem, vv)
        for half in range(2):
            tv = None
            for j in range(8):
                db = half * 8 + j
                tv = em.op("tensor", lambda e, h2=half, j2=j, db=db: e.transpose(
                    ps_tr[h2][:, _ts(j2)], xn[:, _ts(db)], ident_bf[:]),
                    t_sem if j == 7 else None)
            em.wait("vector", t_sem, tv)
            vv = em.op("vector", lambda e, h2=half, k2=k: e.tensor_copy(
                xnT_r[:, h2 * 8:(h2 + 1) * 8, _ts(k2)], ps_tr[h2][:]), v_sem)
            em.wait("tensor", v_sem, vv)



def build_ffn(quant):
    nc = bass.Bass("TRN2", target_bir_lowering=False)
    wdt = I32 if quant == "int8" else F32
    x = nc.dram_tensor("x", [1024, DM], F32, kind="ExternalInput").ap()
    gw = nc.dram_tensor("gw", [INTER, DM], wdt, kind="ExternalInput").ap()
    uw = nc.dram_tensor("uw", [INTER, DM], wdt, kind="ExternalInput").ap()
    dw = nc.dram_tensor("dw", [DM, INTER], wdt, kind="ExternalInput").ap()
    gs = nc.dram_tensor("gs_m", [128, NFB], F32, kind="ExternalInput").ap()
    us = nc.dram_tensor("us_m", [128, NFB], F32, kind="ExternalInput").ap()
    ds = nc.dram_tensor("ds_m", [128, NDB], F32, kind="ExternalInput").ap()
    ident = nc.dram_tensor("ident_bf", [128, 128], BF16, kind="ExternalInput").ap()
    xout = nc.dram_tensor("xout", [1024, DM], F32, kind="ExternalOutput").ap()

    NTT = 8
    Tg = NTT * 128
    HALF = 22
    HC = HALF * 128

    em = Emitter()
    es = ExitStack()
    with es:
        sb = lambda name, shape, dt: es.enter_context(nc.sbuf_tensor(name, shape, dt))
        ps = lambda name, shape, dt: es.enter_context(nc.psum_tensor(name, shape, dt))
        sem = lambda name: es.enter_context(nc.semaphore(name))

        xbuf = [sb(f"xbuf{i}", [128, DM], F32) for i in range(2)]
        stat = sb("stat", [128, 8], F32)
        xn = sb("xn", [128, DM], BF16)
        xnT = sb("xnT", [128, NDB * Tg], BF16)
        ident_bf = sb("ident_sb", [128, 128], BF16)
        gs_sb = sb("gs_sb", [128, NFB], F32)
        us_sb = sb("us_sb", [128, NFB], F32)
        ds_sb = sb("ds_sb", [128, NDB], F32)
        gneg_sb = sb("gneg_sb", [128, NFB], F32)
        uneg_sb = sb("uneg_sb", [128, NFB], F32)
        dneg_sb = sb("dneg_sb", [128, NDB], F32)
        wg_f = [sb("wg_f0", [128, DM], F32)]
        wu_f = [sb("wu_f0", [128, DM], F32)]
        wg_b = sb("wg_b", [128, DM], BF16)
        wu_b = sb("wu_b", [128, DM], BF16)
        wgT = sb("wgT", [128, DM], BF16)
        wuT = sb("wuT", [128, DM], BF16)
        silu_tmp = sb("silu_tmp", [128, 512], F32)
        h_sb = sb("h_sb", [128, HALF * Tg], BF16)
        wd_f = [sb(f"wd_f{i}", [128, HC], F32) for i in range(2)]
        wd_b = sb("wd_b", [128, HC], BF16)
        dwT = sb("dwT", [128, HALF * 512], BF16)
        xsl = [sb(f"xsl{i}", [128, 512], F32) for i in range(2)]
        xosl = [sb(f"xosl{i}", [128, 512], F32) for i in range(2)]

        ps_tr = [ps(f"ps_tr{i}", [128, 1024], BF16) for i in range(2)]
        ps_g = [ps(f"ps_g{i}", [128, 512], F32) for i in range(2)]
        ps_u = [ps(f"ps_u{i}", [128, 512], F32) for i in range(2)]
        ps_d = [ps(f"ps_d{i}", [128, 512], F32) for i in range(2)]

        em.set_milestone_bufs(stat[:, 6:7], stat[:, 7:8])

        ld_c = sem("ld_c")
        ld_x = [sem("ld_x0"), sem("ld_x1")]
        ld_wg = sem("ld_wg")
        ld_wu = sem("ld_wu")
        ld_wd = [sem("ld_wd0"), sem("ld_wd1")]
        ld_xs = [sem("ld_xs0"), sem("ld_xs1")]
        st = sem("st")
        t_sem = sem("t_sem")
        v_sem = sem("v_sem")
        s_sem = sem("s_sem")

        em.set_self_sems({"vector": v_sem, "scalar": s_sem})
        sems = dict(ld_x=ld_x, s=s_sem, v=v_sem, t=t_sem)
        bufs = dict(xbuf=xbuf, stat=stat, ps_tr=ps_tr, xn=xn,
                    ident_bf=ident_bf)

        em.dma("sync", ident_bf[:], ident, ld_c)
        em.dma("sync", gs_sb[:], gs, ld_c)
        em.dma("sync", us_sb[:], us, ld_c)
        lv = em.dma("sync", ds_sb[:], ds, ld_c)
        em.wait("vector", ld_c, lv)
        em.op("vector", lambda e: e.tensor_scalar_mul(ds_sb[:], ds_sb[:], 0.5))
        if quant == "int8":
            em.op("vector", lambda e: e.tensor_scalar_mul(gneg_sb[:], gs_sb[:], -128.0))
            em.op("vector", lambda e: e.tensor_scalar_mul(uneg_sb[:], us_sb[:], -128.0))
            em.op("vector", lambda e: e.tensor_scalar_mul(dneg_sb[:], ds_sb[:], -128.0))
        em.op("vector", lambda e: e.memset(stat[:, 5:6], LN_EPS))
        em.milestone("vector", v_sem)
        em.wait("tensor", ld_c, 64)

        emit_ln_to_xnT(em, x, xnT, NTT, Tg, sems, bufs)
        em.barrier({"vector": v_sem, "tensor": t_sem, "scalar": s_sem})

        def dequant(out_bf, wf, scol, ncol, fsem=None):
            if quant == "int8":
                return em.op("vector", lambda e: e.tensor_scalar(
                    out=out_bf, in0=wf, scalar1=scol, scalar2=ncol,
                    op0=ALU.mult, op1=ALU.add), fsem)
            return em.op("vector", lambda e: e.tensor_scalar(
                out=out_bf, in0=wf, scalar1=scol, scalar2=None,
                op0=ALU.mult), fsem)

        ldq = "gpsimd" if quant == "int8" else "sync"

        for half in range(2):
            for i in range(HALF):
                fb = half * HALF + i
                par = 0
                if half + i > 0:
                    em.wait(ldq, v_sem, em.val(v_sem))
                l1 = em.dma(ldq, wg_f[par][:], gw[_ts(fb), :], ld_wg)
                l2 = em.dma(ldq, wu_f[par][:], uw[_ts(fb), :], ld_wu)
                em.wait("vector", ld_wg, l1)
                em.wait("vector", ld_wu, l2)
                em.wait("vector", t_sem, em.val(t_sem))
                dequant(wg_b[:], wg_f[par][:], gs_sb[:, fb:fb + 1], gneg_sb[:, fb:fb + 1])
                vv = dequant(wu_b[:], wu_f[par][:], us_sb[:, fb:fb + 1], uneg_sb[:, fb:fb + 1], v_sem)
                em.wait("tensor", v_sem, vv)
                for src, dstT in ((wg_b, wgT), (wu_b, wuT)):
                    tv = None
                    for j in range(16):
                        tv = em.op("tensor", lambda e, s2=src, j2=j: e.transpose(
                            ps_tr[j2 // 8][:, _ts(j2 % 8)], s2[:, _ts(j2)], ident_bf[:]),
                            t_sem if j == 15 else None)
                    em.wait("vector", t_sem, tv)
                    vv = None
                    for b2 in range(2):
                        vv = em.op("vector", lambda e, d2=dstT, b3=b2: e.tensor_copy(
                            d2[:, b3 * 1024:(b3 + 1) * 1024], ps_tr[b3][:]),
                            v_sem if b2 == 1 else None)
                    em.wait("tensor", v_sem, vv)
                tv = None
                for wT, psb in ((wgT, ps_g), (wuT, ps_u)):
                    for th in range(2):
                        for db in range(NDB):
                            last = wT is wuT and th == 1 and db == NDB - 1
                            tv2 = em.op("tensor", lambda e, w2=wT, p2=psb, db2=db, th2=th: e.matmul(
                                p2[th2][:], w2[:, _ts(db2)],
                                xnT[:, db2 * Tg + th2 * 512: db2 * Tg + th2 * 512 + 512],
                                start=(db2 == 0), stop=(db2 == NDB - 1)),
                                t_sem if last else None)
                            tv = tv2 if tv2 is not None else tv
                em.wait("scalar", t_sem, tv)
                for th in range(2):
                    em.wait("scalar", v_sem, em.val(v_sem))
                    sv = em.op("scalar", lambda e, th2=th: e.activation(
                        out=silu_tmp[:], in_=ps_g[th2][:], func=AF.Silu), s_sem)
                    em.wait("vector", s_sem, sv)
                    vv = em.op("vector", lambda e, th2=th, i2=i: e.tensor_tensor(
                        out=h_sb[:, i2 * Tg + th2 * 512: i2 * Tg + th2 * 512 + 512],
                        in0=silu_tmp[:], in1=ps_u[th2][:], op=ALU.mult), v_sem)
                em.wait("tensor", v_sem, vv)

            for q in range(4):
                for r in range(4):
                    par = r % 2
                    dmb = q * 4 + r
                    if not (half == 0 and q == 0 and r < 2):
                        em.wait(ldq, v_sem, em.val(v_sem))
                    lv = em.dma(ldq, wd_f[par][:],
                                dw[_ts(dmb), half * HC:(half + 1) * HC], ld_wd[par])
                    em.wait("vector", ld_wd[par], lv)
                    em.wait("vector", t_sem, em.val(t_sem))
                    vv = dequant(wd_b[:], wd_f[par][:],
                                 ds_sb[:, dmb:dmb + 1], dneg_sb[:, dmb:dmb + 1], v_sem)
                    em.wait("tensor", v_sem, vv)
                    done = 0
                    dwT_r = dwT[:].rearrange("p (ibl c) -> p ibl c", c=512)
                    for batch, nb in ((0, 8), (1, 8), (0, 6)):
                        if done == 16:
                            em.wait("tensor", v_sem, em.val(v_sem))
                        tv = None
                        for j in range(nb):
                            ibl = done + j
                            tv = em.op("tensor", lambda e, b2=batch, j2=j, i2=ibl: e.transpose(
                                ps_tr[b2][:, _ts(j2)], wd_b[:, _ts(i2)], ident_bf[:]),
                                t_sem if j == nb - 1 else None)
                        em.wait("vector", t_sem, tv)
                        vv = em.op("vector", lambda e, b2=batch, nb2=nb, d0=done, r2=r: e.tensor_copy(
                            dwT_r[:, d0:d0 + nb2, r2 * 128:(r2 + 1) * 128],
                            ps_tr[b2][:, 0:nb2 * 128].rearrange("p (a c) -> p a c", c=128)),
                            v_sem)
                        done += nb
                    em.wait("tensor", v_sem, vv)
                xsrc = x if half == 0 else xout
                for tt in range(NTT):
                    pp = tt % 2
                    if half + q + tt > 1:
                        em.wait("gpsimd", v_sem, em.val(v_sem))
                    if half == 1:
                        em.wait("gpsimd", st, em.val(st))
                    lv = em.dma("gpsimd", xsl[pp][:],
                                xsrc[_ts(tt), q * 512:(q + 1) * 512], ld_xs[pp])
                    if q + tt > 1 or half > 0:
                        em.wait("tensor", v_sem, em.val(v_sem))
                    tv = None
                    for ibl in range(HALF):
                        tv = em.op("tensor", lambda e, p2=pp, i2=ibl, t2=tt: e.matmul(
                            ps_d[p2][:], h_sb[:, i2 * Tg + t2 * 128: i2 * Tg + (t2 + 1) * 128],
                            dwT[:, i2 * 512:(i2 + 1) * 512],
                            start=(i2 == 0), stop=(i2 == HALF - 1)),
                            t_sem if ibl == HALF - 1 else None)
                    em.wait("vector", t_sem, tv)
                    em.wait("vector", ld_xs[pp], lv)
                    em.wait("vector", st, em.val(st))
                    vv = em.op("vector", lambda e, p2=pp: e.tensor_tensor(
                        out=xosl[p2][:], in0=xsl[p2][:], in1=ps_d[p2][:],
                        op=ALU.add), v_sem)
                    em.wait("gpsimd", v_sem, vv)
                    em.dma("gpsimd", xout[_ts(tt), q * 512:(q + 1) * 512],
                           xosl[pp][:], st)
            em.barrier({"vector": v_sem, "tensor": t_sem, "scalar": s_sem})

        em.wait("gpsimd", st, em.val(st))

        blk = es.enter_context(nc.Block())
        em.replay(blk)
    return nc



_PROGS = {}


def _get_prog(key):
    if key not in _PROGS:
        builders = {
            "ffn_ternary": lambda: build_ffn("ternary"),
            "ffn_int8": lambda: build_ffn("int8"),
            "m1": build_m1, "m2a": build_m2a,
            "m3a0": lambda: build_m3a(0), "m3a1": lambda: build_m3a(1),
            "m3b": build_m3b, "r1": build_r1, "r2": build_r2,
        }
        _PROGS[key] = builders[key]()
    return _PROGS[key]


_IDENT_BF = np.eye(128, dtype=ml_dtypes.bfloat16)
_IDENT_F = np.eye(128, dtype=np.float32)
_ONES_F = np.ones((1, 128), np.float32)
_TRIL = np.tril(np.ones((128, 128), np.float32))
_TRIU = np.ascontiguousarray(np.triu(np.ones((128, 128), np.float32)))


def _scales_mat(s, nb):
    return np.ascontiguousarray(np.asarray(s, np.float32).reshape(nb, 128).T)


def _perm_wparam(wp):
    idx = []
    for h in range(64):
        idx.extend(range(h * 129 + 1, h * 129 + 65))
    for h in range(64):
        idx.extend(range(h * 129 + 65, h * 129 + 129))
    idx.extend(h * 129 for h in range(64))
    return np.ascontiguousarray(wp[np.asarray(idx)])


def _launch(key, in_maps):
    prog = _get_prog(key)
    res = run_bass_kernel_spmd(prog, in_maps, core_ids=list(range(NCORES)))
    return res.results


def run_ffn(x_full, gw, uw, dw, gs, us, ds, quant):
    key = "ffn_int8" if quant == "int8" else "ffn_ternary"
    gs_m, us_m, ds_m = _scales_mat(gs, NFB), _scales_mat(us, NFB), _scales_mat(ds, NDB)
    out = np.empty_like(x_full)
    for g in range(2):
        in_maps = []
        for c in range(NCORES):
            base = c * TPC + g * 1024
            in_maps.append({
                "x": x_full[base:base + 1024],
                "gw": gw, "uw": uw, "dw": dw,
                "gs_m": gs_m, "us_m": us_m, "ds_m": ds_m,
                "ident_bf": _IDENT_BF,
            })
        res = _launch(key, in_maps)
        for c in range(NCORES):
            base = c * TPC + g * 1024
            out[base:base + 1024] = np.asarray(res[c]["xout"])
    return out


def _xpos_tables(core):
    import math as _m
    tpc = TPC
    d = RD
    half = d // 2
    pos = np.arange(core * tpc, (core + 1) * tpc, dtype=np.float64)
    xscale = (np.arange(0, d, 2, dtype=np.float64) + 0.4 * d) / (1.4 * d)
    scale = xscale[None, :] ** (pos[:, None] / 512.0)
    inv_freq = 1.0 / 10000.0 ** (np.arange(half, dtype=np.float64) / half)
    sinu = pos[:, None] * inv_freq[None, :]
    sin, cos = np.sin(sinu), np.cos(sinu)
    dup = lambda t: np.repeat(t, 2, axis=-1)
    f32 = lambda a: np.ascontiguousarray(a.T, np.float32)
    return (f32(dup(cos * scale)), f32(dup(sin * scale)),
            f32(dup(cos / scale)), f32(dup(sin / scale)))


def _ret_tables(core):
    g = _gammas()
    j = np.arange(128)
    deccol = np.stack([gh ** (128 - j) for gh in g], axis=1)
    gcol = np.stack([gh ** j for gh in g], axis=1)
    dt_t = np.zeros((RH, 128, 128))
    for h in range(RH):
        n = np.arange(128)
        D = np.where(n[:, None] >= n[None, :], g[h] ** (n[:, None] - n[None, :]), 0.0)
        dt_t[h] = D.T
    coef = np.zeros((NCORES, RH))
    for jj in range(NCORES):
        if jj < core:
            coef[jj] = g ** (TPC * (core - 1 - jj))
    coefb = np.broadcast_to(coef.reshape(1, NCORES * RH), (128, NCORES * RH))
    rt = np.zeros((128, 128))
    for k in range(64):
        rt[2 * k + 1, 2 * k] = -1.0
        rt[2 * k, 2 * k + 1] = 1.0
    c32 = lambda a: np.ascontiguousarray(a, np.float32)
    return (c32(deccol), c32(gcol), c32(dt_t), c32(coefb),
            rt.astype(ml_dtypes.bfloat16))


def run_mamba(x_full, w_in, wp_orig, w_out):
    wpp = _perm_wparam(np.asarray(wp_orig, np.float32))
    w_in = np.asarray(w_in, np.float32)
    w_out = np.asarray(w_out, np.float32)
    xs = [x_full[c * TPC:(c + 1) * TPC] for c in range(NCORES)]

    m1 = _launch("m1", [{"x": xs[c], "wp": wpp, "ident_bf": _IDENT_BF,
                         "ident_f": _IDENT_F} for c in range(NCORES)])
    m2a = _launch("m2a", [{"x": xs[c], "w_in": w_in, "ident_bf": _IDENT_BF,
                           "ident_f": _IDENT_F} for c in range(NCORES)])
    ys = []
    for part, key in ((0, "m3a0"), (1, "m3a1")):
        maps = [{"bt_s": m1[c]["bt_s"], "ct_s": m1[c]["ct_s"], "la_s": m1[c]["la_s"],
                 "xtok_s": m2a[c]["xtok_s"], "ident_bf": _IDENT_BF, "ident_f": _IDENT_F,
                 "ones_f": _ONES_F, "triu_f": _TRIU, "tril": _TRIL}
                for c in range(NCORES)]
        ys.append(_launch(key, maps))
    m3b = _launch("m3b", [{
        "x": xs[c],
        "y_s": np.ascontiguousarray(np.concatenate(
            [np.asarray(ys[0][c]["y_s"]), np.asarray(ys[1][c]["y_s"])])),
        "res_s": m2a[c]["res_s"], "w_out": w_out,
        "ident_bf": _IDENT_BF, "ident_f": _IDENT_F} for c in range(NCORES)])
    out = np.empty_like(x_full)
    for c in range(NCORES):
        out[c * TPC:(c + 1) * TPC] = np.asarray(m3b[c]["xout"])
    return out


def run_retention(x_full, wq, wk, wv, wg, wo, gnw, gnb):
    wq = np.asarray(wq, np.float32)
    wk = np.asarray(wk, np.float32)
    wv = np.asarray(wv, np.float32)
    wg = np.asarray(wg, np.float32)
    wo = np.asarray(wo, np.float32)
    gnw_b = np.ascontiguousarray(np.broadcast_to(np.asarray(gnw, np.float32), (128, DM)))
    gnb_b = np.ascontiguousarray(np.broadcast_to(np.asarray(gnb, np.float32), (128, DM)))
    xs = [x_full[c * TPC:(c + 1) * TPC] for c in range(NCORES)]

    maps = []
    for c in range(NCORES):
        cq, sq, ck, sk = _xpos_tables(c)
        deccol, gcol, dt_t, coefb, rt = _ret_tables(c)
        maps.append({"x": xs[c], "wq": wq, "wk": wk, "wv": wv, "wg": wg,
                     "cq": cq, "sq": sq, "ck": ck, "sk": sk, "rt_bf": rt,
                     "deccol": deccol, "ident_bf": _IDENT_BF, "ident_f": _IDENT_F})
    r1 = _launch("r1", maps)
    s_all = np.ascontiguousarray(np.stack([np.asarray(r1[c]["slocr_s"]) for c in range(NCORES)]))
    maps = []
    for c in range(NCORES):
        deccol, gcol, dt_t, coefb, rt = _ret_tables(c)
        maps.append({"x": xs[c], "qx_s": r1[c]["qx_s"], "kx_s": r1[c]["kx_s"],
                     "v_s": r1[c]["v_s"], "gate_s": r1[c]["gate_s"],
                     "stret_s": r1[c]["stret_s"], "s_all": s_all, "coefb": coefb,
                     "wo": wo, "gnw_b": gnw_b, "gnb_b": gnb_b, "dt_t": dt_t,
                     "gcol": gcol, "ident_bf": _IDENT_BF, "ident_f": _IDENT_F})
    r2 = _launch("r2", maps)
    out = np.empty_like(x_full)
    for c in range(NCORES):
        out[c * TPC:(c + 1) * TPC] = np.asarray(r2[c]["xout"])
    return out


def kernel(L0_attn_x, L0_attn_w_in, L0_attn_w_param, L0_attn_w_out,
           L0_ffn_gate_w, L0_ffn_up_w, L0_ffn_down_w,
           L0_ffn_gate_s, L0_ffn_up_s, L0_ffn_down_s,
           L1_attn_W_Q, L1_attn_W_K, L1_attn_W_V, L1_attn_W_G, L1_attn_W_O,
           L1_attn_gn_weight, L1_attn_gn_bias,
           L1_ffn_wq_gate, L1_ffn_s_gate, L1_ffn_wq_up, L1_ffn_s_up,
           L1_ffn_wq_down, L1_ffn_s_down):
    x = np.ascontiguousarray(np.asarray(L0_attn_x, np.float32).reshape(SEQ, DM))
    f0 = (np.asarray(L0_ffn_gate_w, np.float32), np.asarray(L0_ffn_up_w, np.float32),
          np.asarray(L0_ffn_down_w, np.float32), L0_ffn_gate_s, L0_ffn_up_s, L0_ffn_down_s)
    f1 = (np.asarray(L1_ffn_wq_gate, np.int32), np.asarray(L1_ffn_wq_up, np.int32),
          np.asarray(L1_ffn_wq_down, np.int32), L1_ffn_s_gate, L1_ffn_s_up, L1_ffn_s_down)

    x = run_ffn(x, f0[0], f0[1], f0[2], f0[3], f0[4], f0[5], "ternary")
    x = run_mamba(x, L0_attn_w_in, L0_attn_w_param, L0_attn_w_out)
    x = run_ffn(x, f0[0], f0[1], f0[2], f0[3], f0[4], f0[5], "ternary")
    x = run_ffn(x, f1[0], f1[1], f1[2], f1[3], f1[4], f1[5], "int8")
    x = run_retention(x, L1_attn_W_Q, L1_attn_W_K, L1_attn_W_V, L1_attn_W_G,
                      L1_attn_W_O, L1_attn_gn_weight, L1_attn_gn_bias)
    x = run_ffn(x, f1[0], f1[1], f1[2], f1[3], f1[4], f1[5], "int8")
    return x.reshape(1, SEQ, DM)



MHEADS = 64
MDS = 64
MDH = 64
NCH = 16


def _mk_common(nc, em, es, with_xnT=True):
    sb = lambda name, shape, dt: es.enter_context(nc.sbuf_tensor(name, shape, dt))
    ps = lambda name, shape, dt: es.enter_context(nc.psum_tensor(name, shape, dt))
    sem = lambda name: es.enter_context(nc.semaphore(name))
    d = {}
    d["stat"] = sb("stat", [128, 8], F32)
    d["ident_bf"] = sb("ident_sb", [128, 128], BF16)
    d["ident_f"] = sb("identf_sb", [128, 128], F32)
    for s in ("ld_c", "st", "t_sem", "v_sem", "s_sem"):
        d[s] = sem(s)
    em.set_self_sems({"vector": d["v_sem"], "scalar": d["s_sem"]})
    em.set_milestone_bufs(d["stat"][:, 6:7], d["stat"][:, 7:8])
    if with_xnT:
        d["xbuf"] = [sb(f"xbuf{i}", [128, DM], F32) for i in range(2)]
        d["xn"] = sb("xn", [128, DM], BF16)
        d["xnT"] = sb("xnT", [128, NDB * TPC], BF16)
        d["ld_x"] = [sem("ld_x0"), sem("ld_x1")]
        d["ps_tr"] = [ps(f"ps_tr{i}", [128, 1024], BF16) for i in range(2)]
    return d


def emit_const_prologue(em, d, ident_ap, identf_ap, extra_consts=()):
    em.dma("sync", d["ident_bf"][:], ident_ap, d["ld_c"])
    em.dma("sync", d["ident_f"][:], identf_ap, d["ld_c"])
    for t_sb, t_ap in extra_consts:
        em.dma("sync", t_sb, t_ap, d["ld_c"])
    lv = em.val(d["ld_c"])
    em.op("vector", lambda e: e.memset(d["stat"][:, 5:6], LN_EPS))
    em.wait("vector", d["ld_c"], lv)
    em.wait("scalar", d["ld_c"], lv)
    em.wait("tensor", d["ld_c"], lv)
    em.milestone("vector", d["v_sem"])


def build_m1():
    nc = bass.Bass("TRN2", target_bir_lowering=False)
    x = nc.dram_tensor("x", [TPC, DM], F32, kind="ExternalInput").ap()
    wp = nc.dram_tensor("wp", [8256, DM], F32, kind="ExternalInput").ap()
    ident = nc.dram_tensor("ident_bf", [128, 128], BF16, kind="ExternalInput").ap()
    identf = nc.dram_tensor("ident_f", [128, 128], F32, kind="ExternalInput").ap()
    bt_s = nc.dram_tensor("bt_s", [TPC, 4096], BF16, kind="ExternalOutput").ap()
    ct_s = nc.dram_tensor("ct_s", [TPC, 4096], BF16, kind="ExternalOutput").ap()
    la_s = nc.dram_tensor("la_s", [TPC, MHEADS], F32, kind="ExternalOutput").ap()

    em = Emitter()
    es = ExitStack()
    with es:
        sb = lambda name, shape, dt: es.enter_context(nc.sbuf_tensor(name, shape, dt))
        ps = lambda name, shape, dt: es.enter_context(nc.psum_tensor(name, shape, dt))
        d = _mk_common(nc, em, es)
        xnT, ps_tr, ident_bf = d["xnT"], d["ps_tr"], d["ident_bf"]
        ld_c, st = d["ld_c"], d["st"]
        t_sem, v_sem, s_sem = d["t_sem"], d["v_sem"], d["s_sem"]

        w_f = [sb(f"w_f{i}", [128, DM], F32) for i in range(2)]
        ld_w = [es.enter_context(nc.semaphore(f"ld_w{i}")) for i in range(2)]
        w_b = sb("w_b", [128, DM], BF16)
        wT = sb("wT", [128, DM], BF16)
        stage = sb("stage", [128, TPC], BF16)
        tokstage = sb("tokstage", [128, TPC], BF16)
        dt_fm = sb("dt_fm", [64, TPC], F32)
        tmp_fm = sb("tmp_fm", [64, TPC], F32)
        la_fm = sb("la_fm", [64, TPC], F32)
        la_tk = sb("la_tk", [128, NCH * 64], F32)
        ps_p = [ps(f"ps_p{i}", [128, 512], F32) for i in range(2)]
        ps_c = ps("ps_c", [128, 512], F32)

        sems = dict(ld_x=d["ld_x"], s=s_sem, v=v_sem, t=t_sem)
        bufs = dict(xbuf=d["xbuf"], stat=d["stat"], ps_tr=ps_tr, xn=d["xn"],
                    ident_bf=ident_bf)

        emit_const_prologue(em, d, ident, identf)
        emit_ln_to_xnT(em, x, xnT, 16, TPC, sems, bufs)
        em.barrier({"vector": v_sem, "tensor": t_sem, "scalar": s_sem})

        state = {"i": 0}

        def proj_block(rows_ap, nrows, out_cb):
            i = state["i"]
            par = i % 2
            if i >= 2:
                em.wait("sync", v_sem, em.val(v_sem))
            lv = em.dma("sync", w_f[par][0:nrows, :], rows_ap, ld_w[par])
            em.wait("vector", ld_w[par], lv)
            em.wait("vector", t_sem, em.val(t_sem))
            vv = em.op("vector", lambda e, nr=nrows, p2=par: e.tensor_copy(
                w_b[0:nr, :], w_f[p2][0:nr, :]))
            em.wait("tensor", v_sem, vv)
            tv = None
            for j in range(16):
                tv = em.op("tensor", lambda e, j2=j, nr=nrows: e.transpose(
                    ps_tr[j2 // 8][:, (j2 % 8) * 128:(j2 % 8) * 128 + nr],
                    w_b[0:nr, _ts(j2)], ident_bf[0:nr, 0:nr]),
                    t_sem if j == 15 else None)
            em.wait("vector", t_sem, tv)
            vv = None
            for b2 in range(2):
                vv = em.op("vector", lambda e, b3=b2, nr=nrows: e.tensor_copy(
                    wT[:].rearrange("p (a c) -> p a c", c=128)[:, b3 * 8:(b3 + 1) * 8, 0:nr],
                    ps_tr[b3][:].rearrange("p (a c) -> p a c", c=128)[:, :, 0:nr]))
            em.wait("tensor", v_sem, vv)
            for tq in range(4):
                pb = tq % 2
                if i + tq > 0:
                    em.wait("tensor", v_sem, em.val(v_sem))
                tv = None
                for db in range(NDB):
                    tv = em.op("tensor", lambda e, db2=db, tq2=tq, pb2=pb, nr=nrows: e.matmul(
                        ps_p[pb2][0:nr, :], wT[:, db2 * 128:db2 * 128 + nr],
                        xnT[:, db2 * TPC + tq2 * 512: db2 * TPC + tq2 * 512 + 512],
                        start=(db2 == 0), stop=(db2 == NDB - 1)),
                        t_sem if db == NDB - 1 else None)
                em.wait("vector", t_sem, tv)
                out_cb(tq, ps_p[pb][0:nrows, :])
            state["i"] += 1

        def tok_spill(dst, fb):
            vv = em.val(v_sem)
            em.wait("tensor", v_sem, vv)
            tv = None
            for j in range(16):
                tv = em.op("tensor", lambda e, j2=j: e.transpose(
                    ps_tr[j2 // 8][:, _ts(j2 % 8)], stage[:, _ts(j2)], ident_bf[:]),
                    t_sem if j == 15 else None)
            em.wait("vector", t_sem, tv)
            em.wait("vector", st, em.val(st))
            vv = None
            for b2 in range(2):
                vv = em.op("vector", lambda e, b3=b2: e.tensor_copy(
                    tokstage[:, b3 * 1024:(b3 + 1) * 1024], ps_tr[b3][:]))
            em.wait("tensor", v_sem, vv)
            em.wait("gpsimd", v_sem, vv)
            em.dma("gpsimd",
                   dst[:, fb * 128:(fb + 1) * 128].rearrange("(c p) f -> p c f", p=128),
                   tokstage[:].rearrange("p (c f) -> p c f", f=128), st)

        for mat, dst in ((0, bt_s), (1, ct_s)):
            for fb in range(32):
                def cb(tq, psum):
                    em.op("vector", lambda e, tq2=tq, p2=psum: e.tensor_copy(
                        stage[:, tq2 * 512:(tq2 + 1) * 512], p2), v_sem)
                proj_block(wp[mat * 4096 + fb * 128: mat * 4096 + (fb + 1) * 128, :], 128, cb)
                tok_spill(dst, fb)

        def cb_dt(tq, psum):
            em.op("vector", lambda e, tq2=tq, p2=psum: e.tensor_copy(
                dt_fm[:, tq2 * 512:(tq2 + 1) * 512], p2), v_sem)
        proj_block(wp[8192:8256, :], 64, cb_dt)

        em.wait("scalar", v_sem, em.val(v_sem))
        em.op("scalar", lambda e: e.activation(out=tmp_fm[:], in_=dt_fm[:], func=AF.Abs))
        em.op("scalar", lambda e: e.activation(out=tmp_fm[:], in_=tmp_fm[:], func=AF.Exp, scale=-1.0))
        em.op("scalar", lambda e: e.activation(out=tmp_fm[:], in_=tmp_fm[:], func=AF.Ln, bias=1.0))
        sv = em.op("scalar", lambda e: e.activation(out=dt_fm[:], in_=dt_fm[:], func=AF.Relu), s_sem)
        em.wait("vector", s_sem, sv)
        em.op("vector", lambda e: e.tensor_add(la_fm[:], dt_fm[:], tmp_fm[:]))
        vv = em.op("vector", lambda e: e.tensor_scalar_mul(la_fm[:], la_fm[:], -1.0), v_sem)
        em.wait("tensor", v_sem, vv)
        for c in range(NCH):
            tv = em.op("tensor", lambda e, c2=c: e.transpose(
                ps_c[0:128, 0:64], la_fm[:, c2 * 128:(c2 + 1) * 128],
                d["ident_f"][0:64, 0:64]), t_sem)
            em.wait("vector", t_sem, tv)
            vv = em.op("vector", lambda e, c2=c: e.tensor_copy(
                la_tk[:, c2 * 64:(c2 + 1) * 64], ps_c[0:128, 0:64]), v_sem)
            em.wait("tensor", v_sem, vv)
        em.wait("gpsimd", v_sem, em.val(v_sem))
        em.wait("gpsimd", st, em.val(st))
        em.dma("gpsimd", la_s.rearrange("(c p) h -> p c h", p=128),
               la_tk[:].rearrange("p (c h) -> p c h", h=64), st)
        em.wait("gpsimd", st, em.val(st))

        blk = es.enter_context(nc.Block())
        em.replay(blk)
    return nc


def build_m2a():
    nc = bass.Bass("TRN2", target_bir_lowering=False)
    x = nc.dram_tensor("x", [TPC, DM], F32, kind="ExternalInput").ap()
    w_in = nc.dram_tensor("w_in", [8192, DM], F32, kind="ExternalInput").ap()
    ident = nc.dram_tensor("ident_bf", [128, 128], BF16, kind="ExternalInput").ap()
    identf = nc.dram_tensor("ident_f", [128, 128], F32, kind="ExternalInput").ap()
    xtok_s = nc.dram_tensor("xtok_s", [NCH, 128, 4096], BF16, kind="ExternalOutput").ap()
    res_s = nc.dram_tensor("res_s", [NCH, 128, 4096], BF16, kind="ExternalOutput").ap()

    em = Emitter()
    es = ExitStack()
    with es:
        sb = lambda name, shape, dt: es.enter_context(nc.sbuf_tensor(name, shape, dt))
        ps = lambda name, shape, dt: es.enter_context(nc.psum_tensor(name, shape, dt))
        d = _mk_common(nc, em, es)
        xnT, ps_tr, ident_bf = d["xnT"], d["ps_tr"], d["ident_bf"]
        st = d["st"]
        t_sem, v_sem, s_sem = d["t_sem"], d["v_sem"], d["s_sem"]

        w_f = [sb(f"w_f{i}", [128, DM], F32) for i in range(2)]
        ld_w = [es.enter_context(nc.semaphore(f"ld_w{i}")) for i in range(2)]
        w_b = sb("w_b", [128, DM], BF16)
        wT = sb("wT", [128, DM], BF16)
        stage = sb("stage", [128, TPC], BF16)
        xstage = sb("xstage", [128, TPC], BF16)
        ps_p = [ps(f"ps_p{i}", [128, 512], F32) for i in range(2)]

        sems = dict(ld_x=d["ld_x"], s=s_sem, v=v_sem, t=t_sem)
        bufs = dict(xbuf=d["xbuf"], stat=d["stat"], ps_tr=ps_tr, xn=d["xn"],
                    ident_bf=ident_bf)

        emit_const_prologue(em, d, ident, identf)
        emit_ln_to_xnT(em, x, xnT, 16, TPC, sems, bufs)
        em.barrier({"vector": v_sem, "tensor": t_sem, "scalar": s_sem})

        for fb in range(64):
            is_res = fb >= 32
            par = fb % 2
            if fb >= 2:
                em.wait("sync", v_sem, em.val(v_sem))
            lv = em.dma("sync", w_f[par][:], w_in[_ts(fb), :], ld_w[par])
            em.wait("vector", ld_w[par], lv)
            em.wait("vector", t_sem, em.val(t_sem))
            vv = em.op("vector", lambda e, p2=par: e.tensor_copy(w_b[:], w_f[p2][:]))
            em.wait("tensor", v_sem, vv)
            tv = None
            for j in range(16):
                tv = em.op("tensor", lambda e, j2=j: e.transpose(
                    ps_tr[j2 // 8][:, _ts(j2 % 8)], w_b[:, _ts(j2)], ident_bf[:]),
                    t_sem if j == 15 else None)
            em.wait("vector", t_sem, tv)
            vv = None
            for b2 in range(2):
                vv = em.op("vector", lambda e, b3=b2: e.tensor_copy(
                    wT[:, b3 * 1024:(b3 + 1) * 1024], ps_tr[b3][:]))
            em.wait("tensor", v_sem, vv)
            em.wait("vector", st, em.val(st))
            em.wait("scalar", st, em.val(st))
            for tq in range(4):
                pb = tq % 2
                if fb + tq > 0:
                    em.wait("tensor", v_sem, em.val(v_sem))
                    em.wait("tensor", s_sem, em.val(s_sem))
                tv = None
                for db in range(NDB):
                    tv = em.op("tensor", lambda e, db2=db, tq2=tq, pb2=pb: e.matmul(
                        ps_p[pb2][:], wT[:, _ts(db2)],
                        xnT[:, db2 * TPC + tq2 * 512: db2 * TPC + tq2 * 512 + 512],
                        start=(db2 == 0), stop=(db2 == NDB - 1)),
                        t_sem if db == NDB - 1 else None)
                em.wait("scalar", t_sem, tv)
                if is_res:
                    em.op("scalar", lambda e, tq2=tq, pb2=pb: e.activation(
                        out=stage[:, tq2 * 512:(tq2 + 1) * 512], in_=ps_p[pb2][:],
                        func=AF.Silu), s_sem)
                else:
                    em.wait("vector", t_sem, tv)
                    em.op("vector", lambda e, tq2=tq, pb2=pb: e.tensor_copy(
                        stage[:, tq2 * 512:(tq2 + 1) * 512], ps_p[pb2][:]), v_sem)
            if is_res:
                sv = em.val(s_sem)
                em.wait("gpsimd", s_sem, sv)
                em.dma("gpsimd",
                       res_s[:, :, (fb - 32) * 128:(fb - 31) * 128].rearrange("c p t -> p c t"),
                       stage[:].rearrange("p (c t) -> p c t", t=128), st)
            else:
                vv = em.val(v_sem)
                em.wait("tensor", v_sem, vv)
                tv = None
                for j in range(16):
                    tv = em.op("tensor", lambda e, j2=j: e.transpose(
                        ps_tr[j2 // 8][:, _ts(j2 % 8)], stage[:, _ts(j2)], ident_bf[:]),
                        t_sem if j == 15 else None)
                em.wait("vector", t_sem, tv)
                vv = None
                for b2 in range(2):
                    vv = em.op("vector", lambda e, b3=b2: e.tensor_copy(
                        xstage[:, b3 * 1024:(b3 + 1) * 1024], ps_tr[b3][:]))
                em.wait("tensor", v_sem, vv)
                em.wait("gpsimd", v_sem, vv)
                em.dma("gpsimd",
                       xtok_s[:, :, fb * 128:(fb + 1) * 128].rearrange("c p f -> p c f"),
                       xstage[:].rearrange("p (c t) -> p c t", t=128), st)
        em.wait("gpsimd", st, em.val(st))
        blk = es.enter_context(nc.Block())
        em.replay(blk)
    return nc


def build_m3a(part):
    z0 = part * 4
    nc = bass.Bass("TRN2", target_bir_lowering=False)
    bt_s = nc.dram_tensor("bt_s", [TPC, 4096], BF16, kind="ExternalInput").ap()
    ct_s = nc.dram_tensor("ct_s", [TPC, 4096], BF16, kind="ExternalInput").ap()
    la_s = nc.dram_tensor("la_s", [TPC, MHEADS], F32, kind="ExternalInput").ap()
    xtok_s = nc.dram_tensor("xtok_s", [NCH, 128, 4096], BF16, kind="ExternalInput").ap()
    ident = nc.dram_tensor("ident_bf", [128, 128], BF16, kind="ExternalInput").ap()
    identf = nc.dram_tensor("ident_f", [128, 128], F32, kind="ExternalInput").ap()
    ones_f = nc.dram_tensor("ones_f", [1, 128], F32, kind="ExternalInput").ap()
    triu_f = nc.dram_tensor("triu_f", [128, 128], F32, kind="ExternalInput").ap()
    tril = nc.dram_tensor("tril", [128, 128], F32, kind="ExternalInput").ap()
    y_s = nc.dram_tensor("y_s", [8, 128, 4096], BF16, kind="ExternalOutput").ap()
    x_flat = xtok_s.rearrange("c t f -> (c t) f")
    y_flat = y_s.rearrange("c t f -> (c t) f")

    em = Emitter()
    es = ExitStack()
    with es:
        sb = lambda name, shape, dt: es.enter_context(nc.sbuf_tensor(name, shape, dt))
        ps = lambda name, shape, dt: es.enter_context(nc.psum_tensor(name, shape, dt))
        sem = lambda name: es.enter_context(nc.semaphore(name))
        d = _mk_common(nc, em, es, with_xnT=False)
        ident_bf, st = d["ident_bf"], d["st"]
        t_sem, v_sem, s_sem = d["t_sem"], d["v_sem"], d["s_sem"]

        b_z = [sb(f"b_z{i}", [128, 128 * 64], BF16) for i in range(2)]
        c_z = [sb(f"c_z{i}", [128, 128 * 64], BF16) for i in range(2)]
        x_z = [sb(f"x_z{i}", [128, 128 * 64], BF16) for i in range(2)]
        lt_z = [sb(f"lt_z{i}", [128, 128], F32) for i in range(2)]
        ld_b = [sem(f"ld_b{i}") for i in range(2)]
        ones_sb = sb("ones_sb", [1, 128], F32)
        triu_sb = sb("triu_sb", [128, 128], F32)
        tril_sb = sb("tril_sb", [128, 128], F32)
        la_mat = sb("la_mat", [128, 128], F32)
        cs_mat = sb("cs_mat", [128, 128], F32)
        row_sb = sb("row_sb", [1, 128], F32)
        e1m = sb("e1m", [128, 128], F32)
        e2m = sb("e2m", [128, 128], F32)
        ecm = sb("ecm", [128, 128], F32)
        decm = sb("decm", [128, 128], F32)
        amat = sb("amat", [64, 128], F32)
        bt_t = sb("bt_t", [64, 128], BF16)
        ct_t = sb("ct_t", [64, 128], BF16)
        sl1 = sb("sl1", [128, 128], F32)
        sl2 = sb("sl2", [128, 128], BF16)
        slt = sb("slt", [128, 128], BF16)
        bd = sb("bd", [128, 64], BF16)
        sbf = sb("sbf", [64, 64], BF16)
        e1t = sb("e1t", [128, 64], F32)
        s_z = sb("s_z", [64, 64], F32)
        ystage = sb("ystage", [128, 128 * 64], BF16)
        ps_a = ps("ps_a", [128, 512], F32)
        ps_b2 = ps("ps_b2", [128, 512], F32)
        ps_ex = ps("ps_ex", [128, 512], F32)
        ps_bc2 = ps("ps_bc2", [128, 512], F32)
        ps_y = ps("ps_y", [128, 512], F32)
        ps_yi = ps("ps_yi", [128, 512], F32)
        ps_st = ps("ps_st", [128, 512], F32)
        ps_tr = ps("ps_tr", [128, 512], BF16)

        consts = [(ones_sb[:], ones_f), (triu_sb[:], triu_f), (tril_sb[:], tril)]
        emit_const_prologue(em, d, ident, identf, consts)

        for zi in range(4):
            z = z0 + zi
            par = zi % 2
            r0 = zi * 256
            g0 = z * 256
            if zi >= 2:
                em.wait("sync", v_sem, em.val(v_sem))
                em.wait("sync", t_sem, em.val(t_sem))
            for buf, srcap in ((b_z, bt_s), (c_z, ct_s), (x_z, x_flat)):
                src4 = srcap[g0:g0 + 256, :].rearrange("(c l2) (h s) -> l2 h c s", l2=2, s=64)
                for l2 in range(2):
                    em.dma("sync",
                           buf[par][l2 * 64:(l2 + 1) * 64, :].rearrange("p (c s) -> p c s", s=64),
                           src4[l2], ld_b[par])
            lv = em.dma("sync", lt_z[par][:],
                        la_s[g0:g0 + 256, :].rearrange("(c l2) h -> c (l2 h)", l2=2),
                        ld_b[par])
            em.wait("tensor", ld_b[par], lv)
            em.wait("tensor", v_sem, em.val(v_sem))
            em.wait("tensor", s_sem, em.val(s_sem))
            tv = em.op("tensor", lambda e, p2=par: e.transpose(
                ps_a[0:128, 0:128], lt_z[p2][:], d["ident_f"][:]), t_sem)
            em.wait("vector", t_sem, tv)
            vv = em.op("vector", lambda e: e.tensor_copy(la_mat[:], ps_a[0:128, 0:128]), v_sem)
            em.wait("tensor", v_sem, vv)
            tv = em.op("tensor", lambda e: e.matmul(
                ps_b2[0:128, 0:128], triu_sb[:], la_mat[:], start=True, stop=True), t_sem)
            em.wait("vector", t_sem, tv)
            vv = em.op("vector", lambda e: e.tensor_copy(cs_mat[:], ps_b2[0:128, 0:128]), v_sem)
            em.wait("tensor", v_sem, vv)
            tv = em.op("tensor", lambda e: e.matmul(
                ps_ex[0:1, 0:128], d["ident_f"][:, 64:65], cs_mat[:], start=True, stop=True), t_sem)
            em.wait("vector", t_sem, tv)
            vv = em.op("vector", lambda e: e.tensor_copy(row_sb[:], ps_ex[0:1, 0:128]), v_sem)
            em.wait("tensor", v_sem, vv)
            tv = em.op("tensor", lambda e: e.matmul(
                ps_bc2[0:128, 0:128], ones_sb[:], row_sb[:], start=True, stop=True), t_sem)
            em.wait("vector", t_sem, tv)
            vv = em.op("vector", lambda e: e.tensor_sub(e1m[:], cs_mat[:], ps_bc2[0:128, 0:128]), v_sem)
            em.wait("scalar", v_sem, vv)
            em.op("scalar", lambda e: e.activation(out=e2m[:], in_=e1m[:], func=AF.Exp, scale=-1.0))
            em.op("scalar", lambda e: e.activation(out=e1m[:], in_=e1m[:], func=AF.Exp))
            sv = em.op("scalar", lambda e: e.activation(out=ecm[:], in_=cs_mat[:], func=AF.Exp), s_sem)
            em.wait("tensor", v_sem, em.val(v_sem))
            tv = em.op("tensor", lambda e: e.matmul(
                ps_ex[0:1, 0:128], d["ident_f"][:, 127:128], cs_mat[:], start=True, stop=True), t_sem)
            em.wait("vector", t_sem, tv)
            vv = em.op("vector", lambda e: e.tensor_copy(row_sb[:], ps_ex[0:1, 0:128]), v_sem)
            em.wait("tensor", v_sem, vv)
            tv = em.op("tensor", lambda e: e.matmul(
                ps_bc2[0:128, 0:128], ones_sb[:], row_sb[:], start=True, stop=True), t_sem)
            em.wait("vector", t_sem, tv)
            vv = em.op("vector", lambda e: e.tensor_sub(decm[:], ps_bc2[0:128, 0:128], cs_mat[:]), v_sem)
            em.wait("scalar", v_sem, vv)
            em.op("scalar", lambda e: e.activation(out=decm[:], in_=decm[:], func=AF.Exp))
            sv = em.op("scalar", lambda e: e.activation(
                out=amat[:], in_=ps_bc2[0:64, 0:128], func=AF.Exp), s_sem)
            em.wait("vector", s_sem, sv)
            em.op("vector", lambda e: e.memset(s_z[:], 0.0))
            vv = em.milestone("vector", v_sem)
            em.wait("tensor", v_sem, vv)
            em.wait("tensor", s_sem, em.val(s_sem))
            for c in range(128):
                em.op("tensor", lambda e, c2=c, p2=par: e.transpose(
                    ps_tr[0:64, 0:128], b_z[p2][:, c2 * 64:(c2 + 1) * 64], ident_bf[:]))
                tv = em.op("tensor", lambda e, c2=c, p2=par: e.transpose(
                    ps_tr[0:64, 128:256], c_z[p2][:, c2 * 64:(c2 + 1) * 64], ident_bf[:]), t_sem)
                em.wait("vector", t_sem, tv)
                em.op("vector", lambda e: e.tensor_copy(bt_t[:], ps_tr[0:64, 0:128]))
                vv = em.op("vector", lambda e: e.tensor_copy(ct_t[:], ps_tr[0:64, 128:256]), v_sem)
                em.wait("tensor", v_sem, vv)
                tv = em.op("tensor", lambda e: e.matmul(
                    ps_a[0:128, 0:128], ct_t[:], bt_t[:], start=True, stop=True), t_sem)
                em.wait("scalar", t_sem, tv)
                em.wait("scalar", v_sem, em.val(v_sem))
                sv = em.op("scalar", lambda e, c2=c: e.activation(
                    out=sl1[:], in_=ps_a[0:128, 0:128], func=AF.Copy,
                    scale=e1m[:, c2:c2 + 1]), s_sem)
                em.wait("vector", s_sem, sv)
                vv = em.op("vector", lambda e: e.tensor_mul(sl2[:], sl1[:], tril_sb[:]), v_sem)
                em.wait("tensor", v_sem, vv)
                tv = em.op("tensor", lambda e: e.transpose(
                    ps_tr[0:128, 256:384], sl2[:], ident_bf[:]), t_sem)
                em.wait("vector", t_sem, tv)
                vv = em.op("vector", lambda e, c2=c: e.tensor_scalar(
                    out=slt[:], in0=ps_tr[0:128, 256:384], scalar1=e2m[:, c2:c2 + 1],
                    scalar2=None, op0=ALU.mult), v_sem)
                em.op("vector", lambda e: e.tensor_copy(sbf[:], s_z[:]))
                vv = em.op("vector", lambda e, c2=c, p2=par: e.tensor_scalar(
                    out=bd[:, 0:64], in0=b_z[p2][:, c2 * 64:(c2 + 1) * 64],
                    scalar1=decm[:, c2:c2 + 1], scalar2=None, op0=ALU.mult), v_sem)
                em.wait("tensor", v_sem, vv)
                em.op("tensor", lambda e, c2=c, p2=par: e.matmul(
                    ps_y[0:128, 0:64], slt[:], x_z[p2][:, c2 * 64:(c2 + 1) * 64],
                    start=True, stop=True))
                em.op("tensor", lambda e: e.matmul(
                    ps_yi[0:128, 0:64], ct_t[:], sbf[:], start=True, stop=True))
                tv = em.op("tensor", lambda e, c2=c, p2=par: e.matmul(
                    ps_st[0:64, 0:64], bd[:, 0:64], x_z[p2][:, c2 * 64:(c2 + 1) * 64],
                    start=True, stop=True), t_sem)
                em.wait("vector", t_sem, tv)
                em.op("vector", lambda e, c2=c: e.tensor_scalar(
                    out=e1t[:, 0:64], in0=ps_yi[0:128, 0:64], scalar1=ecm[:, c2:c2 + 1],
                    scalar2=None, op0=ALU.mult))
                em.op("vector", lambda e, c2=c: e.tensor_add(
                    ystage[:, c2 * 64:(c2 + 1) * 64], e1t[:, 0:64], ps_y[0:128, 0:64]))
                vv = em.op("vector", lambda e, c2=c: e.scalar_tensor_tensor(
                    out=s_z[:], in0=s_z[:], scalar=amat[:, c2:c2 + 1],
                    in1=ps_st[0:64, 0:64], op0=ALU.mult, op1=ALU.add), v_sem)
                em.wait("tensor", v_sem, vv)
            vv = em.val(v_sem)
            em.wait("gpsimd", v_sem, vv)
            em.wait("gpsimd", st, em.val(st))
            dst4 = y_flat[r0:r0 + 256, :].rearrange("(c l2) (h v) -> l2 h c v", l2=2, v=64)
            for l2 in range(2):
                em.dma("gpsimd", dst4[l2],
                       ystage[l2 * 64:(l2 + 1) * 64, :].rearrange("p (c v) -> p c v", v=64),
                       st)
            em.wait("vector", st, em.val(st))
        em.wait("gpsimd", st, em.val(st))
        blk = es.enter_context(nc.Block())
        em.replay(blk)
    return nc


def build_m3b():
    nc = bass.Bass("TRN2", target_bir_lowering=False)
    x = nc.dram_tensor("x", [TPC, DM], F32, kind="ExternalInput").ap()
    y_s = nc.dram_tensor("y_s", [NCH, 128, 4096], BF16, kind="ExternalInput").ap()
    res_s = nc.dram_tensor("res_s", [NCH, 128, 4096], BF16, kind="ExternalInput").ap()
    w_out = nc.dram_tensor("w_out", [DM, 4096], F32, kind="ExternalInput").ap()
    ident = nc.dram_tensor("ident_bf", [128, 128], BF16, kind="ExternalInput").ap()
    identf = nc.dram_tensor("ident_f", [128, 128], F32, kind="ExternalInput").ap()
    xout = nc.dram_tensor("xout", [TPC, DM], F32, kind="ExternalOutput").ap()
    yg_s = nc.dram_tensor("yg_s", [NCH, 128, 4096], BF16, kind="ExternalOutput").ap()

    em = Emitter()
    es = ExitStack()
    with es:
        sb = lambda name, shape, dt: es.enter_context(nc.sbuf_tensor(name, shape, dt))
        ps = lambda name, shape, dt: es.enter_context(nc.psum_tensor(name, shape, dt))
        sem = lambda name: es.enter_context(nc.semaphore(name))
        d = _mk_common(nc, em, es, with_xnT=False)
        ident_bf, st = d["ident_bf"], d["st"]
        t_sem, v_sem, s_sem = d["t_sem"], d["v_sem"], d["s_sem"]

        y_c = [sb(f"y_c{i}", [128, 4096], BF16) for i in range(2)]
        r_c = [sb(f"r_c{i}", [128, 4096], BF16) for i in range(2)]
        yg_c = [sb(f"yg_c{i}", [128, 4096], BF16) for i in range(2)]
        ld_y = [sem(f"ld_y{i}") for i in range(2)]
        ld_g = [sem(f"ld_g{i}") for i in range(2)]
        w_f = [sb(f"w_f{i}", [128, 4096], F32) for i in range(2)]
        ld_w = [sem(f"ld_w{i}") for i in range(2)]
        w_b = sb("w_b", [128, 4096], BF16)
        woT = sb("woT", [128, 32 * 512], BF16)
        xsl = [sb(f"xsl{i}", [128, 512], F32) for i in range(2)]
        xosl = [sb(f"xosl{i}", [128, 512], F32) for i in range(2)]
        ld_xs = [sem(f"ld_xs{i}") for i in range(2)]
        ps_tr = [ps(f"ps_tr{i}", [128, 512], BF16) for i in range(2)]
        ps_o = [ps(f"ps_o{i}", [128, 512], F32) for i in range(2)]

        emit_const_prologue(em, d, ident, identf)

        for c in range(NCH):
            par = c % 2
            if c >= 2:
                em.wait("sync", v_sem, em.val(v_sem))
            l1 = em.dma("sync", y_c[par][:], y_s[c], ld_y[par])
            lv = em.dma("sync", r_c[par][:], res_s[c], ld_y[par])
            em.wait("tensor", ld_y[par], lv)
            if c >= 2:
                em.wait("tensor", st, em.val(st))
            for p2 in range(32):
                pb = p2 % 2
                if p2 >= 2 or c > 0:
                    em.wait("tensor", v_sem, em.val(v_sem))
                for half in range(2):
                    tv = em.op("tensor", lambda e, p3=p2, h2=half, par2=par, pb2=pb: e.transpose(
                        ps_tr[pb2][h2 * 64:(h2 + 1) * 64, 0:128],
                        y_c[par2][:, p3 * 128 + h2 * 64: p3 * 128 + (h2 + 1) * 64],
                        ident_bf[:]), t_sem if half == 1 else None)
                em.wait("vector", t_sem, tv)
                em.op("vector", lambda e, p3=p2, par2=par, pb2=pb: e.tensor_mul(
                    yg_c[par2][:, p3 * 128:(p3 + 1) * 128], ps_tr[pb2][0:128, 0:128],
                    r_c[par2][:, p3 * 128:(p3 + 1) * 128]), v_sem)
            vv = em.val(v_sem)
            em.wait("gpsimd", v_sem, vv)
            em.dma("gpsimd", yg_s[c], yg_c[par][:], st)
        em.wait("gpsimd", st, em.val(st))
        em.barrier({"vector": v_sem, "tensor": t_sem, "scalar": s_sem})

        for q in range(4):
            for r in range(4):
                par = r % 2
                dmb = q * 4 + r
                if q + r > 1:
                    em.wait("sync", v_sem, em.val(v_sem))
                lv = em.dma("sync", w_f[par][:], w_out[_ts(dmb), :], ld_w[par])
                em.wait("vector", ld_w[par], lv)
                em.wait("vector", t_sem, em.val(t_sem))
                vv = em.op("vector", lambda e, p2=par: e.tensor_copy(w_b[:], w_f[p2][:]), v_sem)
                em.wait("tensor", v_sem, vv)
                done = 0
                woT_r = woT[:].rearrange("p (fb c) -> p fb c", c=512)
                for batch, nb in ((0, 4), (1, 4), (0, 4), (1, 4), (0, 4), (1, 4), (0, 4), (1, 4)):
                    if done >= 8:
                        em.wait("tensor", v_sem, em.val(v_sem))
                    tv = None
                    for j in range(nb):
                        fb = done + j
                        tv = em.op("tensor", lambda e, b2=batch, j2=j, f2=fb: e.transpose(
                            ps_tr[b2][:, j2 * 128:(j2 + 1) * 128], w_b[:, _ts(f2)], ident_bf[:]),
                            t_sem if j == nb - 1 else None)
                    em.wait("vector", t_sem, tv)
                    vv = em.op("vector", lambda e, b2=batch, nb2=nb, d0=done, r2=r: e.tensor_copy(
                        woT_r[:, d0:d0 + nb2, r2 * 128:(r2 + 1) * 128],
                        ps_tr[b2][:, 0:nb2 * 128].rearrange("p (a c) -> p a c", c=128)),
                        v_sem)
                    done += nb
                em.wait("tensor", v_sem, vv)
            for c in range(NCH):
                pp = c % 2
                lyg = em.dma("gpsimd", yg_c[pp][:], yg_s[c], ld_g[pp])
                if q + c > 1:
                    em.wait("gpsimd", v_sem, em.val(v_sem))
                lx = em.dma("gpsimd", xsl[pp][:], x[_ts(c), q * 512:(q + 1) * 512], ld_xs[pp])
                if q + c > 1:
                    em.wait("tensor", v_sem, em.val(v_sem))
                em.wait("tensor", ld_g[pp], lyg)
                tv = None
                for fb in range(32):
                    tv = em.op("tensor", lambda e, f2=fb, pp2=pp: e.matmul(
                        ps_o[pp2][:], yg_c[pp2][:, _ts(f2)], woT[:, f2 * 512:(f2 + 1) * 512],
                        start=(f2 == 0), stop=(f2 == 31)), t_sem if fb == 31 else None)
                em.wait("vector", t_sem, tv)
                em.wait("vector", ld_xs[pp], lx)
                em.wait("vector", st, em.val(st))
                vv = em.op("vector", lambda e, pp2=pp: e.tensor_add(
                    xosl[pp2][:], xsl[pp2][:], ps_o[pp2][:]), v_sem)
                em.wait("gpsimd", v_sem, vv)
                em.dma("gpsimd", xout[_ts(c), q * 512:(q + 1) * 512], xosl[pp][:], st)
        em.wait("gpsimd", st, em.val(st))
        blk = es.enter_context(nc.Block())
        em.replay(blk)
    return nc


RH = 16
RD = 128
G128 = None


def _gammas():
    import math as _m
    return 1.0 - np.exp(np.linspace(_m.log(1.0 / 32), _m.log(1.0 / 512), RH))


def build_r1():
    gam = _gammas()
    nc = bass.Bass("TRN2", target_bir_lowering=False)
    x = nc.dram_tensor("x", [TPC, DM], F32, kind="ExternalInput").ap()
    wq = nc.dram_tensor("wq", [RH, DM, RD], F32, kind="ExternalInput").ap()
    wk = nc.dram_tensor("wk", [RH, DM, RD], F32, kind="ExternalInput").ap()
    wv = nc.dram_tensor("wv", [RH, DM, RD], F32, kind="ExternalInput").ap()
    wg = nc.dram_tensor("wg", [DM, DM], F32, kind="ExternalInput").ap()
    cq = nc.dram_tensor("cq", [128, TPC], F32, kind="ExternalInput").ap()
    sq = nc.dram_tensor("sq", [128, TPC], F32, kind="ExternalInput").ap()
    ck = nc.dram_tensor("ck", [128, TPC], F32, kind="ExternalInput").ap()
    sk = nc.dram_tensor("sk", [128, TPC], F32, kind="ExternalInput").ap()
    rt = nc.dram_tensor("rt_bf", [128, 128], BF16, kind="ExternalInput").ap()
    deccol = nc.dram_tensor("deccol", [128, RH], F32, kind="ExternalInput").ap()
    ident = nc.dram_tensor("ident_bf", [128, 128], BF16, kind="ExternalInput").ap()
    identf = nc.dram_tensor("ident_f", [128, 128], F32, kind="ExternalInput").ap()
    qx_s = nc.dram_tensor("qx_s", [NCH, 128, RH, 128], BF16, kind="ExternalOutput").ap()
    kx_s = nc.dram_tensor("kx_s", [NCH, 128, RH, 128], BF16, kind="ExternalOutput").ap()
    v_s = nc.dram_tensor("v_s", [NCH, 128, RH, 128], BF16, kind="ExternalOutput").ap()
    gate_s = nc.dram_tensor("gate_s", [NCH, 128, DM], BF16, kind="ExternalOutput").ap()
    stret_s = nc.dram_tensor("stret_s", [NCH, 128, RH * 128], F32, kind="ExternalOutput").ap()
    slocr_s = nc.dram_tensor("slocr_s", [128, RH * 128], F32, kind="ExternalOutput").ap()

    em = Emitter()
    es = ExitStack()
    with es:
        sb = lambda name, shape, dt: es.enter_context(nc.sbuf_tensor(name, shape, dt))
        ps = lambda name, shape, dt: es.enter_context(nc.psum_tensor(name, shape, dt))
        sem = lambda name: es.enter_context(nc.semaphore(name))
        d = _mk_common(nc, em, es)
        xnT, ps_tr, ident_bf = d["xnT"], d["ps_tr"], d["ident_bf"]
        st = d["st"]
        t_sem, v_sem, s_sem = d["t_sem"], d["v_sem"], d["s_sem"]

        cq_sb = sb("cq_sb", [128, TPC], F32)
        sq_sb = sb("sq_sb", [128, TPC], F32)
        ck_sb = sb("ck_sb", [128, TPC], F32)
        sk_sb = sb("sk_sb", [128, TPC], F32)
        rt_sb = sb("rt_sb", [128, 128], BF16)
        dec_sb = sb("dec_sb", [128, RH], F32)
        wh_sb = [sb(f"wh_sb{i}", [128, 16 * 128], F32) for i in range(2)]
        ld_w = [sem(f"ld_w{i}") for i in range(2)]
        wh_b = sb("wh_b", [128, 16 * 128], BF16)
        qf = sb("qf", [128, TPC], F32)
        qb = sb("qb", [128, 512], BF16)
        rtm = sb("rtm", [128, 512], F32)
        t1 = sb("t1", [128, 512], F32)
        px_sb = sb("px_sb", [128, TPC], BF16)
        vtok = sb("vtok", [128, TPC], BF16)
        kd = sb("kd", [128, 128], BF16)
        ststage = sb("ststage", [128, TPC], F32)
        sloc = sb("sloc", [128, RH * 128], F32)
        wgt = sb("wgt", [128, 512], F32)
        wgt_b = sb("wgt_b", [128, 512], BF16)
        ld_g = sem("ld_g")
        gstage = sb("gstage", [128, DM], BF16)
        ps_p = [ps(f"ps_p{i}", [128, 512], F32) for i in range(2)]
        ps_r = ps("ps_r", [128, 512], F32)
        ps_st2 = [ps(f"ps_st2{i}", [128, 512], F32) for i in range(2)]

        consts = [(cq_sb[:], cq), (sq_sb[:], sq), (ck_sb[:], ck), (sk_sb[:], sk),
                  (rt_sb[:], rt), (dec_sb[:], deccol)]
        emit_const_prologue(em, d, ident, identf, consts)
        em.op("vector", lambda e: e.memset(sloc[:], 0.0))

        sems = dict(ld_x=d["ld_x"], s=s_sem, v=v_sem, t=t_sem)
        bufs = dict(xbuf=d["xbuf"], stat=d["stat"], ps_tr=ps_tr, xn=d["xn"],
                    ident_bf=ident_bf)
        emit_ln_to_xnT(em, x, xnT, 16, TPC, sems, bufs)
        em.barrier({"vector": v_sem, "tensor": t_sem, "scalar": s_sem})

        def proj_head(w_ap, h, par):
            em.wait("scalar", v_sem, em.val(v_sem))
            em.wait("sync", v_sem, em.val(v_sem))
            lv = em.dma("sync", wh_sb[par][:].rearrange("p (a c) -> p a c", c=128),
                        w_ap[h].rearrange("(a p) c -> p a c", p=128), ld_w[par])
            em.wait("vector", ld_w[par], lv)
            em.wait("vector", t_sem, em.val(t_sem))
            vv = em.op("vector", lambda e, p2=par: e.tensor_copy(wh_b[:], wh_sb[p2][:]), v_sem)
            em.wait("tensor", v_sem, vv)
            for tq in range(4):
                pb = tq % 2
                em.wait("tensor", s_sem, em.val(s_sem))
                tv = None
                for db in range(NDB):
                    tv = em.op("tensor", lambda e, db2=db, tq2=tq, pb2=pb: e.matmul(
                        ps_p[pb2][:], wh_b[:, _ts(db2)],
                        xnT[:, db2 * TPC + tq2 * 512: db2 * TPC + tq2 * 512 + 512],
                        start=(db2 == 0), stop=(db2 == NDB - 1)),
                        t_sem if db == NDB - 1 else None)
                em.wait("scalar", t_sem, tv)
                em.op("scalar", lambda e, tq2=tq, pb2=pb: e.activation(
                    out=qf[:, tq2 * 512:(tq2 + 1) * 512], in_=ps_p[pb2][:],
                    func=AF.Copy), s_sem)

        def xpos(c_sb, s_sb, dst_spill, h):
            em.wait("vector", s_sem, em.val(s_sem))
            em.wait("vector", st, em.val(st))
            for tq in range(4):
                vv = em.op("vector", lambda e, tq2=tq: e.tensor_copy(
                    qb[:], qf[:, tq2 * 512:(tq2 + 1) * 512]), v_sem)
                em.wait("tensor", v_sem, vv)
                em.wait("tensor", v_sem, em.val(v_sem))
                tv = em.op("tensor", lambda e: e.matmul(
                    ps_r[:], rt_sb[:], qb[:], start=True, stop=True), t_sem)
                em.wait("vector", t_sem, tv)
                em.op("vector", lambda e, tq2=tq: e.tensor_mul(
                    rtm[:], ps_r[:], s_sb[:, tq2 * 512:(tq2 + 1) * 512]))
                em.op("vector", lambda e, tq2=tq: e.tensor_mul(
                    t1[:], qf[:, tq2 * 512:(tq2 + 1) * 512],
                    c_sb[:, tq2 * 512:(tq2 + 1) * 512]))
                em.op("vector", lambda e, tq2=tq: e.tensor_add(
                    px_sb[:, tq2 * 512:(tq2 + 1) * 512], t1[:], rtm[:]), v_sem)
            vv = em.val(v_sem)
            em.wait("gpsimd", v_sem, vv)
            em.wait("gpsimd", st, em.val(st))
            em.dma("gpsimd", dst_spill[:, :, h, :].rearrange("c d t -> d c t"),
                   px_sb[:].rearrange("p (c t) -> p c t", t=128), st)

        for h in range(RH):
            par = h % 2
            proj_head(wq, h, par)
            xpos(cq_sb, sq_sb, qx_s, h)
            proj_head(wk, h, par)
            xpos(ck_sb, sk_sb, kx_s, h)
            proj_head(wv, h, par)
            em.wait("vector", s_sem, em.val(s_sem))
            vv = em.op("vector", lambda e: e.tensor_copy(vtok[:], qf[:]), v_sem)
            em.wait("tensor", v_sem, vv)
            for c in range(NCH):
                pb = c % 2
                if c >= 2:
                    em.wait("tensor", v_sem, em.val(v_sem))
                tv = em.op("tensor", lambda e, c2=c, pb2=pb: e.transpose(
                    ps_tr[pb2][:, 0:128], vtok[:, c2 * 128:(c2 + 1) * 128].bitcast(BF16)
                    if False else vtok[:, c2 * 128:(c2 + 1) * 128], ident_bf[:]), t_sem)
                em.wait("vector", t_sem, tv)
                em.op("vector", lambda e, c2=c, pb2=pb: e.tensor_copy(
                    px_sb[:, c2 * 128:(c2 + 1) * 128] if False
                    else vtok[:, c2 * 128:(c2 + 1) * 128], ps_tr[pb2][:, 0:128]), v_sem)
            vv = em.val(v_sem)
            em.wait("gpsimd", v_sem, vv)
            em.wait("gpsimd", st, em.val(st))
            em.dma("gpsimd", v_s[:, :, h, :].rearrange("c t v -> t c v"),
                   vtok[:].rearrange("p (c v) -> p c v", v=128), st)
            for c in range(NCH):
                pb = c % 2
                em.wait("tensor", v_sem, em.val(v_sem))
                tv = em.op("tensor", lambda e, c2=c, pb2=pb: e.transpose(
                    ps_tr[pb2][:, 128:256], px_sb[:, c2 * 128:(c2 + 1) * 128],
                    ident_bf[:]), t_sem)
                em.wait("vector", t_sem, tv)
                vv = em.op("vector", lambda e, h2=h, pb2=pb: e.tensor_scalar(
                    out=kd[:], in0=ps_tr[pb2][:, 128:256],
                    scalar1=dec_sb[:, h2:h2 + 1], scalar2=None, op0=ALU.mult), v_sem)
                em.wait("tensor", v_sem, vv)
                tv = em.op("tensor", lambda e, c2=c, pb2=pb: e.matmul(
                    ps_st2[pb2][:, 0:128], kd[:], vtok[:, c2 * 128:(c2 + 1) * 128],
                    start=True, stop=True), t_sem)
                em.wait("vector", t_sem, tv)
                em.op("vector", lambda e, c2=c, pb2=pb: e.tensor_copy(
                    ststage[:, c2 * 128:(c2 + 1) * 128], ps_st2[pb2][0:128, 0:128]))
                em.op("vector", lambda e, h2=h, c2=c: e.scalar_tensor_tensor(
                    out=sloc[:, h2 * 128:(h2 + 1) * 128],
                    in0=sloc[:, h2 * 128:(h2 + 1) * 128],
                    scalar=float(gam[h2] ** 128),
                    in1=ststage[:, c2 * 128:(c2 + 1) * 128],
                    op0=ALU.mult, op1=ALU.add), v_sem)
            vv = em.val(v_sem)
            em.wait("gpsimd", v_sem, vv)
            em.wait("gpsimd", st, em.val(st))
            em.dma("gpsimd", stret_s[:, :, h * 128:(h + 1) * 128].rearrange("c d v -> d c v"),
                   ststage[:].rearrange("p (c v) -> p c v", v=128), st)
            em.wait("vector", st, em.val(st))

        for jb in range(NDB):
            par = jb % 2
            em.wait("sync", v_sem, em.val(v_sem))
            lv = em.dma("sync", wh_sb[par][:].rearrange("p (a c) -> p a c", c=128),
                        wg[:, jb * 128:(jb + 1) * 128].rearrange("(a p) j -> p a j", p=128),
                        ld_w[par])
            em.wait("vector", ld_w[par], lv)
            em.wait("vector", t_sem, em.val(t_sem))
            vv = em.op("vector", lambda e, p2=par: e.tensor_copy(wh_b[:], wh_sb[p2][:]), v_sem)
            em.wait("tensor", v_sem, vv)
            for tq in range(4):
                pb = tq % 2
                em.wait("tensor", s_sem, em.val(s_sem))
                tv = None
                for db in range(NDB):
                    tv = em.op("tensor", lambda e, db2=db, tq2=tq, pb2=pb: e.matmul(
                        ps_p[pb2][:], wh_b[:, _ts(db2)],
                        xnT[:, db2 * TPC + tq2 * 512: db2 * TPC + tq2 * 512 + 512],
                        start=(db2 == 0), stop=(db2 == NDB - 1)),
                        t_sem if db == NDB - 1 else None)
                em.wait("scalar", t_sem, tv)
                em.op("scalar", lambda e, tq2=tq, pb2=pb: e.activation(
                    out=gstage[:, tq2 * 512:(tq2 + 1) * 512], in_=ps_p[pb2][:],
                    func=AF.Silu), s_sem)
            em.wait("vector", s_sem, em.val(s_sem))
            vv = em.milestone("vector", v_sem)
            em.wait("tensor", v_sem, vv)
            em.wait("vector", st, em.val(st))
            for c in range(NCH):
                pb2 = c % 2
                if c >= 2:
                    em.wait("tensor", v_sem, em.val(v_sem))
                tv = em.op("tensor", lambda e, c2=c, p3=pb2: e.transpose(
                    ps_tr[p3][:, 0:128], gstage[:, c2 * 128:(c2 + 1) * 128].bitcast(BF16)
                    if False else gstage[:, c2 * 128:(c2 + 1) * 128], ident_bf[:]),
                    t_sem)
                em.wait("vector", t_sem, tv)
                em.op("vector", lambda e, c2=c, p3=pb2: e.tensor_copy(
                    vtok[:, c2 * 128:(c2 + 1) * 128], ps_tr[p3][:, 0:128]), v_sem)
            vv = em.val(v_sem)
            em.wait("gpsimd", v_sem, vv)
            em.wait("gpsimd", st, em.val(st))
            em.dma("gpsimd", gate_s[:, :, jb * 128:(jb + 1) * 128].rearrange("c t j -> t c j"),
                   vtok[:].rearrange("p (c j) -> p c j", j=128), st)
        vv = em.val(v_sem)
        em.wait("gpsimd", v_sem, vv)
        em.wait("gpsimd", st, em.val(st))
        em.dma("gpsimd", slocr_s, sloc[:], st)
        em.wait("gpsimd", st, em.val(st))
        blk = es.enter_context(nc.Block())
        em.replay(blk)
    return nc


def build_r2():
    gam = _gammas()
    nc = bass.Bass("TRN2", target_bir_lowering=False)
    x = nc.dram_tensor("x", [TPC, DM], F32, kind="ExternalInput").ap()
    qx_s = nc.dram_tensor("qx_s", [NCH, 128, RH, 128], BF16, kind="ExternalInput").ap()
    kx_s = nc.dram_tensor("kx_s", [NCH, 128, RH, 128], BF16, kind="ExternalInput").ap()
    v_s = nc.dram_tensor("v_s", [NCH, 128, RH, 128], BF16, kind="ExternalInput").ap()
    gate_s = nc.dram_tensor("gate_s", [NCH, 128, DM], BF16, kind="ExternalInput").ap()
    stret_s = nc.dram_tensor("stret_s", [NCH, 128, RH * 128], F32, kind="ExternalInput").ap()
    s_all = nc.dram_tensor("s_all", [NCORES, 128, RH * 128], F32, kind="ExternalInput").ap()
    coefb = nc.dram_tensor("coefb", [128, NCORES * RH], F32, kind="ExternalInput").ap()
    wo = nc.dram_tensor("wo", [DM, DM], F32, kind="ExternalInput").ap()
    gnw_b = nc.dram_tensor("gnw_b", [128, DM], F32, kind="ExternalInput").ap()
    gnb_b = nc.dram_tensor("gnb_b", [128, DM], F32, kind="ExternalInput").ap()
    dt_t = nc.dram_tensor("dt_t", [RH, 128, 128], F32, kind="ExternalInput").ap()
    gcol = nc.dram_tensor("gcol", [128, RH], F32, kind="ExternalInput").ap()
    ident = nc.dram_tensor("ident_bf", [128, 128], BF16, kind="ExternalInput").ap()
    identf = nc.dram_tensor("ident_f", [128, 128], F32, kind="ExternalInput").ap()
    xout = nc.dram_tensor("xout", [TPC, DM], F32, kind="ExternalOutput").ap()

    em = Emitter()
    es = ExitStack()
    with es:
        sb = lambda name, shape, dt: es.enter_context(nc.sbuf_tensor(name, shape, dt))
        ps = lambda name, shape, dt: es.enter_context(nc.psum_tensor(name, shape, dt))
        sem = lambda name: es.enter_context(nc.semaphore(name))
        d = _mk_common(nc, em, es, with_xnT=False)
        ident_bf, st = d["ident_bf"], d["st"]
        stat = d["stat"]
        t_sem, v_sem, s_sem = d["t_sem"], d["v_sem"], d["s_sem"]

        qx_c = [sb(f"qx_c{i}", [128, RH * 128], BF16) for i in range(2)]
        kx_c = [sb(f"kx_c{i}", [128, RH * 128], BF16) for i in range(2)]
        v_c = [sb(f"v_c{i}", [128, RH * 128], BF16) for i in range(2)]
        g_c = [sb(f"g_c{i}", [128, DM], BF16) for i in range(2)]
        st_c = [sb(f"st_c{i}", [128, RH * 128], F32) for i in range(2)]
        ld_b = [sem(f"ld_b{i}") for i in range(2)]
        srun = sb("srun", [128, RH * 128], F32)
        sbf = sb("sbf", [128, 128], BF16)
        slt = sb("slt", [128, 128], BF16)
        e1t = sb("e1t", [128, 128], F32)
        ytok = sb("ytok", [128, 128], F32)
        yn = sb("yn", [128, 128], F32)
        ygfm = sb("ygfm", [128, DM], BF16)
        gnw_sb = sb("gnw_sb", [128, DM], F32)
        gnb_sb = sb("gnb_sb", [128, DM], F32)
        dt_sb = sb("dt_sb", [128, RH * 128], F32)
        gcol_sb = sb("gcol_sb", [128, RH], F32)
        wo_f = [sb(f"wo_f{i}", [128, DM], F32) for i in range(2)]
        ld_w = [sem(f"ld_w{i}") for i in range(2)]
        wo_sb = sb("wo_sb", [128, NDB * DM], BF16)
        sallj = sb("sallj", [128, RH * 128], F32)
        coef_sb = sb("coef_sb", [128, NCORES * RH], F32)
        xsl = [sb(f"xsl{i}", [128, 512], F32) for i in range(2)]
        xosl = [sb(f"xosl{i}", [128, 512], F32) for i in range(2)]
        ld_xs = [sem(f"ld_xs{i}") for i in range(2)]
        ldc2 = sem("ldc2")
        ps_sc = [ps(f"ps_sc{i}", [128, 512], F32) for i in range(2)]
        ps_y = ps("ps_y", [128, 512], F32)
        ps_yi = ps("ps_yi", [128, 512], F32)
        ps_tr2 = [ps(f"ps_tr2{i}", [128, 512], BF16) for i in range(2)]
        ps_o = [ps(f"ps_o{i}", [128, 512], F32) for i in range(2)]

        consts = [(gnw_sb[:], gnw_b), (gnb_sb[:], gnb_b),
                  (dt_sb[:].rearrange("p (h i) -> p h i", i=128), dt_t.rearrange("h j i -> j h i")),
                  (gcol_sb[:], gcol),
                  (coef_sb[:], coefb)]
        emit_const_prologue(em, d, ident, identf, consts)

        for r in range(NDB):
            par = r % 2
            if r >= 2:
                em.wait("sync", v_sem, em.val(v_sem))
            lv = em.dma("sync", wo_f[par][:], wo[_ts(r), :], ld_w[par])
            em.wait("vector", ld_w[par], lv)
            em.op("vector", lambda e, r2=r, p2=par: e.tensor_copy(
                wo_sb[:, r2 * DM:(r2 + 1) * DM], wo_f[p2][:]), v_sem)

        em.op("vector", lambda e: e.memset(srun[:], 0.0))
        for j in range(NCORES):
            em.wait("sync", ldc2, em.val(ldc2))
            em.wait("sync", v_sem, em.val(v_sem))
            lv = em.dma("sync", sallj[:], s_all[j], ldc2)
            em.wait("vector", ldc2, lv)
            for h in range(RH):
                em.op("vector", lambda e, j2=j, h2=h: e.scalar_tensor_tensor(
                    out=srun[:, h2 * 128:(h2 + 1) * 128], in0=sallj[:, h2 * 128:(h2 + 1) * 128],
                    scalar=coef_sb[:, j2 * RH + h2: j2 * RH + h2 + 1],
                    in1=srun[:, h2 * 128:(h2 + 1) * 128], op0=ALU.mult, op1=ALU.add))
            em.milestone("vector", v_sem)
        em.barrier({"vector": v_sem, "tensor": t_sem, "scalar": s_sem})

        for c in range(NCH):
            par = c % 2
            if c >= 2:
                em.wait("sync", v_sem, em.val(v_sem))
                em.wait("sync", t_sem, em.val(t_sem))
            em.dma("sync", qx_c[par][:].rearrange("p (h t) -> p h t", t=128),
                   qx_s[c].rearrange("d h t -> d h t"), ld_b[par])
            em.dma("sync", kx_c[par][:].rearrange("p (h t) -> p h t", t=128),
                   kx_s[c].rearrange("d h t -> d h t"), ld_b[par])
            em.dma("sync", v_c[par][:].rearrange("p (h t) -> p h t", t=128),
                   v_s[c].rearrange("t h v -> t h v"), ld_b[par])
            em.dma("sync", g_c[par][:], gate_s[c], ld_b[par])
            lv = em.dma("sync", st_c[par][:], stret_s[c], ld_b[par])
            em.wait("tensor", ld_b[par], lv)
            em.wait("vector", ld_b[par], lv)
            for h in range(RH):
                pb = h % 2
                if h >= 2 or c > 0:
                    em.wait("tensor", v_sem, em.val(v_sem))
                tv = em.op("tensor", lambda e, h2=h, p2=par, pb2=pb: e.matmul(
                    ps_sc[pb2][:, 0:128], kx_c[p2][:, h2 * 128:(h2 + 1) * 128],
                    qx_c[p2][:, h2 * 128:(h2 + 1) * 128], start=True, stop=True), t_sem)
                em.wait("vector", t_sem, tv)
                vv = em.op("vector", lambda e, h2=h, pb2=pb: e.tensor_mul(
                    slt[:], ps_sc[pb2][:, 0:128], dt_sb[:, h2 * 128:(h2 + 1) * 128]), v_sem)
                em.op("vector", lambda e, h2=h: e.tensor_copy(
                    sbf[:], srun[:, h2 * 128:(h2 + 1) * 128]))
                vv = em.milestone("vector", v_sem)
                em.wait("tensor", v_sem, vv)
                em.op("tensor", lambda e, h2=h, p2=par: e.matmul(
                    ps_y[:, 0:128], slt[:], v_c[p2][:, h2 * 128:(h2 + 1) * 128],
                    start=True, stop=True))
                tv = em.op("tensor", lambda e, h2=h, p2=par: e.matmul(
                    ps_yi[:, 0:128], qx_c[p2][:, h2 * 128:(h2 + 1) * 128], sbf[:],
                    start=True, stop=True), t_sem)
                em.wait("vector", t_sem, tv)
                em.op("vector", lambda e, h2=h: e.tensor_scalar(
                    out=e1t[:], in0=ps_yi[:, 0:128], scalar1=gcol_sb[:, h2:h2 + 1],
                    scalar2=None, op0=ALU.mult))
                vv = em.op("vector", lambda e: e.tensor_add(ytok[:], e1t[:], ps_y[:, 0:128]), v_sem)
                em.op("vector", lambda e, h2=h, p2=par: e.scalar_tensor_tensor(
                    out=srun[:, h2 * 128:(h2 + 1) * 128], in0=srun[:, h2 * 128:(h2 + 1) * 128],
                    scalar=float(gam[h2] ** 128), in1=st_c[p2][:, h2 * 128:(h2 + 1) * 128],
                    op0=ALU.mult, op1=ALU.add))
                em.wait("tensor", v_sem, em.val(v_sem))
                em.op("vector", lambda e: e.reduce_sum(stat[:, 0:1], ytok[:], axis=AX.X))
                vv = em.op("vector", lambda e: e.tensor_scalar_mul(stat[:, 1:2], stat[:, 0:1], 1.0 / RD), v_sem)
                vv = em.op("vector", lambda e: e.tensor_scalar(
                    out=ytok[:], in0=ytok[:], scalar1=stat[:, 1:2], scalar2=None,
                    op0=ALU.subtract), v_sem)
                em.wait("scalar", v_sem, vv)
                em.op("scalar", lambda e: e.activation(out=yn[:], in_=ytok[:], func=AF.Square,
                                                       accum_out=stat[:, 2:3]))
                sv = em.op("scalar", lambda e: e.activation(
                    out=stat[:, 3:4], in_=stat[:, 2:3], func=AF.Sqrt, scale=1.0 / RD,
                    bias=stat[:, 5:6]), s_sem)
                em.wait("vector", s_sem, sv)
                em.op("vector", lambda e: e.reciprocal(stat[:, 4:5], stat[:, 3:4]))
                em.op("vector", lambda e: e.tensor_scalar(
                    out=yn[:], in0=ytok[:], scalar1=stat[:, 4:5], scalar2=None, op0=ALU.mult))
                em.op("vector", lambda e, h2=h: e.tensor_mul(
                    yn[:], yn[:], gnw_sb[:, h2 * 128:(h2 + 1) * 128]))
                em.op("vector", lambda e, h2=h: e.tensor_add(
                    yn[:], yn[:], gnb_sb[:, h2 * 128:(h2 + 1) * 128]))
                vv = em.op("vector", lambda e, h2=h, p2=par: e.tensor_mul(
                    slt[:], yn[:], g_c[p2][:, h2 * 128:(h2 + 1) * 128]), v_sem)
                em.wait("tensor", v_sem, vv)
                tv = em.op("tensor", lambda e, pb2=pb: e.transpose(
                    ps_tr2[pb2][:, 0:128], slt[:], ident_bf[:]), t_sem)
                em.wait("vector", t_sem, tv)
                em.op("vector", lambda e, h2=h, pb2=pb: e.tensor_copy(
                    ygfm[:, h2 * 128:(h2 + 1) * 128], ps_tr2[pb2][:, 0:128]), v_sem)
            vv = em.milestone("vector", v_sem)
            em.wait("tensor", v_sem, vv)
            for q in range(4):
                pq = q % 2
                if c + q > 1:
                    em.wait("gpsimd", v_sem, em.val(v_sem))
                lx = em.dma("gpsimd", xsl[pq][:], x[_ts(c), q * 512:(q + 1) * 512], ld_xs[pq])
                if c + q > 1:
                    em.wait("tensor", v_sem, em.val(v_sem))
                tv = None
                for hb in range(NDB):
                    tv = em.op("tensor", lambda e, h2=hb, q2=q, pq2=pq: e.matmul(
                        ps_o[pq2][:], ygfm[:, h2 * 128:(h2 + 1) * 128],
                        wo_sb[:, h2 * DM + q2 * 512: h2 * DM + q2 * 512 + 512],
                        start=(h2 == 0), stop=(h2 == NDB - 1)), t_sem if hb == NDB - 1 else None)
                em.wait("vector", t_sem, tv)
                em.wait("vector", ld_xs[pq], lx)
                em.wait("vector", st, em.val(st))
                vv = em.op("vector", lambda e, pq2=pq: e.tensor_add(
                    xosl[pq2][:], xsl[pq2][:], ps_o[pq2][:]), v_sem)
                em.wait("gpsimd", v_sem, vv)
                em.dma("gpsimd", xout[_ts(c), q * 512:(q + 1) * 512], xosl[pq][:], st)
        em.wait("gpsimd", st, em.val(st))
        blk = es.enter_context(nc.Block())
        em.replay(blk)
    return nc


# version 07 (final)
# speedup vs baseline: 1.3765x; 1.3765x over previous
from contextlib import ExitStack

import numpy as np
import ml_dtypes
import concourse.bass as bass
import concourse.mybir as mybir
from concourse.bass_utils import run_bass_kernel_spmd

F32 = mybir.dt.float32
BF16 = mybir.dt.bfloat16
I32 = mybir.dt.int32
AF = mybir.ActivationFunctionType
ALU = mybir.AluOpType
AX = mybir.AxisListType

NCORES = 8
SEQ = 16384
DM = 2048
TPC = SEQ // NCORES
INTER = 5632
NFB = INTER // 128
NDB = DM // 128
LN_EPS = 1e-5



class Emitter:

    ENGINES = ("sync", "scalar", "vector", "tensor", "gpsimd")
    SELF_SYNC = ("vector", "scalar")

    def __init__(self):
        self.ops = {e: [] for e in self.ENGINES}
        self.cnt = {}
        self.self_sem = {}

    def set_self_sems(self, by_eng):
        self.self_sem = dict(by_eng)

    def wait(self, eng, sem, val):
        if val is None or val <= 0:
            return
        self.ops[eng].append(lambda e, s=sem, v=val: e.wait_ge(s, v))

    def dma(self, eng, out, in_, sem):
        v = self.cnt[sem.name] = self.cnt.get(sem.name, 0) + 16
        self.ops[eng].append(
            lambda e, o=out, i=in_, s=sem: e.dma_start(out=o, in_=i).then_inc(s, 16)
        )
        return v

    def op(self, eng, fn, sem=None):
        if eng in self.SELF_SYNC:
            ssem = self.self_sem[eng]
            assert sem is None or sem is ssem
            cur = self.cnt.get(ssem.name, 0)
            if cur > 0:
                self.ops[eng].append(lambda e, s=ssem, v=cur: e.wait_ge(s, v))
            v = self.cnt[ssem.name] = cur + 1
            self.ops[eng].append(lambda e, f=fn, s=ssem: f(e).then_inc(s, 1))
            return v
        if sem is None:
            self.ops[eng].append(fn)
            return None
        v = self.cnt[sem.name] = self.cnt.get(sem.name, 0) + 1
        self.ops[eng].append(lambda e, f=fn, s=sem: f(e).then_inc(s, 1))
        return v

    def val(self, sem):
        return self.cnt.get(sem.name, 0)

    def milestone(self, eng, sem):
        if eng in self.SELF_SYNC:
            buf = self._ms_v if eng == "vector" else self._ms_s
            if eng == "vector":
                return self.op(eng, lambda e: e.memset(buf, 0.0))
            return self.op(eng, lambda e: e.memzero(buf))
        return self.op(eng, lambda e: e.nop(), sem)

    def set_milestone_bufs(self, vec_buf, sc_buf):
        self._ms_v = vec_buf
        self._ms_s = sc_buf

    def barrier(self, sems_by_eng, waiters=ENGINES):
        targets = []
        tensor_sem = None
        for eng, sem in sems_by_eng.items():
            if eng in self.SELF_SYNC:
                targets.append((self.self_sem[eng], self.milestone(eng, None)))
            else:
                assert eng == "tensor"
                tensor_sem = sem
        if tensor_sem is not None:
            for sem, v in targets:
                self.wait("tensor", sem, v)
            targets.append((tensor_sem, self.milestone("tensor", tensor_sem)))
        for eng in waiters:
            for sem, v in targets:
                if eng == "tensor" and sem is not tensor_sem:
                    continue
                self.wait(eng, sem, v)

    def replay(self, block):
        for eng in self.ENGINES:
            ops = self.ops[eng]
            if not ops:
                continue

            def make(ops_list):
                def body(engine):
                    for f in ops_list:
                        f(engine)
                return body

            getattr(block, eng)(make(ops))


def _ts(i, n=128):
    return slice(i * n, (i + 1) * n)



def emit_ln_to_xnT(em, x_src_ap, xnT, ntt, Tg, sems, bufs):
    ld_x, s_sem, v_sem, t_sem = sems["ld_x"], sems["s"], sems["v"], sems["t"]
    xbuf, stat = bufs["xbuf"], bufs["stat"]
    ps_tr, xn, ident_bf = bufs["ps_tr"], bufs["xn"], bufs["ident_bf"]
    xnT_r = xnT[:].rearrange("p (db t) -> p db t", t=Tg)

    for k in range(ntt):
        xb = xbuf[k % 2]
        if k >= 2:
            em.wait("sync", v_sem, em.val(v_sem))
        lv = em.dma("sync", xb[:], x_src_ap[_ts(k), :], ld_x[k % 2])
        em.wait("vector", ld_x[k % 2], lv)
        em.op("vector", lambda e, xb=xb: e.reduce_sum(stat[:, 0:1], xb[:], axis=AX.X))
        em.op("vector", lambda e: e.tensor_scalar_mul(stat[:, 1:2], stat[:, 0:1], 1.0 / DM))
        vv = em.op("vector", lambda e, xb=xb: e.tensor_scalar(
            out=xb[:], in0=xb[:], scalar1=stat[:, 1:2], scalar2=None,
            op0=ALU.subtract), v_sem)
        em.wait("scalar", v_sem, vv)
        em.op("scalar", lambda e, xb=xb: e.activation(
            out=xn[:], in_=xb[:], func=AF.Square, accum_out=stat[:, 2:3]))
        sv = em.op("scalar", lambda e: e.activation(
            out=stat[:, 3:4], in_=stat[:, 2:3], func=AF.Sqrt,
            scale=1.0 / DM, bias=stat[:, 5:6]), s_sem)
        em.wait("vector", s_sem, sv)
        em.op("vector", lambda e: e.reciprocal(stat[:, 4:5], stat[:, 3:4]))
        em.wait("vector", t_sem, em.val(t_sem))
        vv = em.op("vector", lambda e, xb=xb: e.tensor_scalar(
            out=xn[:], in0=xb[:], scalar1=stat[:, 4:5], scalar2=None,
            op0=ALU.mult), v_sem)
        em.wait("tensor", v_sem, vv)
        for half in range(2):
            tv = None
            for j in range(8):
                db = half * 8 + j
                tv = em.op("tensor", lambda e, h2=half, j2=j, db=db: e.transpose(
                    ps_tr[h2][:, _ts(j2)], xn[:, _ts(db)], ident_bf[:]),
                    t_sem if j == 7 else None)
            em.wait("vector", t_sem, tv)
            vv = em.op("vector", lambda e, h2=half, k2=k: e.tensor_copy(
                xnT_r[:, h2 * 8:(h2 + 1) * 8, _ts(k2)], ps_tr[h2][:]), v_sem)
            em.wait("tensor", v_sem, vv)



def build_ffn(quant):
    nc = bass.Bass("TRN2", target_bir_lowering=False)
    wdt = I32 if quant == "int8" else F32
    x = nc.dram_tensor("x", [1024, DM], F32, kind="ExternalInput").ap()
    gw = nc.dram_tensor("gw", [INTER, DM], wdt, kind="ExternalInput").ap()
    uw = nc.dram_tensor("uw", [INTER, DM], wdt, kind="ExternalInput").ap()
    dw = nc.dram_tensor("dw", [DM, INTER], wdt, kind="ExternalInput").ap()
    gs = nc.dram_tensor("gs_m", [128, NFB], F32, kind="ExternalInput").ap()
    us = nc.dram_tensor("us_m", [128, NFB], F32, kind="ExternalInput").ap()
    ds = nc.dram_tensor("ds_m", [128, NDB], F32, kind="ExternalInput").ap()
    ident = nc.dram_tensor("ident_bf", [128, 128], BF16, kind="ExternalInput").ap()
    xout = nc.dram_tensor("xout", [1024, DM], F32, kind="ExternalOutput").ap()

    NTT = 8
    Tg = NTT * 128
    HALF = 22
    HC = HALF * 128

    em = Emitter()
    es = ExitStack()
    with es:
        sb = lambda name, shape, dt: es.enter_context(nc.sbuf_tensor(name, shape, dt))
        ps = lambda name, shape, dt: es.enter_context(nc.psum_tensor(name, shape, dt))
        sem = lambda name: es.enter_context(nc.semaphore(name))

        xbuf = [sb(f"xbuf{i}", [128, DM], F32) for i in range(2)]
        stat = sb("stat", [128, 8], F32)
        xn = sb("xn", [128, DM], BF16)
        xnT = sb("xnT", [128, NDB * Tg], BF16)
        ident_bf = sb("ident_sb", [128, 128], BF16)
        gs_sb = sb("gs_sb", [128, NFB], F32)
        us_sb = sb("us_sb", [128, NFB], F32)
        ds_sb = sb("ds_sb", [128, NDB], F32)
        wg_r = [sb(f"wg_r{i}", [128, DM], BF16) for i in range(2)]
        wu_r = [sb(f"wu_r{i}", [128, DM], BF16) for i in range(2)]
        wg_b = [sb(f"wg_b{i}", [128, DM], BF16) for i in range(2)]
        wu_b = [sb(f"wu_b{i}", [128, DM], BF16) for i in range(2)]
        wgT = [sb(f"wgT{i}", [128, DM], BF16) for i in range(2)]
        wuT = [sb(f"wuT{i}", [128, DM], BF16) for i in range(2)]
        silu_tmp = sb("silu_tmp", [128, 512], F32)
        h_sb = sb("h_sb", [128, HALF * Tg], BF16)
        wd_r = [sb(f"wd_r{i}", [128, HC], BF16) for i in range(2)]
        wd_b = [sb(f"wd_b{i}", [128, HC], BF16) for i in range(2)]
        dwT = sb("dwT", [128, HALF * 512], BF16)
        xsl = [sb(f"xsl{i}", [128, 512], F32) for i in range(2)]
        xosl = [sb(f"xosl{i}", [128, 512], F32) for i in range(2)]

        ps_tr = [ps(f"ps_tr{i}", [128, 1024], BF16) for i in range(2)]
        ps_g = [ps(f"ps_g{i}", [128, 512], F32) for i in range(2)]
        ps_u = [ps(f"ps_u{i}", [128, 512], F32) for i in range(2)]
        ps_d = [ps(f"ps_d{i}", [128, 512], F32) for i in range(2)]

        ld_c = sem("ld_c")
        ld_x = [sem("ld_x0"), sem("ld_x1")]
        ld_wg = [sem("ld_wg0"), sem("ld_wg1")]
        ld_wu = [sem("ld_wu0"), sem("ld_wu1")]
        ld_wd = [sem("ld_wd0"), sem("ld_wd1")]
        ld_xs = [sem("ld_xs0"), sem("ld_xs1")]
        st = sem("st")
        t_sem = sem("t_sem")
        v_sem = sem("v_sem")
        s_sem = sem("s_sem")

        em.set_self_sems({"vector": v_sem, "scalar": s_sem})
        em.set_milestone_bufs(stat[:, 6:7], stat[:, 7:8])
        sems = dict(ld_x=ld_x, s=s_sem, v=v_sem, t=t_sem)
        bufs = dict(xbuf=xbuf, stat=stat, ps_tr=ps_tr, xn=xn, ident_bf=ident_bf)

        em.dma("sync", ident_bf[:], ident, ld_c)
        em.dma("sync", gs_sb[:], gs, ld_c)
        em.dma("sync", us_sb[:], us, ld_c)
        lv = em.dma("sync", ds_sb[:], ds, ld_c)
        em.wait("vector", ld_c, lv)
        em.op("vector", lambda e: e.tensor_scalar_mul(ds_sb[:], ds_sb[:], 0.5))
        em.op("vector", lambda e: e.memset(stat[:, 5:6], LN_EPS))
        em.milestone("vector", v_sem)
        em.wait("tensor", ld_c, 64)

        cast_done = {"wg": [0, 0], "wu": [0, 0], "wd": [0, 0]}
        tposed = {"wg": [0, 0], "wu": [0, 0], "wd": [0, 0]}

        def dequant(out_bf, wr, scol, fsem=None):
            if quant == "int8":
                return em.op("vector", lambda e: e.tensor_scalar(
                    out=out_bf, in0=wr, scalar1=-128.0, scalar2=scol,
                    op0=ALU.add, op1=ALU.mult), fsem)
            return em.op("vector", lambda e: e.tensor_scalar(
                out=out_bf, in0=wr, scalar1=scol, scalar2=None,
                op0=ALU.mult), fsem)

        emit_ln_to_xnT(em, x, xnT, NTT, Tg, sems, bufs)
        em.barrier({"vector": v_sem, "tensor": t_sem, "scalar": s_sem})

        def load_gu(fb):
            par = fb % 2
            em.wait("gpsimd", v_sem, max(cast_done["wg"][par], cast_done["wu"][par]))
            l1 = em.dma("gpsimd", wg_r[par][:], gw[_ts(fb), :], ld_wg[par])
            l2 = em.dma("gpsimd", wu_r[par][:], uw[_ts(fb), :], ld_wu[par])
            return l1, l2

        def cast_gu(fb, l1, l2):
            par = fb % 2
            em.wait("vector", ld_wg[par], l1)
            em.wait("vector", t_sem, tposed["wg"][par])
            cast_done["wg"][par] = dequant(wg_b[par][:], wg_r[par][:], gs_sb[:, fb:fb + 1], v_sem)
            em.wait("vector", ld_wu[par], l2)
            cast_done["wu"][par] = dequant(wu_b[par][:], wu_r[par][:], us_sb[:, fb:fb + 1], v_sem)

        for half in range(2):
            fb0 = half * HALF
            pend = load_gu(fb0)
            cast_gu(fb0, *pend)
            for i in range(HALF):
                fb = half * HALF + i
                par = fb % 2
                if i + 1 < HALF:
                    nxt = load_gu(fb + 1)
                em.wait("tensor", v_sem, cast_done["wu"][par])
                for mi, (wbv, wTv, key) in enumerate(((wg_b, wgT, "wg"), (wu_b, wuT, "wu"))):
                    tv = None
                    for j in range(16):
                        tv = em.op("tensor", lambda e, s2=wbv[par], j2=j: e.transpose(
                            ps_tr[j2 // 8][:, _ts(j2 % 8)], s2[:, _ts(j2)], ident_bf[:]),
                            t_sem if j == 15 else None)
                    tposed[key][par] = tv
                    em.wait("vector", t_sem, tv)
                    vv = None
                    for b2 in range(2):
                        vv = em.op("vector", lambda e, d2=wTv[par], b3=b2: e.tensor_copy(
                            d2[:, b3 * 1024:(b3 + 1) * 1024], ps_tr[b3][:]),
                            v_sem if b2 == 1 else None)
                    em.wait("tensor", v_sem, vv)
                tv = None
                for wT, psb in ((wgT[par], ps_g), (wuT[par], ps_u)):
                    for th in range(2):
                        for db in range(NDB):
                            last = wT is wuT[par] and th == 1 and db == NDB - 1
                            tv2 = em.op("tensor", lambda e, w2=wT, p2=psb, db2=db, th2=th: e.matmul(
                                p2[th2][:], w2[:, _ts(db2)],
                                xnT[:, db2 * Tg + th2 * 512: db2 * Tg + th2 * 512 + 512],
                                start=(db2 == 0), stop=(db2 == NDB - 1)),
                                t_sem if last else None)
                            tv = tv2 if tv2 is not None else tv
                em.wait("scalar", t_sem, tv)
                if i + 1 < HALF:
                    cast_gu(fb + 1, *nxt)
                for th in range(2):
                    em.wait("scalar", v_sem, em.val(v_sem))
                    sv = em.op("scalar", lambda e, th2=th: e.activation(
                        out=silu_tmp[:], in_=ps_g[th2][:], func=AF.Silu), s_sem)
                    em.wait("vector", s_sem, sv)
                    vv = em.op("vector", lambda e, th2=th, i2=i: e.tensor_tensor(
                        out=h_sb[:, i2 * Tg + th2 * 512: i2 * Tg + th2 * 512 + 512],
                        in0=silu_tmp[:], in1=ps_u[th2][:], op=ALU.mult), v_sem)
                em.wait("tensor", v_sem, vv)

            for q in range(4):
                for r in range(4):
                    par = r % 2
                    dmb = q * 4 + r
                    em.wait("gpsimd", v_sem, cast_done["wd"][par])
                    lv = em.dma("gpsimd", wd_r[par][:],
                                dw[_ts(dmb), half * HC:(half + 1) * HC], ld_wd[par])
                    em.wait("vector", ld_wd[par], lv)
                    em.wait("vector", t_sem, tposed["wd"][par])
                    cast_done["wd"][par] = dequant(wd_b[par][:], wd_r[par][:],
                                                   ds_sb[:, dmb:dmb + 1], v_sem)
                    em.wait("tensor", v_sem, cast_done["wd"][par])
                    done = 0
                    dwT_r = dwT[:].rearrange("p (ibl c) -> p ibl c", c=512)
                    for batch, nb in ((0, 8), (1, 8), (0, 6)):
                        if done == 16:
                            em.wait("tensor", v_sem, em.val(v_sem))
                        tv = None
                        for j in range(nb):
                            ibl = done + j
                            tv = em.op("tensor", lambda e, b2=batch, j2=j, i2=ibl, p2=par: e.transpose(
                                ps_tr[b2][:, _ts(j2)], wd_b[p2][:, _ts(i2)], ident_bf[:]),
                                t_sem if j == nb - 1 else None)
                        em.wait("vector", t_sem, tv)
                        vv = em.op("vector", lambda e, b2=batch, nb2=nb, d0=done, r2=r: e.tensor_copy(
                            dwT_r[:, d0:d0 + nb2, r2 * 128:(r2 + 1) * 128],
                            ps_tr[b2][:, 0:nb2 * 128].rearrange("p (a c) -> p a c", c=128)),
                            v_sem)
                        done += nb
                    tposed["wd"][par] = em.val(t_sem)
                    em.wait("tensor", v_sem, vv)
                xsrc = x if half == 0 else xout
                for tt in range(NTT):
                    pp = tt % 2
                    if half + q + tt > 1:
                        em.wait("gpsimd", v_sem, em.val(v_sem))
                    if half == 1:
                        em.wait("gpsimd", st, em.val(st))
                    lv = em.dma("gpsimd", xsl[pp][:],
                                xsrc[_ts(tt), q * 512:(q + 1) * 512], ld_xs[pp])
                    if q + tt > 1 or half > 0:
                        em.wait("tensor", v_sem, em.val(v_sem))
                    tv = None
                    for ibl in range(HALF):
                        tv = em.op("tensor", lambda e, p2=pp, i2=ibl, t2=tt: e.matmul(
                            ps_d[p2][:], h_sb[:, i2 * Tg + t2 * 128: i2 * Tg + (t2 + 1) * 128],
                            dwT[:, i2 * 512:(i2 + 1) * 512],
                            start=(i2 == 0), stop=(i2 == HALF - 1)),
                            t_sem if ibl == HALF - 1 else None)
                    em.wait("vector", t_sem, tv)
                    em.wait("vector", ld_xs[pp], lv)
                    em.wait("vector", st, em.val(st))
                    vv = em.op("vector", lambda e, p2=pp: e.tensor_tensor(
                        out=xosl[p2][:], in0=xsl[p2][:], in1=ps_d[p2][:],
                        op=ALU.add), v_sem)
                    em.wait("gpsimd", v_sem, vv)
                    em.dma("gpsimd", xout[_ts(tt), q * 512:(q + 1) * 512],
                           xosl[pp][:], st)
            em.barrier({"vector": v_sem, "tensor": t_sem, "scalar": s_sem})
            em.wait("gpsimd", st, em.val(st))

        em.wait("gpsimd", st, em.val(st))

        blk = es.enter_context(nc.Block())
        em.replay(blk)
    return nc


def build_wprep(quant):
    nc = bass.Bass("TRN2", target_bir_lowering=False)
    wdt = I32 if quant == "int8" else F32
    gw = nc.dram_tensor("gw", [INTER, DM], wdt, kind="ExternalInput").ap()
    uw = nc.dram_tensor("uw", [INTER, DM], wdt, kind="ExternalInput").ap()
    dw = nc.dram_tensor("dw", [DM, INTER], wdt, kind="ExternalInput").ap()
    gs = nc.dram_tensor("gs_m", [128, NFB], F32, kind="ExternalInput").ap()
    us = nc.dram_tensor("us_m", [128, NFB], F32, kind="ExternalInput").ap()
    ds = nc.dram_tensor("ds_m", [128, NDB], F32, kind="ExternalInput").ap()
    ident = nc.dram_tensor("ident_bf", [128, 128], BF16, kind="ExternalInput").ap()
    gwt_s = nc.dram_tensor("gwt_s", [NFB, 128, DM], BF16, kind="ExternalOutput").ap()
    uwt_s = nc.dram_tensor("uwt_s", [NFB, 128, DM], BF16, kind="ExternalOutput").ap()
    dwt_s = nc.dram_tensor("dwt_s", [NFB, 128, DM], BF16, kind="ExternalOutput").ap()

    em = Emitter()
    es = ExitStack()
    with es:
        sb = lambda name, shape, dt: es.enter_context(nc.sbuf_tensor(name, shape, dt))
        ps = lambda name, shape, dt: es.enter_context(nc.psum_tensor(name, shape, dt))
        sem = lambda name: es.enter_context(nc.semaphore(name))
        stat = sb("stat", [128, 8], F32)
        ident_bf = sb("ident_sb", [128, 128], BF16)
        gs_sb = sb("gs_sb", [128, NFB], F32)
        us_sb = sb("us_sb", [128, NFB], F32)
        ds_sb = sb("ds_sb", [128, NDB], F32)
        w_r = [sb(f"w_r{i}", [128, INTER], BF16) for i in range(2)]
        w_b = [sb(f"w_b{i}", [128, INTER], BF16) for i in range(2)]
        stage = [sb(f"stage{i}", [128, INTER], BF16) for i in range(2)]
        ps_tr = [ps(f"ps_tr{i}", [128, 1024], BF16) for i in range(2)]
        ld_c = sem("ld_c")
        ld_w = [sem("ld_w0"), sem("ld_w1")]
        st = sem("st")
        t_sem = sem("t_sem")
        v_sem = sem("v_sem")
        s_sem = sem("s_sem")
        em.set_self_sems({"vector": v_sem, "scalar": s_sem})
        em.set_milestone_bufs(stat[:, 6:7], stat[:, 7:8])

        em.dma("sync", ident_bf[:], ident, ld_c)
        em.dma("sync", gs_sb[:], gs, ld_c)
        em.dma("sync", us_sb[:], us, ld_c)
        lv = em.dma("sync", ds_sb[:], ds, ld_c)
        em.wait("vector", ld_c, lv)
        em.op("vector", lambda e: e.tensor_scalar_mul(ds_sb[:], ds_sb[:], 0.5))
        em.milestone("vector", v_sem)
        em.wait("tensor", ld_c, 64)

        def dequant(out_bf, wr, scol, fsem=None):
            if quant == "int8":
                return em.op("vector", lambda e: e.tensor_scalar(
                    out=out_bf, in0=wr, scalar1=-128.0, scalar2=scol,
                    op0=ALU.add, op1=ALU.mult), fsem)
            return em.op("vector", lambda e: e.tensor_scalar(
                out=out_bf, in0=wr, scalar1=scol, scalar2=None, op0=ALU.mult), fsem)

        cast_done = [0, 0]
        store_done = [0, 0]
        jobs = []
        for fb in range(NFB):
            jobs.append((gw[_ts(fb), :], gs_sb[:, fb:fb + 1], gwt_s[fb], DM))
        for fb in range(NFB):
            jobs.append((uw[_ts(fb), :], us_sb[:, fb:fb + 1], uwt_s[fb], DM))
        for dmb in range(NDB):
            jobs.append((dw[_ts(dmb), :], ds_sb[:, dmb:dmb + 1], None, INTER, dmb))

        for ji, job in enumerate(jobs):
            par = ji % 2
            src_ap, scol, dst, width = job[0], job[1], job[2], job[3]
            nt = width // 128
            em.wait("gpsimd", v_sem, cast_done[par])
            lv = em.dma("gpsimd", w_r[par][:, 0:width], src_ap, ld_w[par])
            em.wait("vector", ld_w[par], lv)
            em.wait("vector", t_sem, em.val(t_sem))
            cast_done[par] = dequant(w_b[par][:, 0:width], w_r[par][:, 0:width], scol, v_sem)
            em.wait("tensor", v_sem, cast_done[par])
            em.wait("vector", st, store_done[par])
            done = 0
            while done < nt:
                nb = min(8, nt - done)
                bank = (done // 8) % 2
                if done >= 16:
                    em.wait("tensor", v_sem, em.val(v_sem))
                tv = None
                for j in range(nb):
                    tv = em.op("tensor", lambda e, b2=bank, j2=j, i2=done + j, p2=par: e.transpose(
                        ps_tr[b2][:, _ts(j2)], w_b[p2][:, _ts(i2)], ident_bf[:]),
                        t_sem if j == nb - 1 else None)
                em.wait("vector", t_sem, tv)
                em.op("vector", lambda e, b2=bank, nb2=nb, d0=done, p2=par: e.tensor_copy(
                    stage[p2][:, d0 * 128:(d0 + nb2) * 128], ps_tr[b2][:, 0:nb2 * 128]),
                    v_sem)
                done += nb
            vv = em.milestone("vector", v_sem)
            em.wait("sync", v_sem, vv)
            em.wait("sync", st, em.val(st))
            if dst is not None:
                em.dma("sync", dst, stage[par][:, 0:width], st)
            else:
                dmb = job[4]
                em.dma("sync",
                       dwt_s[:, :, dmb * 128:(dmb + 1) * 128].rearrange("i p c -> p i c"),
                       stage[par][:, 0:width].rearrange("p (i c) -> p i c", c=128), st)
            store_done[par] = em.val(st)
        em.wait("sync", st, em.val(st))
        blk = es.enter_context(nc.Block())
        em.replay(blk)
    return nc


def build_ffn_pt():
    nc = bass.Bass("TRN2", target_bir_lowering=False)
    x = nc.dram_tensor("x", [1024, DM], F32, kind="ExternalInput").ap()
    gwt_s = nc.dram_tensor("gwt_s", [NFB, 128, DM], BF16, kind="ExternalInput").ap()
    uwt_s = nc.dram_tensor("uwt_s", [NFB, 128, DM], BF16, kind="ExternalInput").ap()
    dwt_s = nc.dram_tensor("dwt_s", [NFB, 128, DM], BF16, kind="ExternalInput").ap()
    ident = nc.dram_tensor("ident_bf", [128, 128], BF16, kind="ExternalInput").ap()
    xout = nc.dram_tensor("xout", [1024, DM], F32, kind="ExternalOutput").ap()

    NTT = 8
    Tg = NTT * 128
    HALF = 22

    em = Emitter()
    es = ExitStack()
    with es:
        sb = lambda name, shape, dt: es.enter_context(nc.sbuf_tensor(name, shape, dt))
        ps = lambda name, shape, dt: es.enter_context(nc.psum_tensor(name, shape, dt))
        sem = lambda name: es.enter_context(nc.semaphore(name))
        xbuf = [sb(f"xbuf{i}", [128, DM], F32) for i in range(2)]
        stat = sb("stat", [128, 8], F32)
        xn = sb("xn", [128, DM], BF16)
        xnT = sb("xnT", [128, NDB * Tg], BF16)
        ident_bf = sb("ident_sb", [128, 128], BF16)
        wgT = [sb(f"wgT{i}", [128, DM], BF16) for i in range(2)]
        wuT = [sb(f"wuT{i}", [128, DM], BF16) for i in range(2)]
        silu_tmp = sb("silu_tmp", [128, 512], F32)
        h_sb = sb("h_sb", [128, HALF * Tg], BF16)
        dwT = [sb(f"dwT{i}", [128, HALF * 512], BF16) for i in range(2)]
        xsl = [sb(f"xsl{i}", [128, 512], F32) for i in range(2)]
        xosl = [sb(f"xosl{i}", [128, 512], F32) for i in range(2)]
        ps_tr = [ps(f"ps_tr{i}", [128, 1024], BF16) for i in range(2)]
        ps_g = [ps(f"ps_g{i}", [128, 512], F32) for i in range(2)]
        ps_u = [ps(f"ps_u{i}", [128, 512], F32) for i in range(2)]
        ps_d = [ps(f"ps_d{i}", [128, 512], F32) for i in range(2)]
        ld_c = sem("ld_c")
        ld_x = [sem("ld_x0"), sem("ld_x1")]
        ld_wg = [sem("ld_wg0"), sem("ld_wg1")]
        ld_wu = [sem("ld_wu0"), sem("ld_wu1")]
        ld_wd = [sem("ld_wd0"), sem("ld_wd1")]
        ld_xs = [sem("ld_xs0"), sem("ld_xs1")]
        st = sem("st")
        t_sem = sem("t_sem")
        v_sem = sem("v_sem")
        s_sem = sem("s_sem")
        em.set_self_sems({"vector": v_sem, "scalar": s_sem})
        em.set_milestone_bufs(stat[:, 6:7], stat[:, 7:8])
        sems = dict(ld_x=ld_x, s=s_sem, v=v_sem, t=t_sem)
        bufs = dict(xbuf=xbuf, stat=stat, ps_tr=ps_tr, xn=xn, ident_bf=ident_bf)

        em.dma("sync", ident_bf[:], ident, ld_c)
        em.op("vector", lambda e: e.memset(stat[:, 5:6], LN_EPS))
        em.milestone("vector", v_sem)
        em.wait("tensor", ld_c, 16)

        emit_ln_to_xnT(em, x, xnT, NTT, Tg, sems, bufs)
        em.barrier({"vector": v_sem, "tensor": t_sem, "scalar": s_sem})

        mm_done = {"g": [0, 0], "u": [0, 0], "d": [0, 0]}

        def load_gu(fb):
            par = fb % 2
            em.wait("sync", t_sem, max(mm_done["g"][par], mm_done["u"][par]))
            l1 = em.dma("sync", wgT[par][:], gwt_s[fb], ld_wg[par])
            l2 = em.dma("sync", wuT[par][:], uwt_s[fb], ld_wu[par])
            return l1, l2

        for half in range(2):
            pend = load_gu(half * HALF)
            for i in range(HALF):
                fb = half * HALF + i
                par = fb % 2
                l1, l2 = pend
                if i + 1 < HALF:
                    pend = load_gu(fb + 1)
                em.wait("tensor", ld_wg[par], l1)
                em.wait("tensor", ld_wu[par], l2)
                tv = None
                for key, wT, psb in (("g", wgT[par], ps_g), ("u", wuT[par], ps_u)):
                    for th in range(2):
                        for db in range(NDB):
                            last = th == 1 and db == NDB - 1
                            tv2 = em.op("tensor", lambda e, w2=wT, p2=psb, db2=db, th2=th: e.matmul(
                                p2[th2][:], w2[:, _ts(db2)],
                                xnT[:, db2 * Tg + th2 * 512: db2 * Tg + th2 * 512 + 512],
                                start=(db2 == 0), stop=(db2 == NDB - 1)),
                                t_sem if last else None)
                            tv = tv2 if tv2 is not None else tv
                    mm_done[key][par] = em.val(t_sem) if tv is None else tv
                em.wait("scalar", t_sem, tv)
                for th in range(2):
                    em.wait("scalar", v_sem, em.val(v_sem))
                    sv = em.op("scalar", lambda e, th2=th: e.activation(
                        out=silu_tmp[:], in_=ps_g[th2][:], func=AF.Silu), s_sem)
                    em.wait("vector", s_sem, sv)
                    vv = em.op("vector", lambda e, th2=th, i2=i: e.tensor_tensor(
                        out=h_sb[:, i2 * Tg + th2 * 512: i2 * Tg + th2 * 512 + 512],
                        in0=silu_tmp[:], in1=ps_u[th2][:], op=ALU.mult), v_sem)
                em.wait("tensor", v_sem, vv)

            xsrc = x if half == 0 else xout
            for q in range(4):
                par = q % 2
                em.wait("sync", t_sem, mm_done["d"][par])
                lv = em.dma("sync",
                            dwT[par][:].rearrange("p (i c) -> p i c", c=512),
                            dwt_s[half * HALF:(half + 1) * HALF, :, q * 512:(q + 1) * 512]
                            .rearrange("i p c -> p i c"), ld_wd[par])
                em.wait("tensor", ld_wd[par], lv)
                for tt in range(NTT):
                    pp = tt % 2
                    if half + q + tt > 1:
                        em.wait("gpsimd", v_sem, em.val(v_sem))
                    if half == 1:
                        em.wait("gpsimd", st, em.val(st))
                    lx = em.dma("gpsimd", xsl[pp][:],
                                xsrc[_ts(tt), q * 512:(q + 1) * 512], ld_xs[pp])
                    if q + tt > 1 or half > 0:
                        em.wait("tensor", v_sem, em.val(v_sem))
                    tv = None
                    for ibl in range(HALF):
                        tv = em.op("tensor", lambda e, p2=pp, i2=ibl, t2=tt, p4=par: e.matmul(
                            ps_d[p2][:], h_sb[:, i2 * Tg + t2 * 128: i2 * Tg + (t2 + 1) * 128],
                            dwT[p4][:, i2 * 512:(i2 + 1) * 512],
                            start=(i2 == 0), stop=(i2 == HALF - 1)),
                            t_sem if ibl == HALF - 1 else None)
                    if tt == NTT - 1:
                        mm_done["d"][par] = tv
                    em.wait("vector", t_sem, tv)
                    em.wait("vector", ld_xs[pp], lx)
                    em.wait("vector", st, em.val(st))
                    vv = em.op("vector", lambda e, p2=pp: e.tensor_tensor(
                        out=xosl[p2][:], in0=xsl[p2][:], in1=ps_d[p2][:],
                        op=ALU.add), v_sem)
                    em.wait("gpsimd", v_sem, vv)
                    em.dma("gpsimd", xout[_ts(tt), q * 512:(q + 1) * 512],
                           xosl[pp][:], st)
            em.barrier({"vector": v_sem, "tensor": t_sem, "scalar": s_sem})
            em.wait("gpsimd", st, em.val(st))
        em.wait("gpsimd", st, em.val(st))
        blk = es.enter_context(nc.Block())
        em.replay(blk)
    return nc



_PROGS = {}


def _get_prog(key):
    if key not in _PROGS:
        builders = {
            "ffn_ternary": lambda: build_ffn("ternary"),
            "ffn_int8": lambda: build_ffn("int8"),
            "wprep_ternary": lambda: build_wprep("ternary"),
            "wprep_int8": lambda: build_wprep("int8"),
            "ffn_pt": build_ffn_pt,
            "m1": build_m1, "m2a": build_m2a,
            "m3a0": lambda: build_m3a(0), "m3a1": lambda: build_m3a(1),
            "m3b": build_m3b, "r1": build_r1, "r2": build_r2,
        }
        _PROGS[key] = builders[key]()
    return _PROGS[key]


_IDENT_BF = np.eye(128, dtype=ml_dtypes.bfloat16)
_IDENT_F = np.eye(128, dtype=np.float32)
_ONES_F = np.ones((1, 128), np.float32)
_TRIL = np.tril(np.ones((128, 128), np.float32))
_TRIU = np.ascontiguousarray(np.triu(np.ones((128, 128), np.float32)))


def _scales_mat(s, nb):
    return np.ascontiguousarray(np.asarray(s, np.float32).reshape(nb, 128).T)


def _perm_wparam(wp):
    idx = []
    for h in range(64):
        idx.extend(range(h * 129 + 1, h * 129 + 65))
    for h in range(64):
        idx.extend(range(h * 129 + 65, h * 129 + 129))
    idx.extend(h * 129 for h in range(64))
    return np.ascontiguousarray(wp[np.asarray(idx)])


def _launch(key, in_maps):
    prog = _get_prog(key)
    res = run_bass_kernel_spmd(prog, in_maps, core_ids=list(range(NCORES)))
    return res.results


def run_wprep(gw, uw, dw, gs, us, ds, quant):
    key = "wprep_int8" if quant == "int8" else "wprep_ternary"
    gs_m, us_m, ds_m = _scales_mat(gs, NFB), _scales_mat(us, NFB), _scales_mat(ds, NDB)
    res = _launch(key, [{"gw": gw, "uw": uw, "dw": dw, "gs_m": gs_m,
                         "us_m": us_m, "ds_m": ds_m, "ident_bf": _IDENT_BF}
                        for _ in range(NCORES)])
    return res


def run_ffn_pt(x_full, prep):
    out = np.empty_like(x_full)
    for g in range(2):
        in_maps = []
        for c in range(NCORES):
            base = c * TPC + g * 1024
            in_maps.append({
                "x": x_full[base:base + 1024],
                "gwt_s": prep[c]["gwt_s"], "uwt_s": prep[c]["uwt_s"],
                "dwt_s": prep[c]["dwt_s"], "ident_bf": _IDENT_BF,
            })
        res = _launch("ffn_pt", in_maps)
        for c in range(NCORES):
            base = c * TPC + g * 1024
            out[base:base + 1024] = np.asarray(res[c]["xout"])
    return out


def _xpos_tables(core):
    import math as _m
    tpc = TPC
    d = RD
    half = d // 2
    pos = np.arange(core * tpc, (core + 1) * tpc, dtype=np.float64)
    xscale = (np.arange(0, d, 2, dtype=np.float64) + 0.4 * d) / (1.4 * d)
    scale = xscale[None, :] ** (pos[:, None] / 512.0)
    inv_freq = 1.0 / 10000.0 ** (np.arange(half, dtype=np.float64) / half)
    sinu = pos[:, None] * inv_freq[None, :]
    sin, cos = np.sin(sinu), np.cos(sinu)
    dup = lambda t: np.repeat(t, 2, axis=-1)
    f32 = lambda a: np.ascontiguousarray(a.T, np.float32)
    return (f32(dup(cos * scale)), f32(dup(sin * scale)),
            f32(dup(cos / scale)), f32(dup(sin / scale)))


def _ret_tables(core):
    g = _gammas()
    j = np.arange(128)
    deccol = np.stack([gh ** (128 - j) for gh in g], axis=1)
    gcol = np.stack([gh ** j for gh in g], axis=1)
    dt_t = np.zeros((RH, 128, 128))
    for h in range(RH):
        n = np.arange(128)
        D = np.where(n[:, None] >= n[None, :], g[h] ** (n[:, None] - n[None, :]), 0.0)
        dt_t[h] = D.T
    coef = np.zeros((NCORES, RH))
    for jj in range(NCORES):
        if jj < core:
            coef[jj] = g ** (TPC * (core - 1 - jj))
    coefb = np.broadcast_to(coef.reshape(1, NCORES * RH), (128, NCORES * RH))
    rt = np.zeros((128, 128))
    for k in range(64):
        rt[2 * k + 1, 2 * k] = -1.0
        rt[2 * k, 2 * k + 1] = 1.0
    c32 = lambda a: np.ascontiguousarray(a, np.float32)
    return (c32(deccol), c32(gcol), c32(dt_t), c32(coefb),
            rt.astype(ml_dtypes.bfloat16))


def run_mamba(x_full, w_in, wp_orig, w_out):
    wpp = _perm_wparam(np.asarray(wp_orig, np.float32))
    w_in = np.asarray(w_in, np.float32)
    w_out = np.asarray(w_out, np.float32)
    xs = [x_full[c * TPC:(c + 1) * TPC] for c in range(NCORES)]

    m1 = _launch("m1", [{"x": xs[c], "wp": wpp, "ident_bf": _IDENT_BF,
                         "ident_f": _IDENT_F} for c in range(NCORES)])
    m2a = _launch("m2a", [{"x": xs[c], "w_in": w_in, "ident_bf": _IDENT_BF,
                           "ident_f": _IDENT_F} for c in range(NCORES)])
    ys = []
    for part, key in ((0, "m3a0"), (1, "m3a1")):
        maps = [{"bt_s": m1[c]["bt_s"], "ct_s": m1[c]["ct_s"], "la_s": m1[c]["la_s"],
                 "xtok_s": m2a[c]["xtok_s"], "ident_bf": _IDENT_BF, "ident_f": _IDENT_F,
                 "ones_f": _ONES_F, "triu_f": _TRIU, "tril": _TRIL}
                for c in range(NCORES)]
        ys.append(_launch(key, maps))
    m3b = _launch("m3b", [{
        "x": xs[c],
        "y_s": np.ascontiguousarray(np.concatenate(
            [np.asarray(ys[0][c]["y_s"]), np.asarray(ys[1][c]["y_s"])])),
        "res_s": m2a[c]["res_s"], "w_out": w_out,
        "ident_bf": _IDENT_BF, "ident_f": _IDENT_F} for c in range(NCORES)])
    out = np.empty_like(x_full)
    for c in range(NCORES):
        out[c * TPC:(c + 1) * TPC] = np.asarray(m3b[c]["xout"])
    return out


def run_retention(x_full, wq, wk, wv, wg, wo, gnw, gnb):
    wq = np.asarray(wq, np.float32)
    wk = np.asarray(wk, np.float32)
    wv = np.asarray(wv, np.float32)
    wg = np.asarray(wg, np.float32)
    wo = np.asarray(wo, np.float32)
    gnw_b = np.ascontiguousarray(np.broadcast_to(np.asarray(gnw, np.float32), (128, DM)))
    gnb_b = np.ascontiguousarray(np.broadcast_to(np.asarray(gnb, np.float32), (128, DM)))
    xs = [x_full[c * TPC:(c + 1) * TPC] for c in range(NCORES)]

    maps = []
    for c in range(NCORES):
        cq, sq, ck, sk = _xpos_tables(c)
        deccol, gcol, dt_t, coefb, rt = _ret_tables(c)
        maps.append({"x": xs[c], "wq": wq, "wk": wk, "wv": wv, "wg": wg,
                     "cq": cq, "sq": sq, "ck": ck, "sk": sk, "rt_bf": rt,
                     "deccol": deccol, "ident_bf": _IDENT_BF, "ident_f": _IDENT_F})
    r1 = _launch("r1", maps)
    s_all = np.ascontiguousarray(np.stack([np.asarray(r1[c]["slocr_s"]) for c in range(NCORES)]))
    maps = []
    for c in range(NCORES):
        deccol, gcol, dt_t, coefb, rt = _ret_tables(c)
        maps.append({"x": xs[c], "qx_s": r1[c]["qx_s"], "kx_s": r1[c]["kx_s"],
                     "v_s": r1[c]["v_s"], "gate_s": r1[c]["gate_s"],
                     "stret_s": r1[c]["stret_s"], "s_all": s_all, "coefb": coefb,
                     "wo": wo, "gnw_b": gnw_b, "gnb_b": gnb_b, "dt_t": dt_t,
                     "gcol": gcol, "ident_bf": _IDENT_BF, "ident_f": _IDENT_F})
    r2 = _launch("r2", maps)
    out = np.empty_like(x_full)
    for c in range(NCORES):
        out[c * TPC:(c + 1) * TPC] = np.asarray(r2[c]["xout"])
    return out


def kernel(L0_attn_x, L0_attn_w_in, L0_attn_w_param, L0_attn_w_out,
           L0_ffn_gate_w, L0_ffn_up_w, L0_ffn_down_w,
           L0_ffn_gate_s, L0_ffn_up_s, L0_ffn_down_s,
           L1_attn_W_Q, L1_attn_W_K, L1_attn_W_V, L1_attn_W_G, L1_attn_W_O,
           L1_attn_gn_weight, L1_attn_gn_bias,
           L1_ffn_wq_gate, L1_ffn_s_gate, L1_ffn_wq_up, L1_ffn_s_up,
           L1_ffn_wq_down, L1_ffn_s_down):
    x = np.ascontiguousarray(np.asarray(L0_attn_x, np.float32).reshape(SEQ, DM))
    prep0 = run_wprep(np.asarray(L0_ffn_gate_w, np.float32),
                      np.asarray(L0_ffn_up_w, np.float32),
                      np.asarray(L0_ffn_down_w, np.float32),
                      L0_ffn_gate_s, L0_ffn_up_s, L0_ffn_down_s, "ternary")
    x = run_ffn_pt(x, prep0)
    x = run_mamba(x, L0_attn_w_in, L0_attn_w_param, L0_attn_w_out)
    x = run_ffn_pt(x, prep0)
    prep1 = run_wprep(np.asarray(L1_ffn_wq_gate, np.int32),
                      np.asarray(L1_ffn_wq_up, np.int32),
                      np.asarray(L1_ffn_wq_down, np.int32),
                      L1_ffn_s_gate, L1_ffn_s_up, L1_ffn_s_down, "int8")
    x = run_ffn_pt(x, prep1)
    x = run_retention(x, L1_attn_W_Q, L1_attn_W_K, L1_attn_W_V, L1_attn_W_G,
                      L1_attn_W_O, L1_attn_gn_weight, L1_attn_gn_bias)
    x = run_ffn_pt(x, prep1)
    return x.reshape(1, SEQ, DM)



MHEADS = 64
MDS = 64
MDH = 64
NCH = 16


def _mk_common(nc, em, es, with_xnT=True):
    sb = lambda name, shape, dt: es.enter_context(nc.sbuf_tensor(name, shape, dt))
    ps = lambda name, shape, dt: es.enter_context(nc.psum_tensor(name, shape, dt))
    sem = lambda name: es.enter_context(nc.semaphore(name))
    d = {}
    d["stat"] = sb("stat", [128, 8], F32)
    d["ident_bf"] = sb("ident_sb", [128, 128], BF16)
    d["ident_f"] = sb("identf_sb", [128, 128], F32)
    for s in ("ld_c", "st", "t_sem", "v_sem", "s_sem"):
        d[s] = sem(s)
    em.set_self_sems({"vector": d["v_sem"], "scalar": d["s_sem"]})
    em.set_milestone_bufs(d["stat"][:, 6:7], d["stat"][:, 7:8])
    if with_xnT:
        d["xbuf"] = [sb(f"xbuf{i}", [128, DM], F32) for i in range(2)]
        d["xn"] = sb("xn", [128, DM], BF16)
        d["xnT"] = sb("xnT", [128, NDB * TPC], BF16)
        d["ld_x"] = [sem("ld_x0"), sem("ld_x1")]
        d["ps_tr"] = [ps(f"ps_tr{i}", [128, 1024], BF16) for i in range(2)]
    return d


def emit_const_prologue(em, d, ident_ap, identf_ap, extra_consts=()):
    em.dma("sync", d["ident_bf"][:], ident_ap, d["ld_c"])
    em.dma("sync", d["ident_f"][:], identf_ap, d["ld_c"])
    for t_sb, t_ap in extra_consts:
        em.dma("sync", t_sb, t_ap, d["ld_c"])
    lv = em.val(d["ld_c"])
    em.op("vector", lambda e: e.memset(d["stat"][:, 5:6], LN_EPS))
    em.wait("vector", d["ld_c"], lv)
    em.wait("scalar", d["ld_c"], lv)
    em.wait("tensor", d["ld_c"], lv)
    em.milestone("vector", d["v_sem"])


def build_m1():
    nc = bass.Bass("TRN2", target_bir_lowering=False)
    x = nc.dram_tensor("x", [TPC, DM], F32, kind="ExternalInput").ap()
    wp = nc.dram_tensor("wp", [8256, DM], F32, kind="ExternalInput").ap()
    ident = nc.dram_tensor("ident_bf", [128, 128], BF16, kind="ExternalInput").ap()
    identf = nc.dram_tensor("ident_f", [128, 128], F32, kind="ExternalInput").ap()
    bt_s = nc.dram_tensor("bt_s", [TPC, 4096], BF16, kind="ExternalOutput").ap()
    ct_s = nc.dram_tensor("ct_s", [TPC, 4096], BF16, kind="ExternalOutput").ap()
    la_s = nc.dram_tensor("la_s", [TPC, MHEADS], F32, kind="ExternalOutput").ap()

    em = Emitter()
    es = ExitStack()
    with es:
        sb = lambda name, shape, dt: es.enter_context(nc.sbuf_tensor(name, shape, dt))
        ps = lambda name, shape, dt: es.enter_context(nc.psum_tensor(name, shape, dt))
        d = _mk_common(nc, em, es)
        xnT, ps_tr, ident_bf = d["xnT"], d["ps_tr"], d["ident_bf"]
        ld_c, st = d["ld_c"], d["st"]
        t_sem, v_sem, s_sem = d["t_sem"], d["v_sem"], d["s_sem"]

        ld_w = [es.enter_context(nc.semaphore(f"ld_w{i}")) for i in range(2)]
        w_b = [sb(f"w_b{i}", [128, DM], BF16) for i in range(2)]
        wT = sb("wT", [128, DM], BF16)
        stage = sb("stage", [128, TPC], BF16)
        tokstage = sb("tokstage", [128, TPC], BF16)
        dt_fm = sb("dt_fm", [64, TPC], F32)
        tmp_fm = sb("tmp_fm", [64, TPC], F32)
        la_fm = sb("la_fm", [64, TPC], F32)
        la_tk = sb("la_tk", [128, NCH * 64], F32)
        ps_p = [ps(f"ps_p{i}", [128, 512], F32) for i in range(2)]
        ps_c = ps("ps_c", [128, 512], F32)

        sems = dict(ld_x=d["ld_x"], s=s_sem, v=v_sem, t=t_sem)
        bufs = dict(xbuf=d["xbuf"], stat=d["stat"], ps_tr=ps_tr, xn=d["xn"],
                    ident_bf=ident_bf)

        emit_const_prologue(em, d, ident, identf)
        emit_ln_to_xnT(em, x, xnT, 16, TPC, sems, bufs)
        em.barrier({"vector": v_sem, "tensor": t_sem, "scalar": s_sem})

        state = {"i": 0}

        tposed = [0, 0]

        def proj_block(rows_ap, nrows, out_cb):
            i = state["i"]
            par = i % 2
            if i >= 2:
                em.wait("gpsimd", t_sem, tposed[par])
            lv = em.dma("gpsimd", w_b[par][0:nrows, :], rows_ap, ld_w[par])
            em.wait("tensor", ld_w[par], lv)
            tv = None
            for j in range(16):
                tv = em.op("tensor", lambda e, j2=j, nr=nrows, p2=par: e.transpose(
                    ps_tr[j2 // 8][:, (j2 % 8) * 128:(j2 % 8) * 128 + nr],
                    w_b[p2][0:nr, _ts(j2)], ident_bf[0:nr, 0:nr]),
                    t_sem if j == 15 else None)
            tposed[par] = tv
            em.wait("vector", t_sem, tv)
            vv = None
            for b2 in range(2):
                vv = em.op("vector", lambda e, b3=b2, nr=nrows: e.tensor_copy(
                    wT[:].rearrange("p (a c) -> p a c", c=128)[:, b3 * 8:(b3 + 1) * 8, 0:nr],
                    ps_tr[b3][:].rearrange("p (a c) -> p a c", c=128)[:, :, 0:nr]))
            em.wait("tensor", v_sem, vv)
            for tq in range(4):
                pb = tq % 2
                if i + tq > 0:
                    em.wait("tensor", v_sem, em.val(v_sem))
                tv = None
                for db in range(NDB):
                    tv = em.op("tensor", lambda e, db2=db, tq2=tq, pb2=pb, nr=nrows: e.matmul(
                        ps_p[pb2][0:nr, :], wT[:, db2 * 128:db2 * 128 + nr],
                        xnT[:, db2 * TPC + tq2 * 512: db2 * TPC + tq2 * 512 + 512],
                        start=(db2 == 0), stop=(db2 == NDB - 1)),
                        t_sem if db == NDB - 1 else None)
                em.wait("vector", t_sem, tv)
                out_cb(tq, ps_p[pb][0:nrows, :])
            state["i"] += 1

        def tok_spill(dst, fb):
            vv = em.val(v_sem)
            em.wait("tensor", v_sem, vv)
            tv = None
            for j in range(16):
                tv = em.op("tensor", lambda e, j2=j: e.transpose(
                    ps_tr[j2 // 8][:, _ts(j2 % 8)], stage[:, _ts(j2)], ident_bf[:]),
                    t_sem if j == 15 else None)
            em.wait("vector", t_sem, tv)
            em.wait("vector", st, em.val(st))
            vv = None
            for b2 in range(2):
                vv = em.op("vector", lambda e, b3=b2: e.tensor_copy(
                    tokstage[:, b3 * 1024:(b3 + 1) * 1024], ps_tr[b3][:]))
            em.wait("tensor", v_sem, vv)
            em.wait("sync", v_sem, vv)
            em.dma("sync",
                   dst[:, fb * 128:(fb + 1) * 128].rearrange("(c p) f -> p c f", p=128),
                   tokstage[:].rearrange("p (c f) -> p c f", f=128), st)

        for mat, dst in ((0, bt_s), (1, ct_s)):
            for fb in range(32):
                def cb(tq, psum):
                    em.op("vector", lambda e, tq2=tq, p2=psum: e.tensor_copy(
                        stage[:, tq2 * 512:(tq2 + 1) * 512], p2), v_sem)
                proj_block(wp[mat * 4096 + fb * 128: mat * 4096 + (fb + 1) * 128, :], 128, cb)
                tok_spill(dst, fb)

        def cb_dt(tq, psum):
            em.op("vector", lambda e, tq2=tq, p2=psum: e.tensor_copy(
                dt_fm[:, tq2 * 512:(tq2 + 1) * 512], p2), v_sem)
        proj_block(wp[8192:8256, :], 64, cb_dt)

        em.wait("scalar", v_sem, em.val(v_sem))
        em.op("scalar", lambda e: e.activation(out=tmp_fm[:], in_=dt_fm[:], func=AF.Abs))
        em.op("scalar", lambda e: e.activation(out=tmp_fm[:], in_=tmp_fm[:], func=AF.Exp, scale=-1.0))
        em.op("scalar", lambda e: e.activation(out=tmp_fm[:], in_=tmp_fm[:], func=AF.Ln, bias=1.0))
        sv = em.op("scalar", lambda e: e.activation(out=dt_fm[:], in_=dt_fm[:], func=AF.Relu), s_sem)
        em.wait("vector", s_sem, sv)
        em.op("vector", lambda e: e.tensor_add(la_fm[:], dt_fm[:], tmp_fm[:]))
        vv = em.op("vector", lambda e: e.tensor_scalar_mul(la_fm[:], la_fm[:], -1.0), v_sem)
        em.wait("tensor", v_sem, vv)
        for c in range(NCH):
            tv = em.op("tensor", lambda e, c2=c: e.transpose(
                ps_c[0:128, 0:64], la_fm[:, c2 * 128:(c2 + 1) * 128],
                d["ident_f"][0:64, 0:64]), t_sem)
            em.wait("vector", t_sem, tv)
            vv = em.op("vector", lambda e, c2=c: e.tensor_copy(
                la_tk[:, c2 * 64:(c2 + 1) * 64], ps_c[0:128, 0:64]), v_sem)
            em.wait("tensor", v_sem, vv)
        em.wait("sync", v_sem, em.val(v_sem))
        em.wait("sync", st, em.val(st))
        em.dma("sync", la_s.rearrange("(c p) h -> p c h", p=128),
               la_tk[:].rearrange("p (c h) -> p c h", h=64), st)
        em.wait("sync", st, em.val(st))

        blk = es.enter_context(nc.Block())
        em.replay(blk)
    return nc


def build_m2a():
    nc = bass.Bass("TRN2", target_bir_lowering=False)
    x = nc.dram_tensor("x", [TPC, DM], F32, kind="ExternalInput").ap()
    w_in = nc.dram_tensor("w_in", [8192, DM], F32, kind="ExternalInput").ap()
    ident = nc.dram_tensor("ident_bf", [128, 128], BF16, kind="ExternalInput").ap()
    identf = nc.dram_tensor("ident_f", [128, 128], F32, kind="ExternalInput").ap()
    xtok_s = nc.dram_tensor("xtok_s", [NCH, 128, 4096], BF16, kind="ExternalOutput").ap()
    res_s = nc.dram_tensor("res_s", [NCH, 128, 4096], BF16, kind="ExternalOutput").ap()

    em = Emitter()
    es = ExitStack()
    with es:
        sb = lambda name, shape, dt: es.enter_context(nc.sbuf_tensor(name, shape, dt))
        ps = lambda name, shape, dt: es.enter_context(nc.psum_tensor(name, shape, dt))
        d = _mk_common(nc, em, es)
        xnT, ps_tr, ident_bf = d["xnT"], d["ps_tr"], d["ident_bf"]
        st = d["st"]
        t_sem, v_sem, s_sem = d["t_sem"], d["v_sem"], d["s_sem"]

        ld_w = [es.enter_context(nc.semaphore(f"ld_w{i}")) for i in range(2)]
        w_b = [sb(f"w_b{i}", [128, DM], BF16) for i in range(2)]
        wT = sb("wT", [128, DM], BF16)
        stage = sb("stage", [128, TPC], BF16)
        xstage = sb("xstage", [128, TPC], BF16)
        ps_p = [ps(f"ps_p{i}", [128, 512], F32) for i in range(2)]

        sems = dict(ld_x=d["ld_x"], s=s_sem, v=v_sem, t=t_sem)
        bufs = dict(xbuf=d["xbuf"], stat=d["stat"], ps_tr=ps_tr, xn=d["xn"],
                    ident_bf=ident_bf)

        emit_const_prologue(em, d, ident, identf)
        emit_ln_to_xnT(em, x, xnT, 16, TPC, sems, bufs)
        em.barrier({"vector": v_sem, "tensor": t_sem, "scalar": s_sem})

        tposed = [0, 0]
        for fb in range(64):
            is_res = fb >= 32
            par = fb % 2
            if fb >= 2:
                em.wait("gpsimd", t_sem, tposed[par])
            lv = em.dma("gpsimd", w_b[par][:], w_in[_ts(fb), :], ld_w[par])
            em.wait("tensor", ld_w[par], lv)
            tv = None
            for j in range(16):
                tv = em.op("tensor", lambda e, j2=j, p2=par: e.transpose(
                    ps_tr[j2 // 8][:, _ts(j2 % 8)], w_b[p2][:, _ts(j2)], ident_bf[:]),
                    t_sem if j == 15 else None)
            tposed[par] = tv
            em.wait("vector", t_sem, tv)
            vv = None
            for b2 in range(2):
                vv = em.op("vector", lambda e, b3=b2: e.tensor_copy(
                    wT[:, b3 * 1024:(b3 + 1) * 1024], ps_tr[b3][:]))
            em.wait("tensor", v_sem, vv)
            em.wait("vector", st, em.val(st))
            em.wait("scalar", st, em.val(st))
            for tq in range(4):
                pb = tq % 2
                if fb + tq > 0:
                    em.wait("tensor", v_sem, em.val(v_sem))
                    em.wait("tensor", s_sem, em.val(s_sem))
                tv = None
                for db in range(NDB):
                    tv = em.op("tensor", lambda e, db2=db, tq2=tq, pb2=pb: e.matmul(
                        ps_p[pb2][:], wT[:, _ts(db2)],
                        xnT[:, db2 * TPC + tq2 * 512: db2 * TPC + tq2 * 512 + 512],
                        start=(db2 == 0), stop=(db2 == NDB - 1)),
                        t_sem if db == NDB - 1 else None)
                em.wait("scalar", t_sem, tv)
                if is_res:
                    em.op("scalar", lambda e, tq2=tq, pb2=pb: e.activation(
                        out=stage[:, tq2 * 512:(tq2 + 1) * 512], in_=ps_p[pb2][:],
                        func=AF.Silu), s_sem)
                else:
                    em.wait("vector", t_sem, tv)
                    em.op("vector", lambda e, tq2=tq, pb2=pb: e.tensor_copy(
                        stage[:, tq2 * 512:(tq2 + 1) * 512], ps_p[pb2][:]), v_sem)
            if is_res:
                sv = em.val(s_sem)
                em.wait("sync", s_sem, sv)
                em.dma("sync",
                       res_s[:, :, (fb - 32) * 128:(fb - 31) * 128].rearrange("c p t -> p c t"),
                       stage[:].rearrange("p (c t) -> p c t", t=128), st)
            else:
                vv = em.val(v_sem)
                em.wait("tensor", v_sem, vv)
                tv = None
                for j in range(16):
                    tv = em.op("tensor", lambda e, j2=j: e.transpose(
                        ps_tr[j2 // 8][:, _ts(j2 % 8)], stage[:, _ts(j2)], ident_bf[:]),
                        t_sem if j == 15 else None)
                em.wait("vector", t_sem, tv)
                vv = None
                for b2 in range(2):
                    vv = em.op("vector", lambda e, b3=b2: e.tensor_copy(
                        xstage[:, b3 * 1024:(b3 + 1) * 1024], ps_tr[b3][:]))
                em.wait("tensor", v_sem, vv)
                em.wait("sync", v_sem, vv)
                em.dma("sync",
                       xtok_s[:, :, fb * 128:(fb + 1) * 128].rearrange("c p f -> p c f"),
                       xstage[:].rearrange("p (c t) -> p c t", t=128), st)
        em.wait("sync", st, em.val(st))
        blk = es.enter_context(nc.Block())
        em.replay(blk)
    return nc


def build_m3a(part):
    z0 = part * 4
    nc = bass.Bass("TRN2", target_bir_lowering=False)
    bt_s = nc.dram_tensor("bt_s", [TPC, 4096], BF16, kind="ExternalInput").ap()
    ct_s = nc.dram_tensor("ct_s", [TPC, 4096], BF16, kind="ExternalInput").ap()
    la_s = nc.dram_tensor("la_s", [TPC, MHEADS], F32, kind="ExternalInput").ap()
    xtok_s = nc.dram_tensor("xtok_s", [NCH, 128, 4096], BF16, kind="ExternalInput").ap()
    ident = nc.dram_tensor("ident_bf", [128, 128], BF16, kind="ExternalInput").ap()
    identf = nc.dram_tensor("ident_f", [128, 128], F32, kind="ExternalInput").ap()
    ones_f = nc.dram_tensor("ones_f", [1, 128], F32, kind="ExternalInput").ap()
    triu_f = nc.dram_tensor("triu_f", [128, 128], F32, kind="ExternalInput").ap()
    tril = nc.dram_tensor("tril", [128, 128], F32, kind="ExternalInput").ap()
    y_s = nc.dram_tensor("y_s", [8, 128, 4096], BF16, kind="ExternalOutput").ap()
    x_flat = xtok_s.rearrange("c t f -> (c t) f")
    y_flat = y_s.rearrange("c t f -> (c t) f")

    em = Emitter()
    es = ExitStack()
    with es:
        sb = lambda name, shape, dt: es.enter_context(nc.sbuf_tensor(name, shape, dt))
        ps = lambda name, shape, dt: es.enter_context(nc.psum_tensor(name, shape, dt))
        sem = lambda name: es.enter_context(nc.semaphore(name))
        d = _mk_common(nc, em, es, with_xnT=False)
        ident_bf, st = d["ident_bf"], d["st"]
        t_sem, v_sem, s_sem = d["t_sem"], d["v_sem"], d["s_sem"]

        b_z = [sb(f"b_z{i}", [128, 128 * 64], BF16) for i in range(2)]
        c_z = [sb(f"c_z{i}", [128, 128 * 64], BF16) for i in range(2)]
        x_z = [sb(f"x_z{i}", [128, 128 * 64], BF16) for i in range(2)]
        lt_z = [sb(f"lt_z{i}", [128, 128], F32) for i in range(2)]
        ld_b = [sem(f"ld_b{i}") for i in range(2)]
        ones_sb = sb("ones_sb", [1, 128], F32)
        triu_sb = sb("triu_sb", [128, 128], F32)
        tril_sb = sb("tril_sb", [128, 128], F32)
        la_mat = sb("la_mat", [128, 128], F32)
        cs_mat = sb("cs_mat", [128, 128], F32)
        row_sb = sb("row_sb", [1, 128], F32)
        e1m = sb("e1m", [128, 128], F32)
        e2m = sb("e2m", [128, 128], F32)
        ecm = sb("ecm", [128, 128], F32)
        decm = sb("decm", [128, 128], F32)
        amat = sb("amat", [64, 128], F32)
        bt_t = [sb(f"bt_t{i}", [64, 128], BF16) for i in range(2)]
        ct_t = [sb(f"ct_t{i}", [64, 128], BF16) for i in range(2)]
        sl1 = [sb(f"sl1_{i}", [128, 128], F32) for i in range(2)]
        sl2 = [sb(f"sl2_{i}", [128, 128], BF16) for i in range(2)]
        slt = [sb(f"slt_{i}", [128, 128], BF16) for i in range(2)]
        bd = [sb(f"bd_{i}", [128, 64], BF16) for i in range(2)]
        sbf = [sb(f"sbf_{i}", [64, 64], BF16) for i in range(2)]
        e1t = [sb(f"e1t_{i}", [128, 64], F32) for i in range(2)]
        s_z = sb("s_z", [64, 64], F32)
        ystage = sb("ystage", [128, 128 * 64], BF16)
        pf = [ps(f"pf{i}", [128, 512], F32) for i in range(2)]
        py = [ps(f"py{i}", [128, 512], F32) for i in range(2)]
        ptr = [ps(f"ptr{i}", [128, 512], BF16) for i in range(2)]
        pp = [ps(f"pp{i}", [128, 512], F32) for i in range(2)]

        consts = [(ones_sb[:], ones_f), (triu_sb[:], triu_f), (tril_sb[:], tril)]
        emit_const_prologue(em, d, ident, identf, consts)

        for zi in range(4):
            z = z0 + zi
            par = zi % 2
            r0 = zi * 256
            g0 = z * 256
            if zi >= 2:
                em.wait("sync", v_sem, em.val(v_sem))
                em.wait("sync", t_sem, em.val(t_sem))
            for buf, srcap in ((b_z, bt_s), (c_z, ct_s), (x_z, x_flat)):
                src4 = srcap[g0:g0 + 256, :].rearrange("(c l2) (h s) -> l2 h c s", l2=2, s=64)
                for l2 in range(2):
                    em.dma("sync",
                           buf[par][l2 * 64:(l2 + 1) * 64, :].rearrange("p (c s) -> p c s", s=64),
                           src4[l2], ld_b[par])
            lv = em.dma("sync", lt_z[par][:],
                        la_s[g0:g0 + 256, :].rearrange("(c l2) h -> c (l2 h)", l2=2),
                        ld_b[par])
            em.wait("tensor", ld_b[par], lv)
            em.wait("tensor", v_sem, em.val(v_sem))
            em.wait("tensor", s_sem, em.val(s_sem))
            tv = em.op("tensor", lambda e, p2=par: e.transpose(
                pp[0][0:128, 0:128], lt_z[p2][:], d["ident_f"][:]), t_sem)
            em.wait("vector", t_sem, tv)
            vv = em.op("vector", lambda e: e.tensor_copy(la_mat[:], pp[0][0:128, 0:128]), v_sem)
            em.wait("tensor", v_sem, vv)
            tv = em.op("tensor", lambda e: e.matmul(
                pp[1][0:128, 0:128], triu_sb[:], la_mat[:], start=True, stop=True), t_sem)
            em.wait("vector", t_sem, tv)
            vv = em.op("vector", lambda e: e.tensor_copy(cs_mat[:], pp[1][0:128, 0:128]), v_sem)
            em.wait("tensor", v_sem, vv)
            tv = em.op("tensor", lambda e: e.matmul(
                pp[0][0:1, 0:128], d["ident_f"][:, 64:65], cs_mat[:], start=True, stop=True), t_sem)
            em.wait("vector", t_sem, tv)
            vv = em.op("vector", lambda e: e.tensor_copy(row_sb[:], pp[0][0:1, 0:128]), v_sem)
            em.wait("tensor", v_sem, vv)
            tv = em.op("tensor", lambda e: e.matmul(
                pp[1][0:128, 0:128], ones_sb[:], row_sb[:], start=True, stop=True), t_sem)
            em.wait("vector", t_sem, tv)
            vv = em.op("vector", lambda e: e.tensor_sub(e1m[:], cs_mat[:], pp[1][0:128, 0:128]), v_sem)
            em.wait("scalar", v_sem, vv)
            em.op("scalar", lambda e: e.activation(out=e2m[:], in_=e1m[:], func=AF.Exp, scale=-1.0))
            em.op("scalar", lambda e: e.activation(out=e1m[:], in_=e1m[:], func=AF.Exp))
            sv = em.op("scalar", lambda e: e.activation(out=ecm[:], in_=cs_mat[:], func=AF.Exp), s_sem)
            em.wait("tensor", v_sem, em.val(v_sem))
            em.wait("tensor", s_sem, sv)
            tv = em.op("tensor", lambda e: e.matmul(
                pp[0][0:1, 0:128], d["ident_f"][:, 127:128], cs_mat[:], start=True, stop=True), t_sem)
            em.wait("vector", t_sem, tv)
            vv = em.op("vector", lambda e: e.tensor_copy(row_sb[:], pp[0][0:1, 0:128]), v_sem)
            em.wait("tensor", v_sem, vv)
            tv = em.op("tensor", lambda e: e.matmul(
                pp[1][0:128, 0:128], ones_sb[:], row_sb[:], start=True, stop=True), t_sem)
            em.wait("vector", t_sem, tv)
            vv = em.op("vector", lambda e: e.tensor_sub(decm[:], pp[1][0:128, 0:128], cs_mat[:]), v_sem)
            em.wait("scalar", v_sem, vv)
            em.op("scalar", lambda e: e.activation(out=decm[:], in_=decm[:], func=AF.Exp))
            sv = em.op("scalar", lambda e: e.activation(
                out=amat[:], in_=pp[1][0:64, 0:128], func=AF.Exp), s_sem)
            em.wait("vector", s_sem, sv)
            em.op("vector", lambda e: e.memset(s_z[:], 0.0))
            vv = em.milestone("vector", v_sem)
            em.wait("tensor", v_sem, vv)
            em.wait("tensor", s_sem, em.val(s_sem))

            NEG = -10 ** 9
            tok = {k: [NEG] * 130 for k in
                   ("tTr", "vBC", "tSc", "sSL1", "vSL2", "tSLT", "vSLT",
                    "tMM", "vTail", "vSbf", "vBd")}

            def stage_A(c):
                p = c % 2
                em.wait("tensor", v_sem, max(tok["vBC"][c - 2], tok["vSLT"][c - 2]))
                em.op("tensor", lambda e, c2=c, p2=par: e.transpose(
                    ptr[p][0:64, 0:128], b_z[p2][:, c2 * 64:(c2 + 1) * 64], ident_bf[:]))
                tok["tTr"][c] = em.op("tensor", lambda e, c2=c, p2=par: e.transpose(
                    ptr[p][0:64, 128:256], c_z[p2][:, c2 * 64:(c2 + 1) * 64], ident_bf[:]), t_sem)

            def stage_A2(c):
                p = c % 2
                em.wait("tensor", v_sem, tok["vBC"][c])
                em.wait("tensor", s_sem, tok["sSL1"][c - 2])
                tok["tSc"][c] = em.op("tensor", lambda e, p3=p: e.matmul(
                    pf[p3][0:128, 0:128], ct_t[p3][:], bt_t[p3][:], start=True, stop=True), t_sem)

            def stage_B_pe_slt(c):
                if c < 0:
                    return
                p = c % 2
                em.wait("tensor", v_sem, tok["vSL2"][c])
                tok["tSLT"][c] = em.op("tensor", lambda e, p3=p: e.transpose(
                    ptr[p3][0:128, 256:384], sl2[p3][:], ident_bf[:]), t_sem)

            def stage_B_pe_mm(c):
                if c < 0:
                    return
                p = c % 2
                em.wait("tensor", v_sem, max(tok["vSLT"][c], tok["vSbf"][c], tok["vBd"][c],
                                             tok["vTail"][c - 2]))
                em.op("tensor", lambda e, c2=c, p2=par, p3=p: e.matmul(
                    py[p3][0:128, 0:64], slt[p3][:], x_z[p2][:, c2 * 64:(c2 + 1) * 64],
                    start=True, stop=True))
                em.op("tensor", lambda e, p3=p: e.matmul(
                    py[p3][0:128, 64:128], ct_t[p3][:], sbf[p3][:], start=True, stop=True))
                tok["tMM"][c] = em.op("tensor", lambda e, c2=c, p2=par, p3=p: e.matmul(
                    py[p3][0:64, 128:192], bd[p3][:, 0:64], x_z[p2][:, c2 * 64:(c2 + 1) * 64],
                    start=True, stop=True), t_sem)

            def stage_scalar(c):
                p = c % 2
                em.wait("scalar", t_sem, tok["tSc"][c])
                em.wait("scalar", v_sem, tok["vSL2"][c - 2])
                tok["sSL1"][c] = em.op("scalar", lambda e, c2=c, p3=p: e.activation(
                    out=sl1[p3][:], in_=pf[p3][0:128, 0:128], func=AF.Copy,
                    scale=e1m[:, c2:c2 + 1]), s_sem)

            def stage_ve_front(c):
                p = c % 2
                em.wait("vector", t_sem, tok["tTr"][c])
                em.op("vector", lambda e, p3=p: e.tensor_copy(bt_t[p3][:], ptr[p3][0:64, 0:128]))
                tok["vBC"][c] = em.op("vector", lambda e, p3=p: e.tensor_copy(
                    ct_t[p3][:], ptr[p3][0:64, 128:256]), v_sem)

            def stage_ve_sl2(c):
                p = c % 2
                em.wait("vector", s_sem, tok["sSL1"][c])
                tok["vSL2"][c] = em.op("vector", lambda e, p3=p: e.tensor_mul(
                    sl2[p3][:], sl1[p3][:], tril_sb[:]), v_sem)

            def stage_ve_mid(c):
                if c < 0:
                    return
                p = c % 2
                em.wait("vector", t_sem, tok["tSLT"][c])
                tok["vSLT"][c] = em.op("vector", lambda e, c2=c, p3=p: e.tensor_scalar(
                    out=slt[p3][:], in0=ptr[p3][0:128, 256:384], scalar1=e2m[:, c2:c2 + 1],
                    scalar2=None, op0=ALU.mult), v_sem)
                tok["vSbf"][c] = em.op("vector", lambda e, p3=p: e.tensor_copy(
                    sbf[p3][:], s_z[:]), v_sem)
                tok["vBd"][c] = em.op("vector", lambda e, c2=c, p2=par, p3=p: e.tensor_scalar(
                    out=bd[p3][:, 0:64], in0=b_z[p2][:, c2 * 64:(c2 + 1) * 64],
                    scalar1=decm[:, c2:c2 + 1], scalar2=None, op0=ALU.mult), v_sem)

            def stage_ve_tail(c):
                if c < 0:
                    return
                p = c % 2
                em.wait("vector", t_sem, tok["tMM"][c])
                em.op("vector", lambda e, c2=c, p3=p: e.tensor_scalar(
                    out=e1t[p3][:, 0:64], in0=py[p3][0:128, 64:128], scalar1=ecm[:, c2:c2 + 1],
                    scalar2=None, op0=ALU.mult))
                em.op("vector", lambda e, c2=c, p3=p: e.tensor_add(
                    ystage[:, c2 * 64:(c2 + 1) * 64], e1t[p3][:, 0:64], py[p3][0:128, 0:64]))
                tok["vTail"][c] = em.op("vector", lambda e, c2=c, p3=p: e.scalar_tensor_tensor(
                    out=s_z[:], in0=s_z[:], scalar=amat[:, c2:c2 + 1],
                    in1=py[p3][0:64, 128:192], op0=ALU.mult, op1=ALU.add), v_sem)

            for c in range(129):
                if c < 128:
                    stage_A(c)
                    stage_ve_front(c)
                stage_B_pe_slt(c - 1)
                stage_ve_mid(c - 1)
                stage_B_pe_mm(c - 1)
                if c < 128:
                    stage_A2(c)
                stage_ve_tail(c - 1)
                if c < 128:
                    stage_scalar(c)
                    stage_ve_sl2(c)

            vv = em.val(v_sem)
            em.wait("gpsimd", v_sem, vv)
            em.wait("gpsimd", st, em.val(st))
            dst4 = y_flat[r0:r0 + 256, :].rearrange("(c l2) (h v) -> l2 h c v", l2=2, v=64)
            for l2 in range(2):
                em.dma("gpsimd", dst4[l2],
                       ystage[l2 * 64:(l2 + 1) * 64, :].rearrange("p (c v) -> p c v", v=64),
                       st)
            em.wait("vector", st, em.val(st))
        em.wait("gpsimd", st, em.val(st))
        blk = es.enter_context(nc.Block())
        em.replay(blk)
    return nc


def build_m3b():
    nc = bass.Bass("TRN2", target_bir_lowering=False)
    x = nc.dram_tensor("x", [TPC, DM], F32, kind="ExternalInput").ap()
    y_s = nc.dram_tensor("y_s", [NCH, 128, 4096], BF16, kind="ExternalInput").ap()
    res_s = nc.dram_tensor("res_s", [NCH, 128, 4096], BF16, kind="ExternalInput").ap()
    w_out = nc.dram_tensor("w_out", [DM, 4096], F32, kind="ExternalInput").ap()
    ident = nc.dram_tensor("ident_bf", [128, 128], BF16, kind="ExternalInput").ap()
    identf = nc.dram_tensor("ident_f", [128, 128], F32, kind="ExternalInput").ap()
    xout = nc.dram_tensor("xout", [TPC, DM], F32, kind="ExternalOutput").ap()
    yg_s = nc.dram_tensor("yg_s", [NCH, 128, 4096], BF16, kind="ExternalOutput").ap()

    em = Emitter()
    es = ExitStack()
    with es:
        sb = lambda name, shape, dt: es.enter_context(nc.sbuf_tensor(name, shape, dt))
        ps = lambda name, shape, dt: es.enter_context(nc.psum_tensor(name, shape, dt))
        sem = lambda name: es.enter_context(nc.semaphore(name))
        d = _mk_common(nc, em, es, with_xnT=False)
        ident_bf, st = d["ident_bf"], d["st"]
        t_sem, v_sem, s_sem = d["t_sem"], d["v_sem"], d["s_sem"]

        y_c = [sb(f"y_c{i}", [128, 4096], BF16) for i in range(2)]
        r_c = [sb(f"r_c{i}", [128, 4096], BF16) for i in range(2)]
        yg_c = [sb(f"yg_c{i}", [128, 4096], BF16) for i in range(2)]
        ld_y = [sem(f"ld_y{i}") for i in range(2)]
        ld_g = [sem(f"ld_g{i}") for i in range(2)]
        w_f = [sb(f"w_f{i}", [128, 4096], F32) for i in range(2)]
        ld_w = [sem(f"ld_w{i}") for i in range(2)]
        w_b = sb("w_b", [128, 4096], BF16)
        woT = sb("woT", [128, 32 * 512], BF16)
        xsl = [sb(f"xsl{i}", [128, 512], F32) for i in range(2)]
        xosl = [sb(f"xosl{i}", [128, 512], F32) for i in range(2)]
        ld_xs = [sem(f"ld_xs{i}") for i in range(2)]
        ps_tr = [ps(f"ps_tr{i}", [128, 512], BF16) for i in range(2)]
        ps_o = [ps(f"ps_o{i}", [128, 512], F32) for i in range(2)]

        emit_const_prologue(em, d, ident, identf)

        for c in range(NCH):
            par = c % 2
            if c >= 2:
                em.wait("sync", v_sem, em.val(v_sem))
            l1 = em.dma("sync", y_c[par][:], y_s[c], ld_y[par])
            lv = em.dma("sync", r_c[par][:], res_s[c], ld_y[par])
            em.wait("tensor", ld_y[par], lv)
            if c >= 2:
                em.wait("tensor", st, em.val(st))
            for p2 in range(32):
                pb = p2 % 2
                if p2 >= 2 or c > 0:
                    em.wait("tensor", v_sem, em.val(v_sem))
                for half in range(2):
                    tv = em.op("tensor", lambda e, p3=p2, h2=half, par2=par, pb2=pb: e.transpose(
                        ps_tr[pb2][h2 * 64:(h2 + 1) * 64, 0:128],
                        y_c[par2][:, p3 * 128 + h2 * 64: p3 * 128 + (h2 + 1) * 64],
                        ident_bf[:]), t_sem if half == 1 else None)
                em.wait("vector", t_sem, tv)
                em.op("vector", lambda e, p3=p2, par2=par, pb2=pb: e.tensor_mul(
                    yg_c[par2][:, p3 * 128:(p3 + 1) * 128], ps_tr[pb2][0:128, 0:128],
                    r_c[par2][:, p3 * 128:(p3 + 1) * 128]), v_sem)
            vv = em.val(v_sem)
            em.wait("gpsimd", v_sem, vv)
            em.dma("gpsimd", yg_s[c], yg_c[par][:], st)
        em.wait("gpsimd", st, em.val(st))
        em.barrier({"vector": v_sem, "tensor": t_sem, "scalar": s_sem})

        for q in range(4):
            for r in range(4):
                par = r % 2
                dmb = q * 4 + r
                if q + r > 1:
                    em.wait("sync", v_sem, em.val(v_sem))
                lv = em.dma("sync", w_f[par][:], w_out[_ts(dmb), :], ld_w[par])
                em.wait("vector", ld_w[par], lv)
                em.wait("vector", t_sem, em.val(t_sem))
                vv = em.op("vector", lambda e, p2=par: e.tensor_copy(w_b[:], w_f[p2][:]), v_sem)
                em.wait("tensor", v_sem, vv)
                done = 0
                woT_r = woT[:].rearrange("p (fb c) -> p fb c", c=512)
                for batch, nb in ((0, 4), (1, 4), (0, 4), (1, 4), (0, 4), (1, 4), (0, 4), (1, 4)):
                    if done >= 8:
                        em.wait("tensor", v_sem, em.val(v_sem))
                    tv = None
                    for j in range(nb):
                        fb = done + j
                        tv = em.op("tensor", lambda e, b2=batch, j2=j, f2=fb: e.transpose(
                            ps_tr[b2][:, j2 * 128:(j2 + 1) * 128], w_b[:, _ts(f2)], ident_bf[:]),
                            t_sem if j == nb - 1 else None)
                    em.wait("vector", t_sem, tv)
                    vv = em.op("vector", lambda e, b2=batch, nb2=nb, d0=done, r2=r: e.tensor_copy(
                        woT_r[:, d0:d0 + nb2, r2 * 128:(r2 + 1) * 128],
                        ps_tr[b2][:, 0:nb2 * 128].rearrange("p (a c) -> p a c", c=128)),
                        v_sem)
                    done += nb
                em.wait("tensor", v_sem, vv)
            for c in range(NCH):
                pp = c % 2
                lyg = em.dma("gpsimd", yg_c[pp][:], yg_s[c], ld_g[pp])
                if q + c > 1:
                    em.wait("gpsimd", v_sem, em.val(v_sem))
                lx = em.dma("gpsimd", xsl[pp][:], x[_ts(c), q * 512:(q + 1) * 512], ld_xs[pp])
                if q + c > 1:
                    em.wait("tensor", v_sem, em.val(v_sem))
                em.wait("tensor", ld_g[pp], lyg)
                tv = None
                for fb in range(32):
                    tv = em.op("tensor", lambda e, f2=fb, pp2=pp: e.matmul(
                        ps_o[pp2][:], yg_c[pp2][:, _ts(f2)], woT[:, f2 * 512:(f2 + 1) * 512],
                        start=(f2 == 0), stop=(f2 == 31)), t_sem if fb == 31 else None)
                em.wait("vector", t_sem, tv)
                em.wait("vector", ld_xs[pp], lx)
                em.wait("vector", st, em.val(st))
                vv = em.op("vector", lambda e, pp2=pp: e.tensor_add(
                    xosl[pp2][:], xsl[pp2][:], ps_o[pp2][:]), v_sem)
                em.wait("gpsimd", v_sem, vv)
                em.dma("gpsimd", xout[_ts(c), q * 512:(q + 1) * 512], xosl[pp][:], st)
        em.wait("gpsimd", st, em.val(st))
        blk = es.enter_context(nc.Block())
        em.replay(blk)
    return nc


RH = 16
RD = 128
G128 = None


def _gammas():
    import math as _m
    return 1.0 - np.exp(np.linspace(_m.log(1.0 / 32), _m.log(1.0 / 512), RH))


def build_r1():
    gam = _gammas()
    nc = bass.Bass("TRN2", target_bir_lowering=False)
    x = nc.dram_tensor("x", [TPC, DM], F32, kind="ExternalInput").ap()
    wq = nc.dram_tensor("wq", [RH, DM, RD], F32, kind="ExternalInput").ap()
    wk = nc.dram_tensor("wk", [RH, DM, RD], F32, kind="ExternalInput").ap()
    wv = nc.dram_tensor("wv", [RH, DM, RD], F32, kind="ExternalInput").ap()
    wg = nc.dram_tensor("wg", [DM, DM], F32, kind="ExternalInput").ap()
    cq = nc.dram_tensor("cq", [128, TPC], F32, kind="ExternalInput").ap()
    sq = nc.dram_tensor("sq", [128, TPC], F32, kind="ExternalInput").ap()
    ck = nc.dram_tensor("ck", [128, TPC], F32, kind="ExternalInput").ap()
    sk = nc.dram_tensor("sk", [128, TPC], F32, kind="ExternalInput").ap()
    rt = nc.dram_tensor("rt_bf", [128, 128], BF16, kind="ExternalInput").ap()
    deccol = nc.dram_tensor("deccol", [128, RH], F32, kind="ExternalInput").ap()
    ident = nc.dram_tensor("ident_bf", [128, 128], BF16, kind="ExternalInput").ap()
    identf = nc.dram_tensor("ident_f", [128, 128], F32, kind="ExternalInput").ap()
    qx_s = nc.dram_tensor("qx_s", [NCH, 128, RH, 128], BF16, kind="ExternalOutput").ap()
    kx_s = nc.dram_tensor("kx_s", [NCH, 128, RH, 128], BF16, kind="ExternalOutput").ap()
    v_s = nc.dram_tensor("v_s", [NCH, 128, RH, 128], BF16, kind="ExternalOutput").ap()
    gate_s = nc.dram_tensor("gate_s", [NCH, 128, DM], BF16, kind="ExternalOutput").ap()
    stret_s = nc.dram_tensor("stret_s", [NCH, 128, RH * 128], F32, kind="ExternalOutput").ap()
    slocr_s = nc.dram_tensor("slocr_s", [128, RH * 128], F32, kind="ExternalOutput").ap()

    em = Emitter()
    es = ExitStack()
    with es:
        sb = lambda name, shape, dt: es.enter_context(nc.sbuf_tensor(name, shape, dt))
        ps = lambda name, shape, dt: es.enter_context(nc.psum_tensor(name, shape, dt))
        sem = lambda name: es.enter_context(nc.semaphore(name))
        d = _mk_common(nc, em, es)
        xnT, ps_tr, ident_bf = d["xnT"], d["ps_tr"], d["ident_bf"]
        st = d["st"]
        t_sem, v_sem, s_sem = d["t_sem"], d["v_sem"], d["s_sem"]

        cq_sb = sb("cq_sb", [128, TPC], F32)
        sq_sb = sb("sq_sb", [128, TPC], F32)
        ck_sb = sb("ck_sb", [128, TPC], F32)
        sk_sb = sb("sk_sb", [128, TPC], F32)
        rt_sb = sb("rt_sb", [128, 128], BF16)
        dec_sb = sb("dec_sb", [128, RH], F32)
        ld_w = [sem(f"ld_w{i}") for i in range(2)]
        wh_b = [sb(f"wh_b{i}", [128, 16 * 128], BF16) for i in range(2)]
        qf = sb("qf", [128, TPC], F32)
        qb = sb("qb", [128, 512], BF16)
        rtm = sb("rtm", [128, 512], F32)
        t1 = sb("t1", [128, 512], F32)
        px_sb = sb("px_sb", [128, TPC], BF16)
        vtok = sb("vtok", [128, TPC], BF16)
        kd = sb("kd", [128, 128], BF16)
        ststage = sb("ststage", [128, TPC], F32)
        sloc = sb("sloc", [128, RH * 128], F32)
        wgt = sb("wgt", [128, 512], F32)
        wgt_b = sb("wgt_b", [128, 512], BF16)
        ld_g = sem("ld_g")
        gstage = sb("gstage", [128, DM], BF16)
        ps_p = [ps(f"ps_p{i}", [128, 512], F32) for i in range(2)]
        ps_r = ps("ps_r", [128, 512], F32)
        ps_st2 = [ps(f"ps_st2{i}", [128, 512], F32) for i in range(2)]

        consts = [(cq_sb[:], cq), (sq_sb[:], sq), (ck_sb[:], ck), (sk_sb[:], sk),
                  (rt_sb[:], rt), (dec_sb[:], deccol)]
        emit_const_prologue(em, d, ident, identf, consts)
        em.op("vector", lambda e: e.memset(sloc[:], 0.0))

        sems = dict(ld_x=d["ld_x"], s=s_sem, v=v_sem, t=t_sem)
        bufs = dict(xbuf=d["xbuf"], stat=d["stat"], ps_tr=ps_tr, xn=d["xn"],
                    ident_bf=ident_bf)
        emit_ln_to_xnT(em, x, xnT, 16, TPC, sems, bufs)
        em.barrier({"vector": v_sem, "tensor": t_sem, "scalar": s_sem})

        mmdone = [0, 0]

        def proj_head(w_ap, h, par):
            em.wait("scalar", v_sem, em.val(v_sem))
            em.wait("gpsimd", t_sem, mmdone[par])
            lv = em.dma("gpsimd", wh_b[par][:].rearrange("p (a c) -> p a c", c=128),
                        w_ap[h].rearrange("(a p) c -> p a c", p=128), ld_w[par])
            em.wait("tensor", ld_w[par], lv)
            for tq in range(4):
                pb = tq % 2
                em.wait("tensor", s_sem, em.val(s_sem))
                tv = None
                for db in range(NDB):
                    tv = em.op("tensor", lambda e, db2=db, tq2=tq, pb2=pb, p4=par: e.matmul(
                        ps_p[pb2][:], wh_b[p4][:, _ts(db2)],
                        xnT[:, db2 * TPC + tq2 * 512: db2 * TPC + tq2 * 512 + 512],
                        start=(db2 == 0), stop=(db2 == NDB - 1)),
                        t_sem if db == NDB - 1 else None)
                mmdone[par] = tv
                em.wait("scalar", t_sem, tv)
                em.op("scalar", lambda e, tq2=tq, pb2=pb: e.activation(
                    out=qf[:, tq2 * 512:(tq2 + 1) * 512], in_=ps_p[pb2][:],
                    func=AF.Copy), s_sem)

        def xpos(c_sb, s_sb, dst_spill, h):
            em.wait("vector", s_sem, em.val(s_sem))
            em.wait("vector", st, em.val(st))
            for tq in range(4):
                vv = em.op("vector", lambda e, tq2=tq: e.tensor_copy(
                    qb[:], qf[:, tq2 * 512:(tq2 + 1) * 512]), v_sem)
                em.wait("tensor", v_sem, vv)
                em.wait("tensor", v_sem, em.val(v_sem))
                tv = em.op("tensor", lambda e: e.matmul(
                    ps_r[:], rt_sb[:], qb[:], start=True, stop=True), t_sem)
                em.wait("vector", t_sem, tv)
                em.op("vector", lambda e, tq2=tq: e.tensor_mul(
                    rtm[:], ps_r[:], s_sb[:, tq2 * 512:(tq2 + 1) * 512]))
                em.op("vector", lambda e, tq2=tq: e.tensor_mul(
                    t1[:], qf[:, tq2 * 512:(tq2 + 1) * 512],
                    c_sb[:, tq2 * 512:(tq2 + 1) * 512]))
                em.op("vector", lambda e, tq2=tq: e.tensor_add(
                    px_sb[:, tq2 * 512:(tq2 + 1) * 512], t1[:], rtm[:]), v_sem)
            vv = em.val(v_sem)
            em.wait("sync", v_sem, vv)
            em.wait("sync", st, em.val(st))
            em.dma("sync", dst_spill[:, :, h, :].rearrange("c d t -> d c t"),
                   px_sb[:].rearrange("p (c t) -> p c t", t=128), st)

        for h in range(RH):
            par = h % 2
            proj_head(wq, h, par)
            xpos(cq_sb, sq_sb, qx_s, h)
            proj_head(wk, h, par)
            xpos(ck_sb, sk_sb, kx_s, h)
            proj_head(wv, h, par)
            em.wait("vector", s_sem, em.val(s_sem))
            vv = em.op("vector", lambda e: e.tensor_copy(vtok[:], qf[:]), v_sem)
            em.wait("tensor", v_sem, vv)
            for c in range(NCH):
                pb = c % 2
                if c >= 2:
                    em.wait("tensor", v_sem, em.val(v_sem))
                tv = em.op("tensor", lambda e, c2=c, pb2=pb: e.transpose(
                    ps_tr[pb2][:, 0:128], vtok[:, c2 * 128:(c2 + 1) * 128].bitcast(BF16)
                    if False else vtok[:, c2 * 128:(c2 + 1) * 128], ident_bf[:]), t_sem)
                em.wait("vector", t_sem, tv)
                em.op("vector", lambda e, c2=c, pb2=pb: e.tensor_copy(
                    px_sb[:, c2 * 128:(c2 + 1) * 128] if False
                    else vtok[:, c2 * 128:(c2 + 1) * 128], ps_tr[pb2][:, 0:128]), v_sem)
            vv = em.val(v_sem)
            em.wait("sync", v_sem, vv)
            em.wait("sync", st, em.val(st))
            em.dma("sync", v_s[:, :, h, :].rearrange("c t v -> t c v"),
                   vtok[:].rearrange("p (c v) -> p c v", v=128), st)
            for c in range(NCH):
                pb = c % 2
                em.wait("tensor", v_sem, em.val(v_sem))
                tv = em.op("tensor", lambda e, c2=c, pb2=pb: e.transpose(
                    ps_tr[pb2][:, 128:256], px_sb[:, c2 * 128:(c2 + 1) * 128],
                    ident_bf[:]), t_sem)
                em.wait("vector", t_sem, tv)
                vv = em.op("vector", lambda e, h2=h, pb2=pb: e.tensor_scalar(
                    out=kd[:], in0=ps_tr[pb2][:, 128:256],
                    scalar1=dec_sb[:, h2:h2 + 1], scalar2=None, op0=ALU.mult), v_sem)
                em.wait("tensor", v_sem, vv)
                tv = em.op("tensor", lambda e, c2=c, pb2=pb: e.matmul(
                    ps_st2[pb2][:, 0:128], kd[:], vtok[:, c2 * 128:(c2 + 1) * 128],
                    start=True, stop=True), t_sem)
                em.wait("vector", t_sem, tv)
                em.op("vector", lambda e, c2=c, pb2=pb: e.tensor_copy(
                    ststage[:, c2 * 128:(c2 + 1) * 128], ps_st2[pb2][0:128, 0:128]))
                em.op("vector", lambda e, h2=h, c2=c: e.scalar_tensor_tensor(
                    out=sloc[:, h2 * 128:(h2 + 1) * 128],
                    in0=sloc[:, h2 * 128:(h2 + 1) * 128],
                    scalar=float(gam[h2] ** 128),
                    in1=ststage[:, c2 * 128:(c2 + 1) * 128],
                    op0=ALU.mult, op1=ALU.add), v_sem)
            vv = em.val(v_sem)
            em.wait("sync", v_sem, vv)
            em.wait("sync", st, em.val(st))
            em.dma("sync", stret_s[:, :, h * 128:(h + 1) * 128].rearrange("c d v -> d c v"),
                   ststage[:].rearrange("p (c v) -> p c v", v=128), st)
            em.wait("vector", st, em.val(st))

        for jb in range(NDB):
            par = jb % 2
            em.wait("gpsimd", t_sem, mmdone[par])
            lv = em.dma("gpsimd", wh_b[par][:].rearrange("p (a c) -> p a c", c=128),
                        wg[:, jb * 128:(jb + 1) * 128].rearrange("(a p) j -> p a j", p=128),
                        ld_w[par])
            em.wait("tensor", ld_w[par], lv)
            for tq in range(4):
                pb = tq % 2
                em.wait("tensor", s_sem, em.val(s_sem))
                tv = None
                for db in range(NDB):
                    tv = em.op("tensor", lambda e, db2=db, tq2=tq, pb2=pb, p4=par: e.matmul(
                        ps_p[pb2][:], wh_b[p4][:, _ts(db2)],
                        xnT[:, db2 * TPC + tq2 * 512: db2 * TPC + tq2 * 512 + 512],
                        start=(db2 == 0), stop=(db2 == NDB - 1)),
                        t_sem if db == NDB - 1 else None)
                mmdone[par] = tv
                em.wait("scalar", t_sem, tv)
                em.op("scalar", lambda e, tq2=tq, pb2=pb: e.activation(
                    out=gstage[:, tq2 * 512:(tq2 + 1) * 512], in_=ps_p[pb2][:],
                    func=AF.Silu), s_sem)
            em.wait("vector", s_sem, em.val(s_sem))
            vv = em.milestone("vector", v_sem)
            em.wait("tensor", v_sem, vv)
            em.wait("vector", st, em.val(st))
            for c in range(NCH):
                pb2 = c % 2
                if c >= 2:
                    em.wait("tensor", v_sem, em.val(v_sem))
                tv = em.op("tensor", lambda e, c2=c, p3=pb2: e.transpose(
                    ps_tr[p3][:, 0:128], gstage[:, c2 * 128:(c2 + 1) * 128].bitcast(BF16)
                    if False else gstage[:, c2 * 128:(c2 + 1) * 128], ident_bf[:]),
                    t_sem)
                em.wait("vector", t_sem, tv)
                em.op("vector", lambda e, c2=c, p3=pb2: e.tensor_copy(
                    vtok[:, c2 * 128:(c2 + 1) * 128], ps_tr[p3][:, 0:128]), v_sem)
            vv = em.val(v_sem)
            em.wait("sync", v_sem, vv)
            em.wait("sync", st, em.val(st))
            em.dma("sync", gate_s[:, :, jb * 128:(jb + 1) * 128].rearrange("c t j -> t c j"),
                   vtok[:].rearrange("p (c j) -> p c j", j=128), st)
        vv = em.val(v_sem)
        em.wait("sync", v_sem, vv)
        em.wait("sync", st, em.val(st))
        em.dma("sync", slocr_s, sloc[:], st)
        em.wait("sync", st, em.val(st))
        blk = es.enter_context(nc.Block())
        em.replay(blk)
    return nc


def build_r2():
    gam = _gammas()
    nc = bass.Bass("TRN2", target_bir_lowering=False)
    x = nc.dram_tensor("x", [TPC, DM], F32, kind="ExternalInput").ap()
    qx_s = nc.dram_tensor("qx_s", [NCH, 128, RH, 128], BF16, kind="ExternalInput").ap()
    kx_s = nc.dram_tensor("kx_s", [NCH, 128, RH, 128], BF16, kind="ExternalInput").ap()
    v_s = nc.dram_tensor("v_s", [NCH, 128, RH, 128], BF16, kind="ExternalInput").ap()
    gate_s = nc.dram_tensor("gate_s", [NCH, 128, DM], BF16, kind="ExternalInput").ap()
    stret_s = nc.dram_tensor("stret_s", [NCH, 128, RH * 128], F32, kind="ExternalInput").ap()
    s_all = nc.dram_tensor("s_all", [NCORES, 128, RH * 128], F32, kind="ExternalInput").ap()
    coefb = nc.dram_tensor("coefb", [128, NCORES * RH], F32, kind="ExternalInput").ap()
    wo = nc.dram_tensor("wo", [DM, DM], F32, kind="ExternalInput").ap()
    gnw_b = nc.dram_tensor("gnw_b", [128, DM], F32, kind="ExternalInput").ap()
    gnb_b = nc.dram_tensor("gnb_b", [128, DM], F32, kind="ExternalInput").ap()
    dt_t = nc.dram_tensor("dt_t", [RH, 128, 128], F32, kind="ExternalInput").ap()
    gcol = nc.dram_tensor("gcol", [128, RH], F32, kind="ExternalInput").ap()
    ident = nc.dram_tensor("ident_bf", [128, 128], BF16, kind="ExternalInput").ap()
    identf = nc.dram_tensor("ident_f", [128, 128], F32, kind="ExternalInput").ap()
    xout = nc.dram_tensor("xout", [TPC, DM], F32, kind="ExternalOutput").ap()

    em = Emitter()
    es = ExitStack()
    with es:
        sb = lambda name, shape, dt: es.enter_context(nc.sbuf_tensor(name, shape, dt))
        ps = lambda name, shape, dt: es.enter_context(nc.psum_tensor(name, shape, dt))
        sem = lambda name: es.enter_context(nc.semaphore(name))
        d = _mk_common(nc, em, es, with_xnT=False)
        ident_bf, st = d["ident_bf"], d["st"]
        stat = d["stat"]
        t_sem, v_sem, s_sem = d["t_sem"], d["v_sem"], d["s_sem"]

        qx_c = [sb(f"qx_c{i}", [128, RH * 128], BF16) for i in range(2)]
        kx_c = [sb(f"kx_c{i}", [128, RH * 128], BF16) for i in range(2)]
        v_c = [sb(f"v_c{i}", [128, RH * 128], BF16) for i in range(2)]
        g_c = [sb(f"g_c{i}", [128, DM], BF16) for i in range(2)]
        st_c = [sb(f"st_c{i}", [128, RH * 128], F32) for i in range(2)]
        ld_b = [sem(f"ld_b{i}") for i in range(2)]
        srun = sb("srun", [128, RH * 128], F32)
        sbf = sb("sbf", [128, 128], BF16)
        slt = sb("slt", [128, 128], BF16)
        e1t = sb("e1t", [128, 128], F32)
        ytok = sb("ytok", [128, 128], F32)
        yn = sb("yn", [128, 128], F32)
        ygfm = sb("ygfm", [128, DM], BF16)
        gnw_sb = sb("gnw_sb", [128, DM], F32)
        gnb_sb = sb("gnb_sb", [128, DM], F32)
        dt_sb = sb("dt_sb", [128, RH * 128], F32)
        gcol_sb = sb("gcol_sb", [128, RH], F32)
        wo_f = [sb(f"wo_f{i}", [128, DM], F32) for i in range(2)]
        ld_w = [sem(f"ld_w{i}") for i in range(2)]
        wo_sb = sb("wo_sb", [128, NDB * DM], BF16)
        sallj = sb("sallj", [128, RH * 128], F32)
        coef_sb = sb("coef_sb", [128, NCORES * RH], F32)
        xsl = [sb(f"xsl{i}", [128, 512], F32) for i in range(2)]
        xosl = [sb(f"xosl{i}", [128, 512], F32) for i in range(2)]
        ld_xs = [sem(f"ld_xs{i}") for i in range(2)]
        ldc2 = sem("ldc2")
        ps_sc = [ps(f"ps_sc{i}", [128, 512], F32) for i in range(2)]
        ps_y = ps("ps_y", [128, 512], F32)
        ps_yi = ps("ps_yi", [128, 512], F32)
        ps_tr2 = [ps(f"ps_tr2{i}", [128, 512], BF16) for i in range(2)]
        ps_o = [ps(f"ps_o{i}", [128, 512], F32) for i in range(2)]

        consts = [(gnw_sb[:], gnw_b), (gnb_sb[:], gnb_b),
                  (dt_sb[:].rearrange("p (h i) -> p h i", i=128), dt_t.rearrange("h j i -> j h i")),
                  (gcol_sb[:], gcol),
                  (coef_sb[:], coefb)]
        emit_const_prologue(em, d, ident, identf, consts)

        for r in range(NDB):
            par = r % 2
            if r >= 2:
                em.wait("sync", v_sem, em.val(v_sem))
            lv = em.dma("sync", wo_f[par][:], wo[_ts(r), :], ld_w[par])
            em.wait("vector", ld_w[par], lv)
            em.op("vector", lambda e, r2=r, p2=par: e.tensor_copy(
                wo_sb[:, r2 * DM:(r2 + 1) * DM], wo_f[p2][:]), v_sem)

        em.op("vector", lambda e: e.memset(srun[:], 0.0))
        for j in range(NCORES):
            em.wait("sync", ldc2, em.val(ldc2))
            em.wait("sync", v_sem, em.val(v_sem))
            lv = em.dma("sync", sallj[:], s_all[j], ldc2)
            em.wait("vector", ldc2, lv)
            for h in range(RH):
                em.op("vector", lambda e, j2=j, h2=h: e.scalar_tensor_tensor(
                    out=srun[:, h2 * 128:(h2 + 1) * 128], in0=sallj[:, h2 * 128:(h2 + 1) * 128],
                    scalar=coef_sb[:, j2 * RH + h2: j2 * RH + h2 + 1],
                    in1=srun[:, h2 * 128:(h2 + 1) * 128], op0=ALU.mult, op1=ALU.add))
            em.milestone("vector", v_sem)
        em.barrier({"vector": v_sem, "tensor": t_sem, "scalar": s_sem})

        for c in range(NCH):
            par = c % 2
            if c >= 2:
                em.wait("sync", v_sem, em.val(v_sem))
                em.wait("sync", t_sem, em.val(t_sem))
            em.dma("sync", qx_c[par][:].rearrange("p (h t) -> p h t", t=128),
                   qx_s[c].rearrange("d h t -> d h t"), ld_b[par])
            em.dma("sync", kx_c[par][:].rearrange("p (h t) -> p h t", t=128),
                   kx_s[c].rearrange("d h t -> d h t"), ld_b[par])
            em.dma("sync", v_c[par][:].rearrange("p (h t) -> p h t", t=128),
                   v_s[c].rearrange("t h v -> t h v"), ld_b[par])
            em.dma("sync", g_c[par][:], gate_s[c], ld_b[par])
            lv = em.dma("sync", st_c[par][:], stret_s[c], ld_b[par])
            em.wait("tensor", ld_b[par], lv)
            em.wait("vector", ld_b[par], lv)
            for h in range(RH):
                pb = h % 2
                if h >= 2 or c > 0:
                    em.wait("tensor", v_sem, em.val(v_sem))
                tv = em.op("tensor", lambda e, h2=h, p2=par, pb2=pb: e.matmul(
                    ps_sc[pb2][:, 0:128], kx_c[p2][:, h2 * 128:(h2 + 1) * 128],
                    qx_c[p2][:, h2 * 128:(h2 + 1) * 128], start=True, stop=True), t_sem)
                em.wait("vector", t_sem, tv)
                vv = em.op("vector", lambda e, h2=h, pb2=pb: e.tensor_mul(
                    slt[:], ps_sc[pb2][:, 0:128], dt_sb[:, h2 * 128:(h2 + 1) * 128]), v_sem)
                em.op("vector", lambda e, h2=h: e.tensor_copy(
                    sbf[:], srun[:, h2 * 128:(h2 + 1) * 128]))
                vv = em.milestone("vector", v_sem)
                em.wait("tensor", v_sem, vv)
                em.op("tensor", lambda e, h2=h, p2=par: e.matmul(
                    ps_y[:, 0:128], slt[:], v_c[p2][:, h2 * 128:(h2 + 1) * 128],
                    start=True, stop=True))
                tv = em.op("tensor", lambda e, h2=h, p2=par: e.matmul(
                    ps_yi[:, 0:128], qx_c[p2][:, h2 * 128:(h2 + 1) * 128], sbf[:],
                    start=True, stop=True), t_sem)
                em.wait("vector", t_sem, tv)
                em.op("vector", lambda e, h2=h: e.tensor_scalar(
                    out=e1t[:], in0=ps_yi[:, 0:128], scalar1=gcol_sb[:, h2:h2 + 1],
                    scalar2=None, op0=ALU.mult))
                vv = em.op("vector", lambda e: e.tensor_add(ytok[:], e1t[:], ps_y[:, 0:128]), v_sem)
                em.op("vector", lambda e, h2=h, p2=par: e.scalar_tensor_tensor(
                    out=srun[:, h2 * 128:(h2 + 1) * 128], in0=srun[:, h2 * 128:(h2 + 1) * 128],
                    scalar=float(gam[h2] ** 128), in1=st_c[p2][:, h2 * 128:(h2 + 1) * 128],
                    op0=ALU.mult, op1=ALU.add))
                em.wait("tensor", v_sem, em.val(v_sem))
                em.op("vector", lambda e: e.reduce_sum(stat[:, 0:1], ytok[:], axis=AX.X))
                vv = em.op("vector", lambda e: e.tensor_scalar_mul(stat[:, 1:2], stat[:, 0:1], 1.0 / RD), v_sem)
                vv = em.op("vector", lambda e: e.tensor_scalar(
                    out=ytok[:], in0=ytok[:], scalar1=stat[:, 1:2], scalar2=None,
                    op0=ALU.subtract), v_sem)
                em.wait("scalar", v_sem, vv)
                em.op("scalar", lambda e: e.activation(out=yn[:], in_=ytok[:], func=AF.Square,
                                                       accum_out=stat[:, 2:3]))
                sv = em.op("scalar", lambda e: e.activation(
                    out=stat[:, 3:4], in_=stat[:, 2:3], func=AF.Sqrt, scale=1.0 / RD,
                    bias=stat[:, 5:6]), s_sem)
                em.wait("vector", s_sem, sv)
                em.op("vector", lambda e: e.reciprocal(stat[:, 4:5], stat[:, 3:4]))
                em.op("vector", lambda e: e.tensor_scalar(
                    out=yn[:], in0=ytok[:], scalar1=stat[:, 4:5], scalar2=None, op0=ALU.mult))
                em.op("vector", lambda e, h2=h: e.tensor_mul(
                    yn[:], yn[:], gnw_sb[:, h2 * 128:(h2 + 1) * 128]))
                em.op("vector", lambda e, h2=h: e.tensor_add(
                    yn[:], yn[:], gnb_sb[:, h2 * 128:(h2 + 1) * 128]))
                vv = em.op("vector", lambda e, h2=h, p2=par: e.tensor_mul(
                    slt[:], yn[:], g_c[p2][:, h2 * 128:(h2 + 1) * 128]), v_sem)
                em.wait("tensor", v_sem, vv)
                tv = em.op("tensor", lambda e, pb2=pb: e.transpose(
                    ps_tr2[pb2][:, 0:128], slt[:], ident_bf[:]), t_sem)
                em.wait("vector", t_sem, tv)
                em.op("vector", lambda e, h2=h, pb2=pb: e.tensor_copy(
                    ygfm[:, h2 * 128:(h2 + 1) * 128], ps_tr2[pb2][:, 0:128]), v_sem)
            vv = em.milestone("vector", v_sem)
            em.wait("tensor", v_sem, vv)
            for q in range(4):
                pq = q % 2
                if c + q > 1:
                    em.wait("gpsimd", v_sem, em.val(v_sem))
                lx = em.dma("gpsimd", xsl[pq][:], x[_ts(c), q * 512:(q + 1) * 512], ld_xs[pq])
                if c + q > 1:
                    em.wait("tensor", v_sem, em.val(v_sem))
                tv = None
                for hb in range(NDB):
                    tv = em.op("tensor", lambda e, h2=hb, q2=q, pq2=pq: e.matmul(
                        ps_o[pq2][:], ygfm[:, h2 * 128:(h2 + 1) * 128],
                        wo_sb[:, h2 * DM + q2 * 512: h2 * DM + q2 * 512 + 512],
                        start=(h2 == 0), stop=(h2 == NDB - 1)), t_sem if hb == NDB - 1 else None)
                em.wait("vector", t_sem, tv)
                em.wait("vector", ld_xs[pq], lx)
                em.wait("vector", st, em.val(st))
                vv = em.op("vector", lambda e, pq2=pq: e.tensor_add(
                    xosl[pq2][:], xsl[pq2][:], ps_o[pq2][:]), v_sem)
                em.wait("gpsimd", v_sem, vv)
                em.dma("gpsimd", xout[_ts(c), q * 512:(q + 1) * 512], xosl[pq][:], st)
        em.wait("gpsimd", st, em.val(st))
        blk = es.enter_context(nc.Block())
        em.replay(blk)
    return nc
